# version 09
# baseline (speedup 1.0000x reference)
import numpy as np
import concourse.bass as bass
import concourse.mybir as mybir
from concourse.bass_utils import run_bass_kernel_spmd

F32 = mybir.dt.float32
F16 = mybir.dt.float16
AF = mybir.ActivationFunctionType
OP = mybir.AluOpType
AX = mybir.AxisListType

S = 2048
DM = 1024
DH = 64
INNER = 512
HID = 2730
NKI = 22
KI_LAST = 2730 - 21 * 128
NCORE = 8

PASSES1 = [(0, 0), (0, 2048), (2048, 0)]
PASSES2 = [(0, 0), (0, 512), (1024, 0)]
ENGINES = ("tensor", "vector", "scalar", "gpsimd", "sync", "sdma2", "gdma")
DMA_Q = ("sync", "sdma2", "gdma")
SPANS = [(s, s + 512) for s in range(0, S, 512)]


def make_consts():
    c = {}
    c["IDENT"] = np.eye(128, dtype=np.float32)
    prot = np.zeros((128, 128), dtype=np.float64)
    for b in (0, 64):
        for ml in range(32):
            prot[b + ml + 32, b + ml] = -1.0
        for ml in range(32, 64):
            prot[b + ml - 32, b + ml] = 1.0
    c["PROT"] = prot.astype(np.float32)
    bc0 = np.zeros((128, 128), dtype=np.float32); bc0[0, :] = 1.0
    c["BC0"] = bc0
    bc64 = np.zeros((128, 128), dtype=np.float32); bc64[64, :] = 1.0
    c["BC64"] = bc64
    bch = np.zeros((128, 128), dtype=np.float32)
    bch[0, 0:64] = 1.0; bch[1, 64:128] = 1.0
    c["BCH"] = bch
    c["SELA"] = np.ones((128, 1), dtype=np.float32)
    selh = np.zeros((128, 2), dtype=np.float32)
    selh[0:64, 0] = 1.0; selh[64:128, 1] = 1.0
    c["SELH"] = selh
    d = np.arange(64)
    i32 = d % 32
    inv_freq = 1.0 / (10000.0 ** (2.0 * i32 / 64.0))
    base = (2.0 * i32 + 0.4 * 64.0) / (1.4 * 64.0)
    t = np.arange(S, dtype=np.float64)
    freqs = t[None, :] * inv_freq[:, None]
    power = (t - S // 2) / 256.0
    scale = base[:, None] ** power[None, :]
    cosf, sinf = np.cos(freqs), np.sin(freqs)
    dup = lambda a: np.concatenate([a, a], axis=0).astype(np.float32)
    c["CQ"] = dup(cosf * scale)
    c["SQ"] = dup(sinf * scale)
    c["CK"] = dup(cosf / scale)
    c["SK"] = dup(sinf / scale)
    return c


NDMASEM = 24


class BK:

    def __init__(self, nc):
        self.nc = nc
        self.tasks = []
        self.res = {}
        self.count = {e: 0 for e in ENGINES}
        self.dma_idx = 0
        self.gdma_idx = 0
        self.duse = [0] * NDMASEM
        self.waited = {e: {} for e in ("tensor", "vector", "scalar", "gpsimd", "sync")}

    def task(self, eng, reads, writes, fn, name=""):
        stream = {"sdma2": "scalar", "gdma": "gpsimd"}.get(eng, eng)
        waits = {}
        def dep(d):
            dsem, dval = d
            waits[dsem] = max(waits.get(dsem, 0), dval)
        for r in reads:
            e = self.res.get(r)
            if e and e["w"]:
                dep(e["w"])
        for w in writes:
            e = self.res.get(w)
            if e:
                if e["w"]:
                    dep(e["w"])
                for rr in e["r"]:
                    dep(rr)
        if eng in DMA_Q:
            if eng == "gdma":
                sid = 18 + self.gdma_idx % 6
                self.gdma_idx += 1
            else:
                sid = self.dma_idx % 18
                self.dma_idx += 1
            self.duse[sid] += 1
            if self.duse[sid] > 1:
                waits[f"d{sid}"] = max(waits.get(f"d{sid}", 0), 16 * (self.duse[sid] - 1))
            me = (f"d{sid}", 16 * self.duse[sid])
        else:
            self.count[eng] += 1
            me = (eng, self.count[eng])
        if eng == "tensor":
            waits.pop("tensor", None)
        wd = self.waited[stream]
        eff = {}
        for se, val in waits.items():
            if se == stream and eng not in DMA_Q:
                pass
            if wd.get(se, -1) >= val:
                continue
            eff[se] = val
            wd[se] = val
        for r in reads:
            self.res.setdefault(r, {"w": None, "r": []})["r"].append(me)
        for w in writes:
            self.res[w] = {"w": me, "r": []}
        self.tasks.append({"eng": eng, "waits": eff, "fn": fn, "name": name,
                           "me": me})

    def finalize(self):
        nc = self.nc
        sems = {e: nc.alloc_semaphore(f"sem_{e}") for e in
                ("tensor", "vector", "scalar", "gpsimd", "sync")}
        for i in range(NDMASEM):
            sems[f"d{i}"] = nc.alloc_semaphore(f"dsem_{i}")
        with nc.Block() as block:
            def mk(ename, tasklist):
                def body(eng):
                    for t in tasklist:
                        for se, val in sorted(t["waits"].items()):
                            eng.wait_ge(sems[se], val)
                        inst = t["fn"](eng)
                        if inst is None:
                            inst = eng.nop()
                        tsem, tval = t["me"]
                        inst.then_inc(sems[tsem], 16 if t["eng"] in DMA_Q else 1)
                    if ename == "sync":
                        for e2 in ("tensor", "vector", "scalar", "gpsimd"):
                            if self.count[e2]:
                                eng.wait_ge(sems[e2], self.count[e2])
                        for i in range(NDMASEM):
                            if self.duse[i]:
                                eng.wait_ge(sems[f"d{i}"], 16 * self.duse[i])
                return body
            per = {}
            per["tensor"] = [t for t in self.tasks if t["eng"] == "tensor"]
            per["vector"] = [t for t in self.tasks if t["eng"] == "vector"]
            per["scalar"] = [t for t in self.tasks if t["eng"] in ("scalar", "sdma2")]
            per["gpsimd"] = [t for t in self.tasks if t["eng"] in ("gpsimd", "gdma")]
            per["sync"] = [t for t in self.tasks if t["eng"] == "sync"]
            block.tensor(mk("tensor", per["tensor"]))
            block.vector(mk("vector", per["vector"]))
            block.scalar(mk("scalar", per["scalar"]))
            block.gpsimd(mk("gpsimd", per["gpsimd"]))
            block.sync(mk("sync", per["sync"]))
        return nc


def build_program(stage="full"):
    nc = bass.Bass("TRN2", target_bir_lowering=False)
    C = make_consts()

    x_d = nc.dram_tensor("x", [S, DM], F32, kind="ExternalInput").ap()
    y_d = nc.dram_tensor("y", [S, DM], F32, kind="ExternalOutput").ap()
    Wq_d = nc.dram_tensor("Wq", [DM, DM], F32, kind="ExternalInput").ap()
    Wk_d = nc.dram_tensor("Wk", [DM, DM], F32, kind="ExternalInput").ap()
    Wv_d = nc.dram_tensor("Wv", [DM, DM], F32, kind="ExternalInput").ap()
    wqkv_d = nc.dram_tensor("wqkv", [2, DM, 3 * INNER], F32, kind="ExternalInput").ap()
    qs_d = nc.dram_tensor("qs", [2, DH], F32, kind="ExternalInput").ap()
    ks_d = nc.dram_tensor("ks", [2, DH], F32, kind="ExternalInput").ap()
    wo_d = nc.dram_tensor("wo", [2, INNER, DM], F32, kind="ExternalInput").ap()
    lng_d = nc.dram_tensor("lng", [2, DM], F32, kind="ExternalInput").ap()
    lnb_d = nc.dram_tensor("lnb", [2, DM], F32, kind="ExternalInput").ap()
    w1_d = nc.dram_tensor("w1", [2, DM, 2 * HID], F32, kind="ExternalInput").ap()
    b1_d = nc.dram_tensor("b1", [2, 2 * HID], F32, kind="ExternalInput").ap()
    w2_d = nc.dram_tensor("w2", [2, HID, DM], F32, kind="ExternalInput").ap()
    b2_d = nc.dram_tensor("b2", [2, DM], F32, kind="ExternalInput").ap()

    xsp = nc.dram_tensor("xsp", [8, 128, S], F32).ap()
    qsp = nc.dram_tensor("qsp", [8, 128, 2 * S], F16).ap()
    ksp = nc.dram_tensor("ksp", [8, 128, 2 * S], F16).ap()
    otsp = nc.dram_tensor("otsp", [4, 128, S], F32).ap()
    gsp = nc.dram_tensor("gsp", [NKI, 128, 2 * S], F16).ap()
    w1sp = nc.dram_tensor("w1sp", [2, NKI, 128, 4096], F16).ap()
    w2sp = nc.dram_tensor("w2sp", [2, NKI, 128, 2048], F16).ap()

    consts_d = {k: nc.inline_tensor(v, name=f"c_{k}").ap() for k, v in C.items()}

    XA = nc.alloc_sbuf_tensor("XA", [128, 8 * S], F32).ap()
    YA = nc.alloc_sbuf_tensor("YA", [128, 8 * S], F32).ap()
    FA = nc.alloc_sbuf_tensor("FA", [128, 8 * S], F32).ap()
    CO = nc.alloc_sbuf_tensor("CO", [128, 1856], F32).ap()
    GEL = nc.alloc_sbuf_tensor("GEL", [128, 1024], F32).ap()
    PS = [nc.alloc_psum_tensor(f"PS{i}", [128, 512], F32).ap() for i in range(8)]

    xa = lambda i: XA[:, 2048 * i:2048 * (i + 1)]
    ya = lambda i: YA[:, 2048 * i:2048 * (i + 1)]
    fa = lambda i: FA[:, 2048 * i:2048 * (i + 1)]
    XA16 = XA.bitcast(F16)
    FA16 = FA.bitcast(F16)
    xa16 = lambda i: XA16[:, 4096 * i:4096 * (i + 1)]
    fa16 = lambda i: FA16[:, 4096 * i:4096 * (i + 1)]

    CO_IDENT = CO[:, 0:128]
    CO_PROT = CO[:, 128:256]
    CO_BC0 = CO[:, 256:384]
    CO_BC64 = CO[:, 384:512]
    CO_BCH = CO[:, 512:640]
    CO_SELA = CO[:, 640:641]
    CO_SELH = CO[:, 641:643]
    CO_EPSLN = CO[:, 643:644]
    CO_QS = [CO[:, 644 + l:645 + l] for l in range(2)]
    CO_KS = [CO[:, 646 + l:647 + l] for l in range(2)]
    CO_B2 = [CO[:, 648 + 8 * l:656 + 8 * l] for l in range(2)]
    CO_LNG = [CO[:, 664 + 8 * l:672 + 8 * l] for l in range(2)]
    CO_LNB = [CO[:, 680 + 8 * l:688 + 8 * l] for l in range(2)]
    CO_B1A = [CO[:, 696 + NKI * l:696 + NKI * (l + 1)] for l in range(2)]
    CO_B1G = [CO[:, 740 + NKI * l:740 + NKI * (l + 1)] for l in range(2)]
    CO_BRHS = CO[:, 832:1856]

    bk = BK(nc)
    T = bk.task

    def dma(qeng, dst, src, reads, writes, name="", slow=False):
        if slow:
            T(qeng, reads, writes,
              lambda eng, d=dst, s=src: eng.dma_start(out=d, in_=s,
                                                      allow_slow_non_contiguous=True), name)
        else:
            T(qeng, reads, writes, lambda eng, d=dst, s=src: eng.dma_start(out=d, in_=s), name)

    for key, dst in [("IDENT", CO_IDENT), ("PROT", CO_PROT), ("BC0", CO_BC0),
                     ("BC64", CO_BC64), ("BCH", CO_BCH), ("SELA", CO_SELA),
                     ("SELH", CO_SELH)]:
        dma("sync", dst, consts_d[key][:, :], ["C"], ["C"], f"ld_{key}")
    T("vector", ["C"], ["C"], lambda eng: eng.memset(CO_EPSLN, 1e-5), "eps")
    T("vector", ["BRHS"], ["BRHS"], lambda eng: eng.memset(CO_BRHS, 0.0), "brhs0")
    for l in range(2):
        for half in (0, 1):
            dma("sync", CO_QS[l][64 * half:64 * half + 64, :],
                qs_d[l:l + 1, :].transpose([1, 0]), ["C"], ["C"], "qs", slow=True)
            dma("sync", CO_KS[l][64 * half:64 * half + 64, :],
                ks_d[l:l + 1, :].transpose([1, 0]), ["C"], ["C"], "ks", slow=True)
        dma("sync", CO_B2[l], b2_d[l:l + 1, :].rearrange("a (c p) -> (a p) c", p=128),
            ["C"], ["C"], "b2", slow=True)
        dma("sync", CO_LNG[l], lng_d[l:l + 1, :].rearrange("a (c p) -> (a p) c", p=128),
            ["C"], ["C"], "lng", slow=True)
        dma("sync", CO_LNB[l], lnb_d[l:l + 1, :].rearrange("a (c p) -> (a p) c", p=128),
            ["C"], ["C"], "lnb", slow=True)
        for ki in range(NKI):
            n = 128 if ki < NKI - 1 else KI_LAST
            dma("sync", CO_B1A[l][0:n, ki:ki + 1],
                b1_d[l:l + 1, 128 * ki:128 * ki + n].transpose([1, 0]), ["C"], ["C"], "b1a", slow=True)
            dma("sync", CO_B1G[l][0:n, ki:ki + 1],
                b1_d[l:l + 1, HID + 128 * ki:HID + 128 * ki + n].transpose([1, 0]),
                ["C"], ["C"], "b1g", slow=True)

    for tt in range(16):
        stg = fa(tt % 2)[:, 0:1024]
        skey = f"F{tt % 2}"
        dma("sync", stg, x_d[128 * tt:128 * tt + 128, :], [skey], [skey], "ldx")
        for c in range(8):
            ps, pk = PS[c % 2], f"ps{c % 2}"
            T("tensor", [skey, "C"], [pk],
              lambda eng, ps=ps, stg=stg, c=c:
                  eng.transpose(ps[:, 0:128], stg[:, 128 * c:128 * c + 128], CO_IDENT), "xT")
            T("scalar", [pk], [f"X{c}"],
              lambda eng, ps=ps, c=c, tt=tt:
                  eng.copy(xa(c)[:, 128 * tt:128 * tt + 128], ps[:, 0:128]), "xTe")

    def emit_ln(src, srckeys, out, outkeys, sq_scr, sq_key, st_scr, st_key,
                gamma=None, beta=None, name="ln", split_out=None):
        for si, (s0, s1) in enumerate(SPANS):
            b_sx, k_sx = PS[(si % 2) * 2], f"ps{(si % 2) * 2}"
            b_sq, k_sq = PS[(si % 2) * 2 + 1], f"ps{(si % 2) * 2 + 1}"
            b_r, k_r = PS[4 + (si % 2) * 2], f"ps{4 + (si % 2) * 2}"
            b_m, k_m = PS[5 + (si % 2) * 2], f"ps{5 + (si % 2) * 2}"
            for c in range(8):
                T("tensor", [srckeys[c], "C"], [k_sx],
                  lambda eng, c=c, s0=s0, s1=s1, b=b_sx:
                      eng.matmul(b[0:1, 0:512], CO_SELA, src(c)[:, s0:s1],
                                 start=(c == 0), stop=(c == 7)), name + "_sx")
            for c in range(8):
                sq = sq_scr[:, 512 * (c % 2):512 * (c % 2) + 512]
                if c % 2 == 0:
                    T("scalar", [srckeys[c], sq_key], [sq_key],
                      lambda eng, c=c, s0=s0, s1=s1, sq=sq:
                          eng.activation(sq, src(c)[:, s0:s1], AF.Square), name + "_sqr")
                else:
                    T("vector", [srckeys[c], sq_key], [sq_key],
                      lambda eng, c=c, s0=s0, s1=s1, sq=sq:
                          eng.tensor_tensor(sq, src(c)[:, s0:s1], src(c)[:, s0:s1],
                                            op=OP.mult), name + "_sqrv")
                T("tensor", [sq_key, "C"], [k_sq],
                  lambda eng, c=c, b=b_sq, sq=sq:
                      eng.matmul(b[0:1, 0:512], CO_SELA, sq, start=(c == 0), stop=(c == 7)),
                  name + "_sq")
            st2 = st_scr
            T("scalar", [k_sx], [st_key],
              lambda eng, b=b_sx: eng.copy(st2[0:1, 0:512], b[0:1, 0:512]), name + "_ev1")
            T("scalar", [k_sq, st_key], [st_key],
              lambda eng, b=b_sq: eng.copy(st2[0:1, 512:1024], b[0:1, 0:512]), name + "_ev2")
            T("vector", [st_key], [st_key],
              lambda eng: eng.tensor_scalar(st2[0:1, 0:1024], st2[0:1, 0:1024],
                                            1.0 / DM, None, op0=OP.mult), name + "_sc")
            T("vector", [st_key], [st_key],
              lambda eng: eng.tensor_tensor(st2[0:1, 1024:1536], st2[0:1, 0:512],
                                            st2[0:1, 0:512], op=OP.mult), name + "_m2")
            T("vector", [st_key], [st_key],
              lambda eng: eng.tensor_tensor(st2[0:1, 1024:1536], st2[0:1, 512:1024],
                                            st2[0:1, 1024:1536], op=OP.subtract), name + "_var")
            T("scalar", [st_key, "C"], [st_key],
              lambda eng: eng.activation(st2[0:1, 1536:2048], st2[0:1, 1024:1536],
                                         AF.Ln, bias=CO_EPSLN[0:1, :]), name + "_ln")
            T("scalar", [st_key], ["BRHS"],
              lambda eng: eng.activation(CO_BRHS[0:1, 0:512], st2[0:1, 1536:2048],
                                         AF.Exp, scale=-0.5), name + "_exp")
            T("vector", [st_key, "BRHS"], ["BRHS"],
              lambda eng: eng.tensor_tensor(CO_BRHS[0:1, 512:1024], st2[0:1, 0:512],
                                            CO_BRHS[0:1, 0:512], op=OP.mult), name + "_mr")
            T("tensor", ["BRHS", "C"], [k_r],
              lambda eng, b=b_r: eng.matmul(b[:, 0:512], CO_BC0, CO_BRHS[:, 0:512],
                                            start=True, stop=True), name + "_bcr")
            T("tensor", ["BRHS", "C"], [k_m],
              lambda eng, b=b_m: eng.matmul(b[:, 0:512], CO_BC0, CO_BRHS[:, 512:1024],
                                            start=True, stop=True), name + "_bcm")
            for c in range(8):
                if split_out is None:
                    dst = lambda c=c, s0=s0, s1=s1: out(c)[:, s0:s1]
                    dkey = outkeys[c]
                else:
                    hi_fn, lo_fn, tmp_ap, tmp_key = split_out
                    dst = lambda tmp_ap=tmp_ap: tmp_ap
                    dkey = tmp_key
                T("vector", [srckeys[c], k_r], [dkey],
                  lambda eng, c=c, s0=s0, s1=s1, b=b_r, dst=dst:
                      eng.tensor_tensor(dst(), src(c)[:, s0:s1], b[:, 0:512],
                                        op=OP.mult), name + "_ap1")
                T("vector", [dkey, k_m], [dkey],
                  lambda eng, c=c, s0=s0, s1=s1, b=b_m, dst=dst:
                      eng.tensor_tensor(dst(), dst(), b[:, 0:512],
                                        op=OP.subtract), name + "_ap2")
                if gamma is not None:
                    T("vector", [dkey, "C"], [dkey],
                      lambda eng, c=c, s0=s0, s1=s1, dst=dst:
                          eng.tensor_scalar(dst(), dst(),
                                            gamma[:, c:c + 1], beta[:, c:c + 1],
                                            op0=OP.mult, op1=OP.add), name + "_gb")
                if split_out is not None:
                    hi_fn, lo_fn, tmp_ap, tmp_key = split_out
                    T("scalar", [dkey, outkeys[c]], [outkeys[c]],
                      lambda eng, c=c, s0=s0, dst=dst, hi_fn=hi_fn:
                          eng.copy(hi_fn(c)[:, s0:s0 + 512], dst()), name + "_hi")
                    T("vector", [dkey, outkeys[c]], [outkeys[c]],
                      lambda eng, c=c, s0=s0, dst=dst, hi_fn=hi_fn, lo_fn=lo_fn:
                          eng.tensor_tensor(lo_fn(c)[:, s0:s0 + 512], dst(),
                                            hi_fn(c)[:, s0:s0 + 512], op=OP.subtract),
                      name + "_lo")

    def ldwqkv(l, p):
        wsl = []
        for c in range(8):
            if c < 5:
                wsl.append((xa(3)[:, 384 * c:384 * (c + 1)], "X3"))
            else:
                wsl.append((xa(4)[:, 384 * (c - 5):384 * (c - 4)], "X4"))
        for c in range(8):
            src = bass.AP(wqkv_d.tensor,
                          offset=l * DM * 3 * INNER + (128 * c) * (3 * INNER) + 128 * p,
                          ap=[[3 * INNER, 128], [INNER, 3], [1, 128]])
            dma("sync", wsl[c][0].rearrange("p (b n) -> p b n", b=3), src,
                [wsl[c][1]], [wsl[c][1]], "ldwqkv")
        return wsl

    def emit_pair(inv, l, p, wsl):
        lt = f"i{inv}l{l}p{p}"
        PQ, PK, PV = xa(0), xa(1), xa(2)
        kPQ, kPK, kPV = "X0", "X1", "X2"
        def qkv_rhs(c):
            return ya(c) if c < 4 else fa(c - 4)
        qkv_rkey = [f"Y{c}" if c < 4 else f"F{c - 4}" for c in range(8)]
        for si, (s0, s1) in enumerate(SPANS):
            for c in range(8):
                w, wkey = wsl[c]
                for t_i, bank in enumerate([0, 1, 2]):
                    T("tensor", [wkey, qkv_rkey[c]], [f"ps{bank}"],
                      lambda eng, c=c, t_i=t_i, bank=bank, s0=s0, s1=s1, w=w:
                          eng.matmul(PS[bank][:, 0:512], w[:, 128 * t_i:128 * t_i + 128],
                                     qkv_rhs(c)[:, s0:s1], start=(c == 0), stop=(c == 7)),
                      "qkv_mm")
            for bank, (dst, key) in enumerate([(PQ, kPQ), (PK, kPK), (PV, kPV)]):
                T("scalar", [f"ps{bank}"], [key],
                  lambda eng, bank=bank, dst=dst, s0=s0:
                      eng.copy(dst[:, s0:s0 + 512], PS[bank][:, 0:512]), "qkv_ev")
        for TP, tkey, cc_, ss_, scl in [(PQ, kPQ, "CQ", "SQ", CO_QS[l]),
                                        (PK, kPK, "CK", "SK", CO_KS[l])]:
            dma("sync", xa(3), consts_d[cc_][:, :], ["X3"], ["X3"], "ldC")
            dma("sync", xa(4), consts_d[ss_][:, :], ["X4"], ["X4"], "ldS")
            for si, (s0, s1) in enumerate(SPANS):
                bss, brq, brot = PS[3], PS[4], PS[5]
                sq = fa(4)[:, 0:512]
                T("scalar", [tkey, "F4"], ["F4"],
                  lambda eng, TP=TP, s0=s0, sq=sq:
                      eng.activation(sq, TP[:, s0:s0 + 512], AF.Square), "l2_sq")
                T("tensor", ["F4", "C"], ["ps3"],
                  lambda eng, sq=sq:
                      eng.matmul(bss[0:2, 0:512], CO_SELH, sq, start=True, stop=True), "l2_mm")
                T("vector", ["ps3"], ["ps3"],
                  lambda eng: eng.tensor_scalar(bss[0:2, 0:512], bss[0:2, 0:512], 1e-18,
                                                None, op0=OP.max), "l2_clip")
                T("scalar", ["ps3"], ["ps3"],
                  lambda eng: eng.activation(bss[0:2, 0:512], bss[0:2, 0:512], AF.Ln), "l2_ln")
                T("scalar", ["ps3"], ["BRHS"],
                  lambda eng: eng.activation(CO_BRHS[0:2, 0:512], bss[0:2, 0:512],
                                             AF.Exp, scale=-0.5), "l2_exp")
                T("tensor", ["BRHS", "C"], ["ps4"],
                  lambda eng: eng.matmul(brq[:, 0:512], CO_BCH, CO_BRHS[:, 0:512],
                                         start=True, stop=True), "l2_bc")
                T("vector", [tkey, "ps4", "C"], [tkey],
                  lambda eng, TP=TP, s0=s0, scl=scl:
                      eng.scalar_tensor_tensor(TP[:, s0:s0 + 512], TP[:, s0:s0 + 512],
                                               scl, brq[:, 0:512], op0=OP.mult, op1=OP.mult),
                  "l2_ap")
                T("tensor", [tkey, "C"], ["ps5"],
                  lambda eng, TP=TP, s0=s0:
                      eng.matmul(brot[:, 0:512], CO_PROT, TP[:, s0:s0 + 512],
                                 start=True, stop=True), "rot_mm")
                T("vector", [tkey, "X3", "F4"], ["F4"],
                  lambda eng, TP=TP, s0=s0:
                      eng.tensor_tensor(fa(4)[:, 512:1024], TP[:, s0:s0 + 512],
                                        xa(3)[:, s0:s0 + 512], op=OP.mult), "rot_t1")
                T("vector", ["ps5", "X4", "F5"], ["F5"],
                  lambda eng, s0=s0:
                      eng.tensor_tensor(fa(5)[:, 0:512], brot[:, 0:512],
                                        xa(4)[:, s0:s0 + 512], op=OP.mult), "rot_t2")
                T("vector", ["F4", "F5"], [tkey],
                  lambda eng, TP=TP, s0=s0:
                      eng.tensor_tensor(TP[:, s0:s0 + 512], fa(4)[:, 512:1024],
                                        fa(5)[:, 0:512], op=OP.add), "rot_add")
        nwsl = ldwqkv(l, p + 1) if p + 1 < 4 else None
        VT0 = 5 * 2048
        T("vector", ["X5"], ["X5"],
          lambda eng: eng.memset(XA[:, VT0 + 64:VT0 + 16 * 132:132], 1.0), "vt1")
        T("vector", ["X5"], ["X5"],
          lambda eng: eng.memset(XA[:, VT0 + 65:VT0 + 16 * 132:132], 1.0), "vt2")
        T("vector", ["X5", "X6"], ["X5", "X6"],
          lambda eng: eng.memset(XA[:, VT0 + 130:VT0 + 16 * 132:132], 1.0), "vt3")
        T("vector", ["X5", "X6"], ["X5", "X6"],
          lambda eng: eng.memset(XA[:, VT0 + 131:VT0 + 16 * 132:132], 1.0), "vt4")
        for tt in range(16):
            bank = 6 + (tt % 2)
            T("tensor", [kPV, "C"], [f"ps{bank}"],
              lambda eng, tt=tt, bank=bank:
                  eng.transpose(PS[bank][:, 0:128], PV[:, 128 * tt:128 * tt + 128], CO_IDENT),
              "vt_T")
            T("scalar", [f"ps{bank}"], ["X5", "X6"],
              lambda eng, tt=tt, bank=bank:
                  eng.copy(XA[:, VT0 + 132 * tt:VT0 + 132 * tt + 64], PS[bank][:, 0:64]),
              "vt_evA")
            T("scalar", [f"ps{bank}"], ["X5", "X6"],
              lambda eng, tt=tt, bank=bank:
                  eng.copy(XA[:, VT0 + 132 * tt + 66:VT0 + 132 * tt + 130],
                           PS[bank][:, 64:128]), "vt_evB")
        for w in range(4):
            emit_window(l, p, w, PQ, PK, kPQ, kPK, VT0)
        dma("sdma2", otsp[p, 0:64, :], xa(7)[0:64, :], ["X7"], [f"ot{p}"], "ot_spA")
        return nwsl

    def emit_window(l, p, w, PQ, PK, kPQ, kPK, VT0):
        s0 = 512 * w
        OTA = xa(7)
        first_av = [True, True]

        def geom(m):
            ca = 4 * (w - 1) + m if m < 4 else 4 * w + (m - 4)
            q_lo = 0 if m < 4 else 128 * (m - 4)
            q_hi = 128 * (m + 1) if m < 4 else 512
            return ca, q_lo, q_hi - q_lo

        def emit_st(m, i):
            ca, q_lo, n = geom(m)
            for h, (bank, base) in enumerate([((i % 2) * 2, 0), ((i % 2) * 2 + 1, 64)]):
                T("tensor", [kPQ, kPK], [f"ps{bank}"],
                  lambda eng, bank=bank, base=base, ca=ca, q_lo=q_lo, n=n:
                      eng.matmul(PS[bank][:, q_lo:q_lo + n],
                                 PK[base:base + 64, 128 * ca:128 * ca + 128],
                                 PQ[base:base + 64, s0 + q_lo:s0 + q_lo + n],
                                 start=True, stop=True), "st_mm")

        def emit_expmask_av(m, i):
            ca, q_lo, n = geom(m)
            for h, bank in enumerate([(i % 2) * 2, (i % 2) * 2 + 1]):
                qk = f"F6q{2 * (i % 2) + h}"
                off = 1024 * (i % 2) + 512 * h
                T("scalar", [f"ps{bank}", qk], [qk],
                  lambda eng, bank=bank, off=off, q_lo=q_lo, n=n:
                      eng.activation(fa(6)[:, off + q_lo:off + q_lo + n],
                                     PS[bank][:, q_lo:q_lo + n], AF.Exp, scale=8.0), "st_exp")
                if m < 4:
                    T("gpsimd", [qk], [qk],
                      lambda eng, off=off, m=m, n=n:
                          eng.affine_select(fa(6)[:, off:off + n], fa(6)[:, off:off + n],
                                            pattern=[[-1, n]], compare_op=OP.is_ge,
                                            fill=0.0, base=128 * m, channel_multiplier=1),
                      "mask_lb")
                else:
                    T("gpsimd", [qk], [qk],
                      lambda eng, off=off, q_lo=q_lo, n=n:
                          eng.affine_select(fa(6)[:, off + q_lo:off + q_lo + n],
                                            fa(6)[:, off + q_lo:off + q_lo + n],
                                            pattern=[[1, n]], compare_op=OP.is_ge,
                                            fill=0.0, base=0, channel_multiplier=-1),
                      "mask_iw")
            for h, bank in enumerate([4, 5]):
                qk = f"F6q{2 * (i % 2) + h}"
                off = 1024 * (i % 2) + 512 * h
                vt = XA[:, VT0 + 132 * ca + 66 * h:VT0 + 132 * ca + 66 * h + 66]
                st = first_av[h]
                T("tensor", [qk, "X5", "X6"], [f"ps{bank}"],
                  lambda eng, bank=bank, vt=vt, off=off, q_lo=q_lo, n=n, st=st:
                      eng.matmul(PS[bank][0:66, q_lo:q_lo + n], vt,
                                 fa(6)[:, off + q_lo:off + q_lo + n],
                                 start=st, stop=False, skip_group_check=True), "av_mm")
                first_av[h] = False

        ms = [m for m in [4, 0, 1, 2, 3, 5, 6, 7] if not (w == 0 and m < 4)]
        emit_st(ms[0], 0)
        for i, m in enumerate(ms):
            if i + 1 < len(ms):
                emit_st(ms[i + 1], i + 1)
            emit_expmask_av(m, i)
        RSS = CO[:, 832:1344]
        for h, bank in enumerate([4, 5]):
            T("scalar", [f"ps{bank}", "BRHS"], ["BRHS"],
              lambda eng, bank=bank: eng.copy(RSS[64:65, 0:512], PS[bank][64:65, 0:512]),
              "av_sums")
            T("tensor", ["BRHS", "C"], [f"ps{6 + h}"],
              lambda eng, h=h: eng.matmul(PS[6 + h][0:64, 0:512], CO_BC64[:, 0:64],
                                          RSS[:, 0:512], start=True, stop=True), "av_bc")
            T("scalar", [f"ps{6 + h}"], [f"ps{6 + h}"],
              lambda eng, h=h: eng.activation(PS[6 + h][0:64, 0:512], PS[6 + h][0:64, 0:512],
                                              AF.Ln), "av_lnr")
            T("scalar", [f"ps{6 + h}", "F4"], ["F4"],
              lambda eng, h=h: eng.activation(fa(4)[0:64, 512 * h:512 * h + 512],
                                              PS[6 + h][0:64, 0:512], AF.Exp, scale=-1.0),
              "av_rec")
        T("vector", ["ps4", "F4", "X7"], ["X7"],
          lambda eng: eng.tensor_tensor(OTA[0:64, s0:s0 + 512], PS[4][0:64, 0:512],
                                        fa(4)[0:64, 0:512], op=OP.mult), "av_evA")
        T("vector", ["ps5", "F4", "F5"], ["F5"],
          lambda eng: eng.tensor_tensor(fa(5)[0:64, 0:512], PS[5][0:64, 0:512],
                                        fa(4)[0:64, 512:1024], op=OP.mult), "av_evB")
        dma("sdma2", otsp[p, 64:128, s0:s0 + 512], fa(5)[0:64, 0:512], ["F5"], [f"ot{p}"],
            "ot_spB")

    def emit_ffn_layer(inv, l):
        for j in range(4):
            c = 4 + j
            T("vector", [f"F{j}"], [f"F{j}"],
              lambda eng, j=j: eng.memset(fa(j)[:, 0:1], 0.0), "shz")
            T("vector", [f"Y{c}", f"F{j}"], [f"F{j}"],
              lambda eng, j=j, c=c: eng.tensor_copy(fa(j)[:, 1:2048], ya(c)[:, 0:2047]), "shc")
        wsl = ldwqkv(l, 0)
        for p in range(4):
            wsl = emit_pair(inv, l, p, wsl)
        for j in range(4):
            dst = fa(j // 2)[:, 1024 * (j % 2):1024 * (j % 2) + 1024]
            dma("sync", dst, wo_d[l, 128 * j:128 * j + 128, :],
                [f"F{j // 2}"], [f"F{j // 2}"], "ldwo")
        for si, (s0, s1) in enumerate(SPANS):
            dma("sync", fa(6)[:, 0:512], otsp[0, :, s0:s1], ["F6q0", "ot0"], ["F6q0"], "ldot")
            for j in range(4):
                ots = fa(6)[:, 512 * j:512 * j + 512]
                if j + 1 < 4:
                    dma("sync", fa(6)[:, 512 * (j + 1):512 * (j + 1) + 512],
                        otsp[j + 1, :, s0:s1], [f"F6q{j + 1}", f"ot{j + 1}"],
                        [f"F6q{j + 1}"], "ldot")
                for fo in range(8):
                    wob = fa(j // 2)[:, 1024 * (j % 2) + 128 * fo:1024 * (j % 2) + 128 * fo + 128]
                    T("tensor", [f"F6q{j}", f"F{j // 2}"], [f"ps{fo}"],
                      lambda eng, fo=fo, j=j, wob=wob, ots=ots:
                          eng.matmul(PS[fo][:, 0:512], wob, ots, start=(j == 0), stop=(j == 3)),
                      "wo_mm")
            for fo in range(8):
                T("vector", [f"ps{fo}", f"Y{fo}"], [f"Y{fo}"],
                  lambda eng, fo=fo, s0=s0:
                      eng.tensor_tensor(ya(fo)[:, s0:s0 + 512], ya(fo)[:, s0:s0 + 512],
                                        PS[fo][:, 0:512], op=OP.add), "wo_resid")
        if stage == f"attn_i{inv}l{l}":
            return False
        emit_mlp(inv, l)
        return True

    def emit_mlp(inv, l):
        for j in range(4):
            c = 4 + j
            T("vector", [f"F{j}"], [f"F{j}"],
              lambda eng, j=j: eng.memset(fa(j)[:, 0:1], 0.0), "shz2")
            T("vector", [f"Y{c}", f"F{j}"], [f"F{j}"],
              lambda eng, j=j, c=c: eng.tensor_copy(fa(j)[:, 1:2048], ya(c)[:, 0:2047]), "shc2")
        hsrc = lambda c: (ya(c) if c < 4 else fa(c - 4))
        hkeys = [f"Y{c}" if c < 4 else f"F{c - 4}" for c in range(8)]
        h2h = lambda c: xa16(c)[:, 0:2048]
        h2l = lambda c: xa16(c)[:, 2048:4096]
        emit_ln(hsrc, hkeys, None, [f"X{c}" for c in range(8)],
                fa(4)[:, 0:1024], "F4", fa(5), "F5",
                gamma=CO_LNG[l], beta=CO_LNB[l], name=f"lnm{inv}{l}",
                split_out=(h2h, h2l, fa(4)[:, 1024:1536], "F4"))
        if stage == "mlpdbgH" and inv == 0 and l == 0:
            for c in range(8):
                dma("sync", y_d[128 * c:128 * c + 128, :], xa(c)[:, 0:1024],
                    [f"X{c}"], ["dbg"], "dbgH")
                dma("sync", y_d[1024 + 128 * c:1024 + 128 * c + 128, :], xa(c)[:, 1024:2048],
                    [f"X{c}"], ["dbg"], "dbgH2")
            raise StopIteration
        def ldw1(ki):
            n = 128 if ki < NKI - 1 else KI_LAST
            if inv == 0:
                stg = fa(6 + ki % 2)
                skeys = (["F6q0", "F6q1", "F6q2", "F6q3"] if ki % 2 == 0 else ["F7"])
                src = bass.AP(w1_d.tensor,
                              offset=l * DM * 2 * HID + 128 * ki,
                              ap=[[2 * HID, 128], [HID, 2], [2 * HID * 128, 8], [1, n]])
                dma("sync", stg.rearrange("p (b c n) -> p b c n", b=2, c=8)[:, :, :, 0:n],
                    src, skeys, skeys, "ldw1")
            else:
                w16 = fa16(ki % 2)
                wkey = f"F{ki % 2}"
                dma("sync", w16[:, :], w1sp[l, ki, :, :], [wkey, f"w1s{l}_{ki}"],
                    [wkey], "ldw1s")

        def split_w1(ki):
            if inv != 0:
                return
            stg = fa(6 + ki % 2)
            skeys = (["F6q0", "F6q1", "F6q2", "F6q3"] if ki % 2 == 0 else ["F7"])
            skey = skeys[0]
            w16 = fa16(ki % 2)
            wkey = f"F{ki % 2}"
            T("scalar", skeys + [wkey], [wkey],
              lambda eng, stg=stg, w16=w16: eng.copy(w16[:, 0:2048], stg[:, 0:2048]), "w1hi")
            T("vector", skeys + [wkey], [wkey],
              lambda eng, stg=stg, w16=w16:
                  eng.tensor_tensor(w16[:, 2048:4096], stg[:, 0:2048], w16[:, 0:2048],
                                    op=OP.subtract), "w1lo")
            dma("sdma2", w1sp[l, ki, :, :], w16[:, :], [wkey], [f"w1s{l}_{ki}"], "w1spl")

        ldw1(0)
        split_w1(0)
        if NKI > 1:
            ldw1(1)
        for ki in range(NKI):
            n = 128 if ki < NKI - 1 else KI_LAST
            w16 = fa16(ki % 2)
            wkey = f"F{ki % 2}"
            for si, (s0, s1) in enumerate(SPANS):
                if si == 1:
                    if ki + 1 < NKI:
                        split_w1(ki + 1)
                    if inv == 0 and ki + 2 < NKI:
                        ldw1(ki + 2)
                bA, kA = PS[(si % 2) * 2], f"ps{(si % 2) * 2}"
                bG, kG = PS[(si % 2) * 2 + 1], f"ps{(si % 2) * 2 + 1}"
                for half, (bank, kb) in [(0, (bA, kA)), (1, (bG, kG))]:
                    for pi, (w_off, x_off) in enumerate(PASSES1):
                        for c in range(8):
                            T("tensor", [wkey, f"X{c}"], [kb],
                              lambda eng, c=c, s0=s0, n=n, bank=bank, w16=w16,
                                     half=half, w_off=w_off, x_off=x_off, pi=pi:
                                  eng.matmul(bank[0:n, 0:512],
                                             w16[:, w_off + 1024 * half + 128 * c:
                                                 w_off + 1024 * half + 128 * c + n],
                                             xa16(c)[:, x_off + 512 * (s0 // 512):
                                                     x_off + 512 * (s0 // 512) + 512],
                                             start=(pi == 0 and c == 0),
                                             stop=(pi == len(PASSES1) - 1 and c == 7)), "u16")
                gsc, gk = GEL[:, 512 * (si % 2):512 * (si % 2) + 512], f"GE{si % 2}"
                T("scalar", [kG, "C", gk], [gk],
                  lambda eng, ki=ki, n=n, bG=bG, gsc=gsc:
                      eng.activation(gsc[0:n, 0:512], bG[0:n, 0:512], AF.Gelu,
                                     bias=CO_B1G[l][0:n, ki:ki + 1]), "gelu")
                T("vector", [kA, gk, "C", "F5"], ["F5"],
                  lambda eng, ki=ki, si=si, n=n, bA=bA, gsc=gsc:
                      eng.scalar_tensor_tensor(fa(5)[0:n, 512 * si:512 * si + 512],
                                               bA[0:n, 0:512], CO_B1A[l][0:n, ki:ki + 1],
                                               gsc[0:n, 0:512], op0=OP.add, op1=OP.mult),
                  "glu")
            if inv != 0 and ki + 2 < NKI:
                ldw1(ki + 2)
            g16 = fa16(2 + ki % 2)
            gkey = f"F{2 + ki % 2}"
            T("scalar", ["F5", gkey], [gkey],
              lambda eng, g16=g16, n=n: eng.copy(g16[0:n, 0:2048], fa(5)[0:n, :]), "ghi")
            T("vector", ["F5", gkey], [gkey],
              lambda eng, g16=g16, n=n:
                  eng.tensor_tensor(g16[0:n, 2048:4096], fa(5)[0:n, :], g16[0:n, 0:2048],
                                    op=OP.subtract), "glo")
            dma("sdma2", gsp[ki, 0:n, :], g16[0:n, :], [gkey], [f"g{ki}"], "g_sp")
        W2R = [(fa16(0)[:, 0:2048], "F0"), (fa16(1)[:, 0:2048], "F1"),
               (fa16(3)[:, 0:2048], "F3")]
        RSTG = [(fa(4)[:, 0:1024], "F4"), (fa(5)[:, 0:1024], "F5")]

        def p2_loads(ki, si, s0):
            n = 128 if ki < NKI - 1 else KI_LAST
            w16, wk = W2R[ki % 3]
            g16 = fa16(2)[:, 1024 * (ki % 3):1024 * (ki % 3) + 1024]
            if inv == 0 and si == 0:
                wstg, sk = RSTG[ki % 2]
                dma("sync", wstg[0:n, :], w2_d[l, 128 * ki:128 * ki + n, :],
                    [sk], [sk], "ldw2")
                T("scalar", [sk, wk], [wk],
                  lambda eng, wstg=wstg, w16=w16: eng.copy(w16[:, 0:1024], wstg[:, 0:1024]),
                  "w2hi")
                T("vector", [sk, wk], [wk],
                  lambda eng, wstg=wstg, w16=w16:
                      eng.tensor_tensor(w16[:, 1024:2048], wstg[:, 0:1024], w16[:, 0:1024],
                                        op=OP.subtract), "w2lo")
                dma("sdma2", w2sp[l, ki, :, :], w16[:, :], [wk], [f"w2s{l}_{ki}"], "w2spl")
            else:
                dma("sync", w16[:, :], w2sp[l, ki, :, :], [wk, f"w2s{l}_{ki}"], [wk],
                    "ldw2s")
            src_g = bass.AP(gsp.tensor,
                            offset=ki * 128 * 4096 + s0,
                            ap=[[4096, n], [2048, 2], [1, 512]])
            dma("gdma", g16.rearrange("p (h x) -> p h x", h=2)[0:n, :, :], src_g,
                [f"g{ki}", "F2"], ["F2"], "ldg")

        for si, (s0, s1) in enumerate(SPANS):
            p2_loads(0, si, s0)
            if NKI > 1:
                p2_loads(1, si, s0)
            for ki in range(NKI):
                n = 128 if ki < NKI - 1 else KI_LAST
                w16, wk = W2R[ki % 3]
                g16 = fa16(2)[:, 1024 * (ki % 3):1024 * (ki % 3) + 1024]
                if ki + 2 < NKI:
                    p2_loads(ki + 2, si, s0)
                for pi, (w_off, g_off) in enumerate(PASSES2):
                    for fo in range(8):
                        T("tensor", [wk, "F2"], [f"ps{fo}"],
                          lambda eng, fo=fo, ki=ki, n=n, w16=w16, g16=g16,
                                 w_off=w_off, g_off=g_off, pi=pi:
                              eng.matmul(PS[fo][:, 0:512],
                                         w16[0:n, w_off + 128 * fo:w_off + 128 * fo + 128],
                                         g16[0:n, g_off:g_off + 512],
                                         start=(ki == 0 and pi == 0),
                                         stop=(ki == NKI - 1 and pi == len(PASSES2) - 1)), "w216")
            for fo in range(8):
                T("vector", [f"ps{fo}", f"Y{fo}", "C"], [f"Y{fo}"],
                  lambda eng, fo=fo, s0=s0:
                      eng.scalar_tensor_tensor(ya(fo)[:, s0:s0 + 512], PS[fo][:, 0:512],
                                               CO_B2[l][:, fo:fo + 1], ya(fo)[:, s0:s0 + 512],
                                               op0=OP.add, op1=OP.add), "w2_resid")

    def emit_ffn(inv):
        emit_ln(lambda c: xa(c), [f"X{c}" for c in range(8)],
                lambda c: ya(c), [f"Y{c}" for c in range(8)],
                fa(4)[:, 0:1024], "F4", fa(5), "F5", name=f"ln0_{inv}")
        if stage == "ln0":
            return "partial"
        for c in range(8):
            dma("sdma2", xsp[c, :, :], xa(c), [f"X{c}"], [f"xsp{c}"], "xspill")
        for l in range(2):
            emit_ffn_layer(inv, l)
            if stage in (f"attn_i{inv}l{l}", f"ffn_i{inv}l{l}"):
                return "partial"
        for c in range(8):
            dma("sync", xa(c), xsp[c, :, :], [f"xsp{c}", f"X{c}"], [f"X{c}"], "xrest")
            T("vector", [f"X{c}", f"Y{c}"], [f"X{c}"],
              lambda eng, c=c: eng.scalar_tensor_tensor(xa(c), ya(c), 0.5, xa(c),
                                                        op0=OP.mult, op1=OP.add), "resid_ffn")
        return "complete"

    def emit_dense():
        emit_ln(lambda c: xa(c), [f"X{c}" for c in range(8)],
                lambda c: fa(c), [f"F{c}" for c in range(8)],
                ya(0)[:, 0:1024], "Y0", ya(1), "Y1", name="lnd")
        for c in range(8):
            dma("sdma2", xsp[c, :, :], xa(c), [f"X{c}"], [f"xsp{c}"], "xspd")

        def proj2(wd, store, name):
            for c in range(8):
                wb = XA[:, 2048 * (c // 2) + 1024 * (c % 2):
                        2048 * (c // 2) + 1024 * (c % 2) + 1024]
                dma("sync", wb, wd[128 * c:128 * c + 128, :],
                    [f"X{c // 2}"], [f"X{c // 2}"], "ldW_" + name)
            for fo in range(8):
                for si, (s0, s1) in enumerate(SPANS):
                    bank = (fo * 4 + si) % 2
                    for c in range(8):
                        wb = XA[:, 2048 * (c // 2) + 1024 * (c % 2):
                                2048 * (c // 2) + 1024 * (c % 2) + 1024]
                        T("tensor", [f"X{c // 2}", f"F{c}"], [f"ps{bank}"],
                          lambda eng, c=c, fo=fo, s0=s0, bank=bank, wb=wb:
                              eng.matmul(PS[bank][:, 0:512], wb[:, 128 * fo:128 * fo + 128],
                                         fa(c)[:, s0:s0 + 512], start=(c == 0), stop=(c == 7)),
                          "dmm_" + name)
                    store(fo, si, bank)

        def store_v(fo, si, bank):
            s0 = 512 * si
            T("scalar", [f"ps{bank}", "X4"], ["X4"],
              lambda eng, bank=bank, s0=s0: eng.copy(xa(4)[:, s0:s0 + 512], PS[bank][:, 0:512]),
              "dv_ev")
            if si == 3:
                for tt in range(16):
                    b2_ = 4 + (tt % 2)
                    T("tensor", ["X4", "C"], [f"ps{b2_}"],
                      lambda eng, tt=tt, b2_=b2_:
                          eng.transpose(PS[b2_][:, 0:128], xa(4)[:, 128 * tt:128 * tt + 128],
                                        CO_IDENT), "dv_T")
                    T("scalar", [f"ps{b2_}"], [f"YV{tt}"],
                      lambda eng, tt=tt, fo=fo, b2_=b2_:
                          eng.copy(YA[:, 1024 * tt + 128 * fo:1024 * tt + 128 * fo + 128],
                                   PS[b2_][:, 0:128]), "dv_Te")

        def store_spill(dst, tag):
            def f(fo, si, bank):
                s0 = 512 * si
                T("scalar", [f"ps{bank}", "X4"], ["X4"],
                  lambda eng, bank=bank, s0=s0:
                      eng.copy(xa16(4)[:, s0:s0 + 512], PS[bank][:, 0:512]), "dq_evh")
                T("vector", [f"ps{bank}", "X4"], ["X4"],
                  lambda eng, bank=bank, s0=s0:
                      eng.tensor_tensor(xa16(4)[:, 2048 + s0:2048 + s0 + 512],
                                        PS[bank][:, 0:512], xa16(4)[:, s0:s0 + 512],
                                        op=OP.subtract), "dq_evl")
                if si == 3:
                    dma("sdma2", dst[fo, :, :], xa16(4)[:, 0:4096], ["X4"], [f"{tag}{fo}"],
                        "dq_sp")
            return f

        proj2(Wv_d, store_v, "v")
        proj2(Wq_d, store_spill(qsp, "qq"), "q")
        proj2(Wk_d, store_spill(ksp, "kk"), "k")
        for c in range(8):
            dma("sync", xa(c), xsp[c, :, :], [f"xsp{c}", f"X{c}"], [f"X{c}"], "xrest_d")

        for g in range(8):
            q0 = 256 * g
            qg = fa16(0)[:, 0:4096].rearrange("p (c n) -> p c n", c=8)
            src_h = bass.AP(qsp.tensor, offset=q0, ap=[[2 * S, 128], [128 * 2 * S, 8], [1, 256]])
            src_l = bass.AP(qsp.tensor, offset=S + q0,
                            ap=[[2 * S, 128], [128 * 2 * S, 8], [1, 256]])
            dma("sync", qg[:, :, 0:256], src_h,
                ["F0"] + [f"qq{fo}" for fo in range(8)], ["F0"], "ldqgh")
            dma("sync", qg[:, :, 256:512], src_l, ["F0"], ["F0"], "ldqgl")
            for kspan in range(4):
                ks0 = 512 * kspan
                base = 1 + 2 * (kspan % 2)
                ksl = FA16[:, 4096 * base:4096 * base + 8192]
                kkeys = [f"F{base}", f"F{base + 1}"]
                kg = ksl.rearrange("p (c n) -> p c n", c=8)
                srck_h = bass.AP(ksp.tensor, offset=ks0,
                                 ap=[[2 * S, 128], [128 * 2 * S, 8], [1, 512]])
                srck_l = bass.AP(ksp.tensor, offset=S + ks0,
                                 ap=[[2 * S, 128], [128 * 2 * S, 8], [1, 512]])
                dma("sync", kg[:, :, 0:512], srck_h,
                    kkeys + [f"kk{fo}" for fo in range(8)], kkeys, "ldkgh")
                dma("sync", kg[:, :, 512:1024], srck_l, kkeys, kkeys, "ldkgl")
                for qt in range(2):
                    bank = qt
                    for pi, (qh, kh) in enumerate([(0, 0), (0, 1), (1, 0)]):
                        for c in range(8):
                            T("tensor", ["F0"] + kkeys, [f"ps{bank}"],
                              lambda eng, c=c, qt=qt, bank=bank, ksl=ksl, pi=pi, qh=qh, kh=kh:
                                  eng.matmul(PS[bank][:, 0:512],
                                             fa16(0)[:, 512 * c + 256 * qh + 128 * qt:
                                                     512 * c + 256 * qh + 128 * qt + 128],
                                             ksl[:, 1024 * c + 512 * kh:1024 * c + 512 * kh + 512],
                                             start=(pi == 0 and c == 0),
                                             stop=(pi == 2 and c == 7)), "ds_mm")
                    T("scalar", [f"ps{bank}", f"F{5 + qt}"], [f"F{5 + qt}"],
                      lambda eng, qt=qt, bank=bank, ks0=ks0:
                          eng.copy(fa(5 + qt)[:, ks0:ks0 + 512], PS[bank][:, 0:512]), "ds_ev")
            for qt in range(2):
                st = fa(5 + qt)
                skey = f"F{5 + qt}"
                mx = CO[:, 1408 + qt:1409 + qt]
                sm = CO[:, 1410 + qt:1411 + qt]
                rc = CO[:, 1412 + qt:1413 + qt]
                T("vector", [skey, "BRHS"], ["BRHS"],
                  lambda eng, st=st, mx=mx:
                      eng.tensor_reduce(mx, st[:, 0:2048], axis=AX.X, op=OP.max), "dsm_max")
                T("vector", ["BRHS"], ["BRHS"],
                  lambda eng, mx=mx: eng.tensor_scalar(mx, mx, -1.0 / 32.0, None, op0=OP.mult),
                  "dsm_neg")
                T("scalar", [skey, "BRHS"], [skey, "BRHS"],
                  lambda eng, st=st, mx=mx, sm=sm:
                      eng.activation(st[:, 0:2048], st[:, 0:2048], AF.Exp, bias=mx,
                                     scale=1.0 / 32.0, accum_out=sm), "dsm_exp")
                T("vector", ["BRHS"], ["BRHS"],
                  lambda eng, sm=sm, rc=rc: eng.reciprocal(rc, sm), "dsm_rec")
                T("vector", [skey, "BRHS"], [skey],
                  lambda eng, st=st, rc=rc:
                      eng.tensor_scalar(st[:, 0:2048], st[:, 0:2048], rc, None, op0=OP.mult),
                  "dsm_nrm")
                if stage == "densedbgP" and g == 0:
                    dma("sync", y_d[256 * qt:256 * qt + 128, :], st[:, 0:1024],
                        [skey], ["dbg"], "dbgP")
                    dma("sync", y_d[256 * qt + 128:256 * qt + 256, :], st[:, 1024:2048],
                        [skey], ["dbg"], "dbgP2")
                for kc in range(16):
                    b2_ = 2 + (kc % 2)
                    pt_t = fa(0) if kc < 8 else fa(7)
                    pt_key = "F0" if kc < 8 else "F7"
                    T("tensor", [skey, "C"], [f"ps{b2_}"],
                      lambda eng, st=st, kc=kc, b2_=b2_:
                          eng.transpose(PS[b2_][:, 0:128], st[:, 128 * kc:128 * kc + 128],
                                        CO_IDENT), "dsm_T")
                    T("scalar", [f"ps{b2_}", pt_key], [pt_key],
                      lambda eng, kc=kc, qt=qt, b2_=b2_, pt_t=pt_t:
                          eng.copy(pt_t[:, 256 * (kc % 8) + 128 * qt:256 * (kc % 8) + 128 * qt + 128],
                                   PS[b2_][:, 0:128]), "dsm_Te")
            for kc in range(16):
                pt_t = fa(0) if kc < 8 else fa(7)
                pt_key = "F0" if kc < 8 else "F7"
                for dc in range(8):
                    T("tensor", [pt_key, f"YV{kc}"], [f"ps{dc}"],
                      lambda eng, kc=kc, dc=dc, pt_t=pt_t:
                          eng.matmul(PS[dc][:, 0:256],
                                     YA[:, 1024 * kc + 128 * dc:1024 * kc + 128 * dc + 128],
                                     pt_t[:, 256 * (kc % 8):256 * (kc % 8) + 256],
                                     start=(kc == 0), stop=(kc == 15)), "dav_mm")
            for dc in range(8):
                T("vector", [f"ps{dc}", f"X{dc}"], [f"X{dc}"],
                  lambda eng, dc=dc, q0=q0:
                      eng.tensor_tensor(xa(dc)[:, q0:q0 + 256], xa(dc)[:, q0:q0 + 256],
                                        PS[dc][:, 0:256], op=OP.add), "dav_resid")

    try:
        r = emit_ffn(0)
    except StopIteration:
        return bk.finalize()
    if stage in ("full", "dense", "densedbgP"):
        emit_dense()
    if stage == "full":
        emit_ffn(1)

    if stage == "densedbgP":
        return bk.finalize()
    out_is_y = (r == "partial")
    outsrc = (lambda c: ya(c)) if out_is_y else (lambda c: xa(c))
    okeys = [f"Y{c}" for c in range(8)] if out_is_y else [f"X{c}" for c in range(8)]
    for tt in range(16):
        stg = fa(4)[:, 0:1024]
        for c in range(8):
            bank = c % 2
            T("tensor", [okeys[c], "C"], [f"ps{bank}"],
              lambda eng, c=c, tt=tt, bank=bank:
                  eng.transpose(PS[bank][:, 0:128], outsrc(c)[:, 128 * tt:128 * tt + 128],
                                CO_IDENT), "oT")
            T("scalar", [f"ps{bank}", "F4"], ["F4"],
              lambda eng, c=c, bank=bank, stg=stg:
                  eng.copy(stg[:, 128 * c:128 * c + 128], PS[bank][:, 0:128]), "oTe")
        dma("sync", y_d[128 * tt:128 * tt + 128, :], stg, ["F4"], ["F4"], "sto")

    return bk.finalize()


def _in_maps(L0_attn_x, L0_attn_Wq, L0_attn_Wk, L0_attn_Wv, L0_ffn_wqkv,
             L0_ffn_q_scale, L0_ffn_k_scale, L0_ffn_wo, L0_ffn_ln_g, L0_ffn_ln_b,
             L0_ffn_w1, L0_ffn_b1, L0_ffn_w2, L0_ffn_b2):
    cvt = lambda a: np.ascontiguousarray(np.asarray(a, dtype=np.float32))
    x = cvt(L0_attn_x)
    common = {"Wq": cvt(L0_attn_Wq), "Wk": cvt(L0_attn_Wk), "Wv": cvt(L0_attn_Wv),
              "wqkv": cvt(L0_ffn_wqkv), "qs": cvt(L0_ffn_q_scale), "ks": cvt(L0_ffn_k_scale),
              "wo": cvt(L0_ffn_wo), "lng": cvt(L0_ffn_ln_g), "lnb": cvt(L0_ffn_ln_b),
              "w1": cvt(L0_ffn_w1), "b1": cvt(L0_ffn_b1), "w2": cvt(L0_ffn_w2),
              "b2": cvt(L0_ffn_b2)}
    return [dict(common, x=np.ascontiguousarray(x[b])) for b in range(NCORE)]


def kernel(**inputs):
    nc = build_program("full")
    in_maps = _in_maps(**inputs)
    res = run_bass_kernel_spmd(nc, in_maps, core_ids=list(range(NCORE)))
    out = np.stack([res.results[b]["y"] for b in range(NCORE)], axis=0)
    return out.astype(np.float32)


# version 10 (final)
# speedup vs baseline: 1.0100x; 1.0100x over previous
import numpy as np
import concourse.bass as bass
import concourse.mybir as mybir
from concourse.bass_utils import run_bass_kernel_spmd

F32 = mybir.dt.float32
F16 = mybir.dt.float16
AF = mybir.ActivationFunctionType
OP = mybir.AluOpType
AX = mybir.AxisListType

S = 2048
DM = 1024
DH = 64
INNER = 512
HID = 2730
NKI = 22
KI_LAST = 2730 - 21 * 128
NCORE = 8

PASSES1 = [(0, 0), (0, 2048), (2048, 0)]
PASSES2 = [(0, 0), (0, 512), (1024, 0)]
ENGINES = ("tensor", "vector", "scalar", "gpsimd", "sync", "sdma2", "gdma")
DMA_Q = ("sync", "sdma2", "gdma")
SPANS = [(s, s + 512) for s in range(0, S, 512)]


def make_consts():
    c = {}
    c["IDENT"] = np.eye(128, dtype=np.float32)
    prot = np.zeros((128, 128), dtype=np.float64)
    for b in (0, 64):
        for ml in range(32):
            prot[b + ml + 32, b + ml] = -1.0
        for ml in range(32, 64):
            prot[b + ml - 32, b + ml] = 1.0
    c["PROT"] = prot.astype(np.float32)
    bc0 = np.zeros((128, 128), dtype=np.float32); bc0[0, :] = 1.0
    c["BC0"] = bc0
    bc64 = np.zeros((128, 128), dtype=np.float32); bc64[64, :] = 1.0
    c["BC64"] = bc64
    bch = np.zeros((128, 128), dtype=np.float32)
    bch[0, 0:64] = 1.0; bch[1, 64:128] = 1.0
    c["BCH"] = bch
    c["SELA"] = np.ones((128, 1), dtype=np.float32)
    selh = np.zeros((128, 2), dtype=np.float32)
    selh[0:64, 0] = 1.0; selh[64:128, 1] = 1.0
    c["SELH"] = selh
    d = np.arange(64)
    i32 = d % 32
    inv_freq = 1.0 / (10000.0 ** (2.0 * i32 / 64.0))
    base = (2.0 * i32 + 0.4 * 64.0) / (1.4 * 64.0)
    t = np.arange(S, dtype=np.float64)
    freqs = t[None, :] * inv_freq[:, None]
    power = (t - S // 2) / 256.0
    scale = base[:, None] ** power[None, :]
    cosf, sinf = np.cos(freqs), np.sin(freqs)
    dup = lambda a: np.concatenate([a, a], axis=0).astype(np.float32)
    c["CQ"] = dup(cosf * scale)
    c["SQ"] = dup(sinf * scale)
    c["CK"] = dup(cosf / scale)
    c["SK"] = dup(sinf / scale)
    return c


NDMASEM = 24


class BK:

    def __init__(self, nc):
        self.nc = nc
        self.tasks = []
        self.res = {}
        self.count = {e: 0 for e in ENGINES}
        self.dma_idx = 0
        self.gdma_idx = 0
        self.duse = [0] * NDMASEM
        self.waited = {e: {} for e in ("tensor", "vector", "scalar", "gpsimd", "sync")}

    def task(self, eng, reads, writes, fn, name=""):
        stream = {"sdma2": "scalar", "gdma": "gpsimd"}.get(eng, eng)
        waits = {}
        def dep(d):
            dsem, dval = d
            waits[dsem] = max(waits.get(dsem, 0), dval)
        for r in reads:
            e = self.res.get(r)
            if e and e["w"]:
                dep(e["w"])
        for w in writes:
            e = self.res.get(w)
            if e:
                if e["w"]:
                    dep(e["w"])
                for rr in e["r"]:
                    dep(rr)
        if eng in DMA_Q:
            if eng == "gdma":
                sid = 18 + self.gdma_idx % 6
                self.gdma_idx += 1
            else:
                sid = self.dma_idx % 18
                self.dma_idx += 1
            self.duse[sid] += 1
            if self.duse[sid] > 1:
                waits[f"d{sid}"] = max(waits.get(f"d{sid}", 0), 16 * (self.duse[sid] - 1))
            me = (f"d{sid}", 16 * self.duse[sid])
        else:
            self.count[eng] += 1
            me = (eng, self.count[eng])
        if eng == "tensor":
            waits.pop("tensor", None)
        wd = self.waited[stream]
        eff = {}
        for se, val in waits.items():
            if se == stream and eng not in DMA_Q:
                pass
            if wd.get(se, -1) >= val:
                continue
            eff[se] = val
            wd[se] = val
        for r in reads:
            self.res.setdefault(r, {"w": None, "r": []})["r"].append(me)
        for w in writes:
            self.res[w] = {"w": me, "r": []}
        self.tasks.append({"eng": eng, "waits": eff, "fn": fn, "name": name,
                           "me": me})

    def finalize(self):
        nc = self.nc
        sems = {e: nc.alloc_semaphore(f"sem_{e}") for e in
                ("tensor", "vector", "scalar", "gpsimd", "sync")}
        for i in range(NDMASEM):
            sems[f"d{i}"] = nc.alloc_semaphore(f"dsem_{i}")
        with nc.Block() as block:
            def mk(ename, tasklist):
                def body(eng):
                    for t in tasklist:
                        for se, val in sorted(t["waits"].items()):
                            eng.wait_ge(sems[se], val)
                        inst = t["fn"](eng)
                        if inst is None:
                            inst = eng.nop()
                        tsem, tval = t["me"]
                        inst.then_inc(sems[tsem], 16 if t["eng"] in DMA_Q else 1)
                    if ename == "sync":
                        for e2 in ("tensor", "vector", "scalar", "gpsimd"):
                            if self.count[e2]:
                                eng.wait_ge(sems[e2], self.count[e2])
                        for i in range(NDMASEM):
                            if self.duse[i]:
                                eng.wait_ge(sems[f"d{i}"], 16 * self.duse[i])
                return body
            per = {}
            per["tensor"] = [t for t in self.tasks if t["eng"] == "tensor"]
            per["vector"] = [t for t in self.tasks if t["eng"] == "vector"]
            per["scalar"] = [t for t in self.tasks if t["eng"] in ("scalar", "sdma2")]
            per["gpsimd"] = [t for t in self.tasks if t["eng"] in ("gpsimd", "gdma")]
            per["sync"] = [t for t in self.tasks if t["eng"] == "sync"]
            block.tensor(mk("tensor", per["tensor"]))
            block.vector(mk("vector", per["vector"]))
            block.scalar(mk("scalar", per["scalar"]))
            block.gpsimd(mk("gpsimd", per["gpsimd"]))
            block.sync(mk("sync", per["sync"]))
        return nc


def build_program(stage="full"):
    nc = bass.Bass("TRN2", target_bir_lowering=False)
    C = make_consts()

    x_d = nc.dram_tensor("x", [S, DM], F32, kind="ExternalInput").ap()
    y_d = nc.dram_tensor("y", [S, DM], F32, kind="ExternalOutput").ap()
    Wq_d = nc.dram_tensor("Wq", [DM, DM], F32, kind="ExternalInput").ap()
    Wk_d = nc.dram_tensor("Wk", [DM, DM], F32, kind="ExternalInput").ap()
    Wv_d = nc.dram_tensor("Wv", [DM, DM], F32, kind="ExternalInput").ap()
    wqkv_d = nc.dram_tensor("wqkv", [2, DM, 3 * INNER], F32, kind="ExternalInput").ap()
    qs_d = nc.dram_tensor("qs", [2, DH], F32, kind="ExternalInput").ap()
    ks_d = nc.dram_tensor("ks", [2, DH], F32, kind="ExternalInput").ap()
    wo_d = nc.dram_tensor("wo", [2, INNER, DM], F32, kind="ExternalInput").ap()
    lng_d = nc.dram_tensor("lng", [2, DM], F32, kind="ExternalInput").ap()
    lnb_d = nc.dram_tensor("lnb", [2, DM], F32, kind="ExternalInput").ap()
    w1_d = nc.dram_tensor("w1", [2, DM, 2 * HID], F32, kind="ExternalInput").ap()
    b1_d = nc.dram_tensor("b1", [2, 2 * HID], F32, kind="ExternalInput").ap()
    w2_d = nc.dram_tensor("w2", [2, HID, DM], F32, kind="ExternalInput").ap()
    b2_d = nc.dram_tensor("b2", [2, DM], F32, kind="ExternalInput").ap()

    xsp = nc.dram_tensor("xsp", [8, 128, S], F32).ap()
    qsp = nc.dram_tensor("qsp", [8, 128, 2 * S], F16).ap()
    ksp = nc.dram_tensor("ksp", [8, 128, 2 * S], F16).ap()
    otsp = nc.dram_tensor("otsp", [4, 128, S], F32).ap()
    gsp = nc.dram_tensor("gsp", [NKI, 128, 2 * S], F16).ap()
    w1sp = nc.dram_tensor("w1sp", [2, NKI, 128, 4096], F16).ap()
    w2sp = nc.dram_tensor("w2sp", [2, NKI, 128, 2048], F16).ap()

    consts_d = {k: nc.inline_tensor(v, name=f"c_{k}").ap() for k, v in C.items()}

    XA = nc.alloc_sbuf_tensor("XA", [128, 8 * S], F32).ap()
    YA = nc.alloc_sbuf_tensor("YA", [128, 8 * S], F32).ap()
    FA = nc.alloc_sbuf_tensor("FA", [128, 8 * S], F32).ap()
    CO = nc.alloc_sbuf_tensor("CO", [128, 1856], F32).ap()
    GEL = nc.alloc_sbuf_tensor("GEL", [128, 1024], F32).ap()
    PS = [nc.alloc_psum_tensor(f"PS{i}", [128, 512], F32).ap() for i in range(8)]

    xa = lambda i: XA[:, 2048 * i:2048 * (i + 1)]
    ya = lambda i: YA[:, 2048 * i:2048 * (i + 1)]
    fa = lambda i: FA[:, 2048 * i:2048 * (i + 1)]
    XAr = XA.rearrange("p (c n) -> p c n", c=8)
    YAr = YA.rearrange("p (c n) -> p c n", c=8)
    FAr = FA.rearrange("p (c n) -> p c n", c=8)
    XA16 = XA.bitcast(F16)
    FA16 = FA.bitcast(F16)
    xa16 = lambda i: XA16[:, 4096 * i:4096 * (i + 1)]
    fa16 = lambda i: FA16[:, 4096 * i:4096 * (i + 1)]

    CO_IDENT = CO[:, 0:128]
    CO_PROT = CO[:, 128:256]
    CO_BC0 = CO[:, 256:384]
    CO_BC64 = CO[:, 384:512]
    CO_BCH = CO[:, 512:640]
    CO_SELA = CO[:, 640:641]
    CO_SELH = CO[:, 641:643]
    CO_EPSLN = CO[:, 643:644]
    CO_QS = [CO[:, 644 + l:645 + l] for l in range(2)]
    CO_KS = [CO[:, 646 + l:647 + l] for l in range(2)]
    CO_B2 = [CO[:, 648 + 8 * l:656 + 8 * l] for l in range(2)]
    CO_LNG = [CO[:, 664 + 8 * l:672 + 8 * l] for l in range(2)]
    CO_LNB = [CO[:, 680 + 8 * l:688 + 8 * l] for l in range(2)]
    CO_B1A = [CO[:, 696 + NKI * l:696 + NKI * (l + 1)] for l in range(2)]
    CO_B1G = [CO[:, 740 + NKI * l:740 + NKI * (l + 1)] for l in range(2)]
    CO_BRHS = CO[:, 832:1856]

    bk = BK(nc)
    T = bk.task

    def dma(qeng, dst, src, reads, writes, name="", slow=False):
        if slow:
            T(qeng, reads, writes,
              lambda eng, d=dst, s=src: eng.dma_start(out=d, in_=s,
                                                      allow_slow_non_contiguous=True), name)
        else:
            T(qeng, reads, writes, lambda eng, d=dst, s=src: eng.dma_start(out=d, in_=s), name)

    for key, dst in [("IDENT", CO_IDENT), ("PROT", CO_PROT), ("BC0", CO_BC0),
                     ("BC64", CO_BC64), ("BCH", CO_BCH), ("SELA", CO_SELA),
                     ("SELH", CO_SELH)]:
        dma("sync", dst, consts_d[key][:, :], ["C"], ["C"], f"ld_{key}")
    T("vector", ["C"], ["C"], lambda eng: eng.memset(CO_EPSLN, 1e-5), "eps")
    T("vector", ["BRHS"], ["BRHS"], lambda eng: eng.memset(CO_BRHS, 0.0), "brhs0")
    for l in range(2):
        for half in (0, 1):
            dma("sync", CO_QS[l][64 * half:64 * half + 64, :],
                qs_d[l:l + 1, :].transpose([1, 0]), ["C"], ["C"], "qs", slow=True)
            dma("sync", CO_KS[l][64 * half:64 * half + 64, :],
                ks_d[l:l + 1, :].transpose([1, 0]), ["C"], ["C"], "ks", slow=True)
        dma("sync", CO_B2[l], b2_d[l:l + 1, :].rearrange("a (c p) -> (a p) c", p=128),
            ["C"], ["C"], "b2", slow=True)
        dma("sync", CO_LNG[l], lng_d[l:l + 1, :].rearrange("a (c p) -> (a p) c", p=128),
            ["C"], ["C"], "lng", slow=True)
        dma("sync", CO_LNB[l], lnb_d[l:l + 1, :].rearrange("a (c p) -> (a p) c", p=128),
            ["C"], ["C"], "lnb", slow=True)
        for ki in range(NKI):
            n = 128 if ki < NKI - 1 else KI_LAST
            dma("sync", CO_B1A[l][0:n, ki:ki + 1],
                b1_d[l:l + 1, 128 * ki:128 * ki + n].transpose([1, 0]), ["C"], ["C"], "b1a", slow=True)
            dma("sync", CO_B1G[l][0:n, ki:ki + 1],
                b1_d[l:l + 1, HID + 128 * ki:HID + 128 * ki + n].transpose([1, 0]),
                ["C"], ["C"], "b1g", slow=True)

    for tt in range(16):
        stg = fa(tt % 2)[:, 0:1024]
        skey = f"F{tt % 2}"
        dma("sync", stg, x_d[128 * tt:128 * tt + 128, :], [skey], [skey], "ldx")
        for c in range(8):
            ps, pk = PS[c % 2], f"ps{c % 2}"
            T("tensor", [skey, "C"], [pk],
              lambda eng, ps=ps, stg=stg, c=c:
                  eng.transpose(ps[:, 0:128], stg[:, 128 * c:128 * c + 128], CO_IDENT), "xT")
            T("scalar", [pk], [f"X{c}"],
              lambda eng, ps=ps, c=c, tt=tt:
                  eng.copy(xa(c)[:, 128 * tt:128 * tt + 128], ps[:, 0:128]), "xTe")

    def emit_ln(srcA4, srcB4, src, srckeys, out, outkeys, sqA, sqA_key, sqB, sqB_key,
                st_scr, st_key, gamma=None, beta=None, name="ln", split_out=None):
        for si, (s0, s1) in enumerate(SPANS):
            b_sx, k_sx = PS[(si % 2) * 2], f"ps{(si % 2) * 2}"
            b_sq, k_sq = PS[(si % 2) * 2 + 1], f"ps{(si % 2) * 2 + 1}"
            b_r, k_r = PS[4 + (si % 2) * 2], f"ps{4 + (si % 2) * 2}"
            b_m, k_m = PS[5 + (si % 2) * 2], f"ps{5 + (si % 2) * 2}"
            for c in range(8):
                T("tensor", [srckeys[c], "C"], [k_sx],
                  lambda eng, c=c, s0=s0, s1=s1, b=b_sx:
                      eng.matmul(b[0:1, 0:512], CO_SELA, src(c)[:, s0:s1],
                                 start=(c == 0), stop=(c == 7)), name + "_sx")
            T("scalar", srckeys[0:4] + [sqA_key], [sqA_key],
              lambda eng, s0=s0:
                  eng.activation(sqA.rearrange("p (c n) -> p c n", c=4),
                                 srcA4(s0), AF.Square), name + "_sqA")
            T("vector", srckeys[4:8] + [sqB_key], [sqB_key],
              lambda eng, s0=s0:
                  eng.tensor_tensor(sqB.rearrange("p (c n) -> p c n", c=4),
                                    srcB4(s0), srcB4(s0), op=OP.mult), name + "_sqB")
            for c in range(8):
                sq = (sqA if c < 4 else sqB)[:, 512 * (c % 4):512 * (c % 4) + 512]
                T("tensor", [(sqA_key if c < 4 else sqB_key), "C"], [k_sq],
                  lambda eng, c=c, b=b_sq, sq=sq:
                      eng.matmul(b[0:1, 0:512], CO_SELA, sq, start=(c == 0), stop=(c == 7)),
                  name + "_sq")
            st2 = st_scr
            T("scalar", [k_sx], [st_key],
              lambda eng, b=b_sx: eng.copy(st2[0:1, 0:512], b[0:1, 0:512]), name + "_ev1")
            T("scalar", [k_sq, st_key], [st_key],
              lambda eng, b=b_sq: eng.copy(st2[0:1, 512:1024], b[0:1, 0:512]), name + "_ev2")
            T("vector", [st_key], [st_key],
              lambda eng: eng.tensor_scalar(st2[0:1, 0:1024], st2[0:1, 0:1024],
                                            1.0 / DM, None, op0=OP.mult), name + "_sc")
            T("vector", [st_key], [st_key],
              lambda eng: eng.tensor_tensor(st2[0:1, 1024:1536], st2[0:1, 0:512],
                                            st2[0:1, 0:512], op=OP.mult), name + "_m2")
            T("vector", [st_key], [st_key],
              lambda eng: eng.tensor_tensor(st2[0:1, 1024:1536], st2[0:1, 512:1024],
                                            st2[0:1, 1024:1536], op=OP.subtract), name + "_var")
            T("scalar", [st_key, "C"], [st_key],
              lambda eng: eng.activation(st2[0:1, 1536:2048], st2[0:1, 1024:1536],
                                         AF.Ln, bias=CO_EPSLN[0:1, :]), name + "_ln")
            T("scalar", [st_key], ["BRHS"],
              lambda eng: eng.activation(CO_BRHS[0:1, 0:512], st2[0:1, 1536:2048],
                                         AF.Exp, scale=-0.5), name + "_exp")
            T("vector", [st_key, "BRHS"], ["BRHS"],
              lambda eng: eng.tensor_tensor(CO_BRHS[0:1, 512:1024], st2[0:1, 0:512],
                                            CO_BRHS[0:1, 0:512], op=OP.mult), name + "_mr")
            T("tensor", ["BRHS", "C"], [k_r],
              lambda eng, b=b_r: eng.matmul(b[:, 0:512], CO_BC0, CO_BRHS[:, 0:512],
                                            start=True, stop=True), name + "_bcr")
            T("tensor", ["BRHS", "C"], [k_m],
              lambda eng, b=b_m: eng.matmul(b[:, 0:512], CO_BC0, CO_BRHS[:, 512:1024],
                                            start=True, stop=True), name + "_bcm")
            for c in range(8):
                if split_out is None:
                    dst = lambda c=c, s0=s0, s1=s1: out(c)[:, s0:s1]
                    dkey = outkeys[c]
                    T("vector", [srckeys[c], k_r], [dkey],
                      lambda eng, c=c, s0=s0, s1=s1, b=b_r, dst=dst:
                          eng.tensor_tensor(dst(), src(c)[:, s0:s1], b[:, 0:512],
                                            op=OP.mult), name + "_ap1")
                    T("vector", [dkey, k_m], [dkey],
                      lambda eng, b=b_m, dst=dst:
                          eng.tensor_tensor(dst(), dst(), b[:, 0:512],
                                            op=OP.subtract), name + "_ap2")
                else:
                    hi_fn, lo_fn = split_out
                    tmp = GEL[:, 512 * (c % 2):512 * (c % 2) + 512]
                    tk = f"GE{c % 2}"
                    T("vector", [srckeys[c], k_r, tk], [tk],
                      lambda eng, c=c, s0=s0, s1=s1, b=b_r, tmp=tmp:
                          eng.tensor_tensor(tmp, src(c)[:, s0:s1], b[:, 0:512],
                                            op=OP.mult), name + "_ap1")
                    T("vector", [tk, k_m], [tk],
                      lambda eng, b=b_m, tmp=tmp:
                          eng.tensor_tensor(tmp, tmp, b[:, 0:512],
                                            op=OP.subtract), name + "_ap2")
                    T("vector", [tk, "C"], [tk],
                      lambda eng, c=c, tmp=tmp:
                          eng.tensor_scalar(tmp, tmp, gamma[:, c:c + 1], beta[:, c:c + 1],
                                            op0=OP.mult, op1=OP.add), name + "_gb")
                    T("scalar", [tk, outkeys[c]], [outkeys[c]],
                      lambda eng, c=c, s0=s0, tmp=tmp, hi_fn=hi_fn:
                          eng.copy(hi_fn(c)[:, s0:s0 + 512], tmp), name + "_hi")
                    T("vector", [tk, outkeys[c]], [outkeys[c]],
                      lambda eng, c=c, s0=s0, tmp=tmp, hi_fn=hi_fn, lo_fn=lo_fn:
                          eng.tensor_tensor(lo_fn(c)[:, s0:s0 + 512], tmp,
                                            hi_fn(c)[:, s0:s0 + 512], op=OP.subtract),
                      name + "_lo")

    def ldwqkv(l, p):
        wsl = []
        for c in range(8):
            if c < 5:
                wsl.append((xa(3)[:, 384 * c:384 * (c + 1)], "X3"))
            else:
                wsl.append((xa(4)[:, 384 * (c - 5):384 * (c - 4)], "X4"))
        for c in range(8):
            src = bass.AP(wqkv_d.tensor,
                          offset=l * DM * 3 * INNER + (128 * c) * (3 * INNER) + 128 * p,
                          ap=[[3 * INNER, 128], [INNER, 3], [1, 128]])
            dma("sync", wsl[c][0].rearrange("p (b n) -> p b n", b=3), src,
                [wsl[c][1]], [wsl[c][1]], "ldwqkv")
        return wsl

    def emit_pair(inv, l, p, wsl):
        lt = f"i{inv}l{l}p{p}"
        PQ, PK, PV = xa(0), xa(1), xa(2)
        kPQ, kPK, kPV = "X0", "X1", "X2"
        def qkv_rhs(c):
            return ya(c) if c < 4 else fa(c - 4)
        qkv_rkey = [f"Y{c}" if c < 4 else f"F{c - 4}" for c in range(8)]
        for si, (s0, s1) in enumerate(SPANS):
            for c in range(8):
                w, wkey = wsl[c]
                for t_i, bank in enumerate([0, 1, 2]):
                    T("tensor", [wkey, qkv_rkey[c]], [f"ps{bank}"],
                      lambda eng, c=c, t_i=t_i, bank=bank, s0=s0, s1=s1, w=w:
                          eng.matmul(PS[bank][:, 0:512], w[:, 128 * t_i:128 * t_i + 128],
                                     qkv_rhs(c)[:, s0:s1], start=(c == 0), stop=(c == 7)),
                      "qkv_mm")
            for bank, (dst, key) in enumerate([(PQ, kPQ), (PK, kPK), (PV, kPV)]):
                T("scalar", [f"ps{bank}"], [key],
                  lambda eng, bank=bank, dst=dst, s0=s0:
                      eng.copy(dst[:, s0:s0 + 512], PS[bank][:, 0:512]), "qkv_ev")
        for TP, tkey, cc_, ss_, scl in [(PQ, kPQ, "CQ", "SQ", CO_QS[l]),
                                        (PK, kPK, "CK", "SK", CO_KS[l])]:
            dma("sync", xa(3), consts_d[cc_][:, :], ["X3"], ["X3"], "ldC")
            dma("sync", xa(4), consts_d[ss_][:, :], ["X4"], ["X4"], "ldS")
            for si, (s0, s1) in enumerate(SPANS):
                bss, brq, brot = PS[3], PS[4], PS[5]
                sq = fa(4)[:, 0:512]
                T("scalar", [tkey, "F4"], ["F4"],
                  lambda eng, TP=TP, s0=s0, sq=sq:
                      eng.activation(sq, TP[:, s0:s0 + 512], AF.Square), "l2_sq")
                T("tensor", ["F4", "C"], ["ps3"],
                  lambda eng, sq=sq:
                      eng.matmul(bss[0:2, 0:512], CO_SELH, sq, start=True, stop=True), "l2_mm")
                T("vector", ["ps3"], ["ps3"],
                  lambda eng: eng.tensor_scalar(bss[0:2, 0:512], bss[0:2, 0:512], 1e-18,
                                                None, op0=OP.max), "l2_clip")
                T("scalar", ["ps3"], ["ps3"],
                  lambda eng: eng.activation(bss[0:2, 0:512], bss[0:2, 0:512], AF.Ln), "l2_ln")
                T("scalar", ["ps3"], ["BRHS"],
                  lambda eng: eng.activation(CO_BRHS[0:2, 0:512], bss[0:2, 0:512],
                                             AF.Exp, scale=-0.5), "l2_exp")
                T("tensor", ["BRHS", "C"], ["ps4"],
                  lambda eng: eng.matmul(brq[:, 0:512], CO_BCH, CO_BRHS[:, 0:512],
                                         start=True, stop=True), "l2_bc")
                T("vector", [tkey, "ps4", "C"], [tkey],
                  lambda eng, TP=TP, s0=s0, scl=scl:
                      eng.scalar_tensor_tensor(TP[:, s0:s0 + 512], TP[:, s0:s0 + 512],
                                               scl, brq[:, 0:512], op0=OP.mult, op1=OP.mult),
                  "l2_ap")
                T("tensor", [tkey, "C"], ["ps5"],
                  lambda eng, TP=TP, s0=s0:
                      eng.matmul(brot[:, 0:512], CO_PROT, TP[:, s0:s0 + 512],
                                 start=True, stop=True), "rot_mm")
                T("vector", [tkey, "X3", "F4"], ["F4"],
                  lambda eng, TP=TP, s0=s0:
                      eng.tensor_tensor(fa(4)[:, 512:1024], TP[:, s0:s0 + 512],
                                        xa(3)[:, s0:s0 + 512], op=OP.mult), "rot_t1")
                T("vector", ["ps5", "X4", "F5"], ["F5"],
                  lambda eng, s0=s0:
                      eng.tensor_tensor(fa(5)[:, 0:512], brot[:, 0:512],
                                        xa(4)[:, s0:s0 + 512], op=OP.mult), "rot_t2")
                T("vector", ["F4", "F5"], [tkey],
                  lambda eng, TP=TP, s0=s0:
                      eng.tensor_tensor(TP[:, s0:s0 + 512], fa(4)[:, 512:1024],
                                        fa(5)[:, 0:512], op=OP.add), "rot_add")
        nwsl = ldwqkv(l, p + 1) if p + 1 < 4 else None
        VT0 = 5 * 2048
        T("vector", ["X5"], ["X5"],
          lambda eng: eng.memset(XA[:, VT0 + 64:VT0 + 16 * 132:132], 1.0), "vt1")
        T("vector", ["X5"], ["X5"],
          lambda eng: eng.memset(XA[:, VT0 + 65:VT0 + 16 * 132:132], 1.0), "vt2")
        T("vector", ["X5", "X6"], ["X5", "X6"],
          lambda eng: eng.memset(XA[:, VT0 + 130:VT0 + 16 * 132:132], 1.0), "vt3")
        T("vector", ["X5", "X6"], ["X5", "X6"],
          lambda eng: eng.memset(XA[:, VT0 + 131:VT0 + 16 * 132:132], 1.0), "vt4")
        for tt in range(16):
            bank = 6 + (tt % 2)
            T("tensor", [kPV, "C"], [f"ps{bank}"],
              lambda eng, tt=tt, bank=bank:
                  eng.transpose(PS[bank][:, 0:128], PV[:, 128 * tt:128 * tt + 128], CO_IDENT),
              "vt_T")
            T("scalar", [f"ps{bank}"], ["X5", "X6"],
              lambda eng, tt=tt, bank=bank:
                  eng.copy(XA[:, VT0 + 132 * tt:VT0 + 132 * tt + 64], PS[bank][:, 0:64]),
              "vt_evA")
            T("scalar", [f"ps{bank}"], ["X5", "X6"],
              lambda eng, tt=tt, bank=bank:
                  eng.copy(XA[:, VT0 + 132 * tt + 66:VT0 + 132 * tt + 130],
                           PS[bank][:, 64:128]), "vt_evB")
        for w in range(4):
            emit_window(l, p, w, PQ, PK, kPQ, kPK, VT0)
        dma("sdma2", otsp[p, 0:64, :], xa(7)[0:64, :], ["X7"], [f"ot{p}"], "ot_spA")
        return nwsl

    def emit_window(l, p, w, PQ, PK, kPQ, kPK, VT0):
        s0 = 512 * w
        OTA = xa(7)
        first_av = [True, True]

        def geom(m):
            ca = 4 * (w - 1) + m if m < 4 else 4 * w + (m - 4)
            q_lo = 0 if m < 4 else 128 * (m - 4)
            q_hi = 128 * (m + 1) if m < 4 else 512
            return ca, q_lo, q_hi - q_lo

        def emit_st(m, i):
            ca, q_lo, n = geom(m)
            for h, (bank, base) in enumerate([((i % 2) * 2, 0), ((i % 2) * 2 + 1, 64)]):
                T("tensor", [kPQ, kPK], [f"ps{bank}"],
                  lambda eng, bank=bank, base=base, ca=ca, q_lo=q_lo, n=n:
                      eng.matmul(PS[bank][:, q_lo:q_lo + n],
                                 PK[base:base + 64, 128 * ca:128 * ca + 128],
                                 PQ[base:base + 64, s0 + q_lo:s0 + q_lo + n],
                                 start=True, stop=True), "st_mm")

        def emit_expmask_av(m, i):
            ca, q_lo, n = geom(m)
            for h, bank in enumerate([(i % 2) * 2, (i % 2) * 2 + 1]):
                qk = f"F6q{2 * (i % 2) + h}"
                off = 1024 * (i % 2) + 512 * h
                T("scalar", [f"ps{bank}", qk], [qk],
                  lambda eng, bank=bank, off=off, q_lo=q_lo, n=n:
                      eng.activation(fa(6)[:, off + q_lo:off + q_lo + n],
                                     PS[bank][:, q_lo:q_lo + n], AF.Exp, scale=8.0), "st_exp")
                if m < 4:
                    T("gpsimd", [qk], [qk],
                      lambda eng, off=off, m=m, n=n:
                          eng.affine_select(fa(6)[:, off:off + n], fa(6)[:, off:off + n],
                                            pattern=[[-1, n]], compare_op=OP.is_ge,
                                            fill=0.0, base=128 * m, channel_multiplier=1),
                      "mask_lb")
                else:
                    T("gpsimd", [qk], [qk],
                      lambda eng, off=off, q_lo=q_lo, n=n:
                          eng.affine_select(fa(6)[:, off + q_lo:off + q_lo + n],
                                            fa(6)[:, off + q_lo:off + q_lo + n],
                                            pattern=[[1, n]], compare_op=OP.is_ge,
                                            fill=0.0, base=0, channel_multiplier=-1),
                      "mask_iw")
            for h, bank in enumerate([4, 5]):
                qk = f"F6q{2 * (i % 2) + h}"
                off = 1024 * (i % 2) + 512 * h
                vt = XA[:, VT0 + 132 * ca + 66 * h:VT0 + 132 * ca + 66 * h + 66]
                st = first_av[h]
                T("tensor", [qk, "X5", "X6"], [f"ps{bank}"],
                  lambda eng, bank=bank, vt=vt, off=off, q_lo=q_lo, n=n, st=st:
                      eng.matmul(PS[bank][0:66, q_lo:q_lo + n], vt,
                                 fa(6)[:, off + q_lo:off + q_lo + n],
                                 start=st, stop=False, skip_group_check=True), "av_mm")
                first_av[h] = False

        ms = [m for m in [4, 0, 1, 2, 3, 5, 6, 7] if not (w == 0 and m < 4)]
        emit_st(ms[0], 0)
        for i, m in enumerate(ms):
            if i + 1 < len(ms):
                emit_st(ms[i + 1], i + 1)
            emit_expmask_av(m, i)
        RSS = CO[:, 832:1344]
        for h, bank in enumerate([4, 5]):
            T("scalar", [f"ps{bank}", "BRHS"], ["BRHS"],
              lambda eng, bank=bank: eng.copy(RSS[64:65, 0:512], PS[bank][64:65, 0:512]),
              "av_sums")
            T("tensor", ["BRHS", "C"], [f"ps{6 + h}"],
              lambda eng, h=h: eng.matmul(PS[6 + h][0:64, 0:512], CO_BC64[:, 0:64],
                                          RSS[:, 0:512], start=True, stop=True), "av_bc")
            T("scalar", [f"ps{6 + h}"], [f"ps{6 + h}"],
              lambda eng, h=h: eng.activation(PS[6 + h][0:64, 0:512], PS[6 + h][0:64, 0:512],
                                              AF.Ln), "av_lnr")
            T("scalar", [f"ps{6 + h}", "F4"], ["F4"],
              lambda eng, h=h: eng.activation(fa(4)[0:64, 512 * h:512 * h + 512],
                                              PS[6 + h][0:64, 0:512], AF.Exp, scale=-1.0),
              "av_rec")
        T("vector", ["ps4", "F4", "X7"], ["X7"],
          lambda eng: eng.tensor_tensor(OTA[0:64, s0:s0 + 512], PS[4][0:64, 0:512],
                                        fa(4)[0:64, 0:512], op=OP.mult), "av_evA")
        T("vector", ["ps5", "F4", "F5"], ["F5"],
          lambda eng: eng.tensor_tensor(fa(5)[0:64, 0:512], PS[5][0:64, 0:512],
                                        fa(4)[0:64, 512:1024], op=OP.mult), "av_evB")
        dma("sdma2", otsp[p, 64:128, s0:s0 + 512], fa(5)[0:64, 0:512], ["F5"], [f"ot{p}"],
            "ot_spB")

    def emit_ffn_layer(inv, l):
        for j in range(4):
            c = 4 + j
            T("vector", [f"F{j}"], [f"F{j}"],
              lambda eng, j=j: eng.memset(fa(j)[:, 0:1], 0.0), "shz")
            T("vector", [f"Y{c}", f"F{j}"], [f"F{j}"],
              lambda eng, j=j, c=c: eng.tensor_copy(fa(j)[:, 1:2048], ya(c)[:, 0:2047]), "shc")
        wsl = ldwqkv(l, 0)
        for p in range(4):
            wsl = emit_pair(inv, l, p, wsl)
        for j in range(4):
            dst = fa(j // 2)[:, 1024 * (j % 2):1024 * (j % 2) + 1024]
            dma("sync", dst, wo_d[l, 128 * j:128 * j + 128, :],
                [f"F{j // 2}"], [f"F{j // 2}"], "ldwo")
        for si, (s0, s1) in enumerate(SPANS):
            dma("sync", fa(6)[:, 0:512], otsp[0, :, s0:s1], ["F6q0", "ot0"], ["F6q0"], "ldot")
            for j in range(4):
                ots = fa(6)[:, 512 * j:512 * j + 512]
                if j + 1 < 4:
                    dma("sync", fa(6)[:, 512 * (j + 1):512 * (j + 1) + 512],
                        otsp[j + 1, :, s0:s1], [f"F6q{j + 1}", f"ot{j + 1}"],
                        [f"F6q{j + 1}"], "ldot")
                for fo in range(8):
                    wob = fa(j // 2)[:, 1024 * (j % 2) + 128 * fo:1024 * (j % 2) + 128 * fo + 128]
                    T("tensor", [f"F6q{j}", f"F{j // 2}"], [f"ps{fo}"],
                      lambda eng, fo=fo, j=j, wob=wob, ots=ots:
                          eng.matmul(PS[fo][:, 0:512], wob, ots, start=(j == 0), stop=(j == 3)),
                      "wo_mm")
            for fo in range(8):
                T("vector", [f"ps{fo}", f"Y{fo}"], [f"Y{fo}"],
                  lambda eng, fo=fo, s0=s0:
                      eng.tensor_tensor(ya(fo)[:, s0:s0 + 512], ya(fo)[:, s0:s0 + 512],
                                        PS[fo][:, 0:512], op=OP.add), "wo_resid")
        if stage == f"attn_i{inv}l{l}":
            return False
        emit_mlp(inv, l)
        return True

    def emit_mlp(inv, l):
        for j in range(4):
            c = 4 + j
            T("vector", [f"F{j}"], [f"F{j}"],
              lambda eng, j=j: eng.memset(fa(j)[:, 0:1], 0.0), "shz2")
            T("vector", [f"Y{c}", f"F{j}"], [f"F{j}"],
              lambda eng, j=j, c=c: eng.tensor_copy(fa(j)[:, 1:2048], ya(c)[:, 0:2047]), "shc2")
        hsrc = lambda c: (ya(c) if c < 4 else fa(c - 4))
        hkeys = [f"Y{c}" if c < 4 else f"F{c - 4}" for c in range(8)]
        h2h = lambda c: xa16(c)[:, 0:2048]
        h2l = lambda c: xa16(c)[:, 2048:4096]
        emit_ln(lambda s0: YAr[:, 0:4, s0:s0 + 512], lambda s0: FAr[:, 0:4, s0:s0 + 512],
                hsrc, hkeys, None, [f"X{c}" for c in range(8)],
                fa(4), "F4", fa(7), "F7", fa(5), "F5",
                gamma=CO_LNG[l], beta=CO_LNB[l], name=f"lnm{inv}{l}",
                split_out=(h2h, h2l))
        if stage == "mlpdbgH" and inv == 0 and l == 0:
            for c in range(8):
                dma("sync", y_d[128 * c:128 * c + 128, :], xa(c)[:, 0:1024],
                    [f"X{c}"], ["dbg"], "dbgH")
                dma("sync", y_d[1024 + 128 * c:1024 + 128 * c + 128, :], xa(c)[:, 1024:2048],
                    [f"X{c}"], ["dbg"], "dbgH2")
            raise StopIteration
        def ldw1(ki):
            n = 128 if ki < NKI - 1 else KI_LAST
            if inv == 0:
                stg = fa(6 + ki % 2)
                skeys = (["F6q0", "F6q1", "F6q2", "F6q3"] if ki % 2 == 0 else ["F7"])
                src = bass.AP(w1_d.tensor,
                              offset=l * DM * 2 * HID + 128 * ki,
                              ap=[[2 * HID, 128], [HID, 2], [2 * HID * 128, 8], [1, n]])
                dma("sync", stg.rearrange("p (b c n) -> p b c n", b=2, c=8)[:, :, :, 0:n],
                    src, skeys, skeys, "ldw1")
            else:
                w16 = fa16(ki % 2)
                wkey = f"F{ki % 2}"
                dma("sync", w16[:, :], w1sp[l, ki, :, :], [wkey, f"w1s{l}_{ki}"],
                    [wkey], "ldw1s")

        def split_w1(ki):
            if inv != 0:
                return
            stg = fa(6 + ki % 2)
            skeys = (["F6q0", "F6q1", "F6q2", "F6q3"] if ki % 2 == 0 else ["F7"])
            skey = skeys[0]
            w16 = fa16(ki % 2)
            wkey = f"F{ki % 2}"
            T("scalar", skeys + [wkey], [wkey],
              lambda eng, stg=stg, w16=w16: eng.copy(w16[:, 0:2048], stg[:, 0:2048]), "w1hi")
            T("vector", skeys + [wkey], [wkey],
              lambda eng, stg=stg, w16=w16:
                  eng.tensor_tensor(w16[:, 2048:4096], stg[:, 0:2048], w16[:, 0:2048],
                                    op=OP.subtract), "w1lo")
            dma("sdma2", w1sp[l, ki, :, :], w16[:, :], [wkey], [f"w1s{l}_{ki}"], "w1spl")

        ldw1(0)
        split_w1(0)
        if NKI > 1:
            ldw1(1)
        for ki in range(NKI):
            n = 128 if ki < NKI - 1 else KI_LAST
            w16 = fa16(ki % 2)
            wkey = f"F{ki % 2}"
            for si, (s0, s1) in enumerate(SPANS):
                if si == 1:
                    if ki + 1 < NKI:
                        split_w1(ki + 1)
                    if inv == 0 and ki + 2 < NKI:
                        ldw1(ki + 2)
                bA, kA = PS[(si % 2) * 2], f"ps{(si % 2) * 2}"
                bG, kG = PS[(si % 2) * 2 + 1], f"ps{(si % 2) * 2 + 1}"
                for half, (bank, kb) in [(0, (bA, kA)), (1, (bG, kG))]:
                    for pi, (w_off, x_off) in enumerate(PASSES1):
                        for c in range(8):
                            T("tensor", [wkey, f"X{c}"], [kb],
                              lambda eng, c=c, s0=s0, n=n, bank=bank, w16=w16,
                                     half=half, w_off=w_off, x_off=x_off, pi=pi:
                                  eng.matmul(bank[0:n, 0:512],
                                             w16[:, w_off + 1024 * half + 128 * c:
                                                 w_off + 1024 * half + 128 * c + n],
                                             xa16(c)[:, x_off + 512 * (s0 // 512):
                                                     x_off + 512 * (s0 // 512) + 512],
                                             start=(pi == 0 and c == 0),
                                             stop=(pi == len(PASSES1) - 1 and c == 7)), "u16")
                gsc, gk = GEL[:, 512 * (si % 2):512 * (si % 2) + 512], f"GE{si % 2}"
                T("scalar", [kG, "C", gk], [gk],
                  lambda eng, ki=ki, n=n, bG=bG, gsc=gsc:
                      eng.activation(gsc[0:n, 0:512], bG[0:n, 0:512], AF.Gelu,
                                     bias=CO_B1G[l][0:n, ki:ki + 1]), "gelu")
                T("vector", [kA, gk, "C", "F5"], ["F5"],
                  lambda eng, ki=ki, si=si, n=n, bA=bA, gsc=gsc:
                      eng.scalar_tensor_tensor(fa(5)[0:n, 512 * si:512 * si + 512],
                                               bA[0:n, 0:512], CO_B1A[l][0:n, ki:ki + 1],
                                               gsc[0:n, 0:512], op0=OP.add, op1=OP.mult),
                  "glu")
            if inv != 0 and ki + 2 < NKI:
                ldw1(ki + 2)
            g16 = fa16(2 + ki % 2)
            gkey = f"F{2 + ki % 2}"
            T("scalar", ["F5", gkey], [gkey],
              lambda eng, g16=g16, n=n: eng.copy(g16[0:n, 0:2048], fa(5)[0:n, :]), "ghi")
            T("vector", ["F5", gkey], [gkey],
              lambda eng, g16=g16, n=n:
                  eng.tensor_tensor(g16[0:n, 2048:4096], fa(5)[0:n, :], g16[0:n, 0:2048],
                                    op=OP.subtract), "glo")
            dma("sdma2", gsp[ki, 0:n, :], g16[0:n, :], [gkey], [f"g{ki}"], "g_sp")
        W2R = [(fa16(0)[:, 0:2048], "F0"), (fa16(1)[:, 0:2048], "F1"),
               (fa16(3)[:, 0:2048], "F3")]
        RSTG = [(fa(4)[:, 0:1024], "F4"), (fa(5)[:, 0:1024], "F5")]

        def p2_loads(ki, si, s0):
            n = 128 if ki < NKI - 1 else KI_LAST
            w16, wk = W2R[ki % 3]
            g16 = fa16(2)[:, 1024 * (ki % 3):1024 * (ki % 3) + 1024]
            if inv == 0 and si == 0:
                wstg, sk = RSTG[ki % 2]
                dma("sync", wstg[0:n, :], w2_d[l, 128 * ki:128 * ki + n, :],
                    [sk], [sk], "ldw2")
                T("scalar", [sk, wk], [wk],
                  lambda eng, wstg=wstg, w16=w16: eng.copy(w16[:, 0:1024], wstg[:, 0:1024]),
                  "w2hi")
                T("vector", [sk, wk], [wk],
                  lambda eng, wstg=wstg, w16=w16:
                      eng.tensor_tensor(w16[:, 1024:2048], wstg[:, 0:1024], w16[:, 0:1024],
                                        op=OP.subtract), "w2lo")
                dma("sdma2", w2sp[l, ki, :, :], w16[:, :], [wk], [f"w2s{l}_{ki}"], "w2spl")
            else:
                dma("sync", w16[:, :], w2sp[l, ki, :, :], [wk, f"w2s{l}_{ki}"], [wk],
                    "ldw2s")
            src_g = bass.AP(gsp.tensor,
                            offset=ki * 128 * 4096 + s0,
                            ap=[[4096, n], [2048, 2], [1, 512]])
            dma("gdma", g16.rearrange("p (h x) -> p h x", h=2)[0:n, :, :], src_g,
                [f"g{ki}", "F2"], ["F2"], "ldg")

        for si, (s0, s1) in enumerate(SPANS):
            p2_loads(0, si, s0)
            if NKI > 1:
                p2_loads(1, si, s0)
            for ki in range(NKI):
                n = 128 if ki < NKI - 1 else KI_LAST
                w16, wk = W2R[ki % 3]
                g16 = fa16(2)[:, 1024 * (ki % 3):1024 * (ki % 3) + 1024]
                if ki + 2 < NKI:
                    p2_loads(ki + 2, si, s0)
                for pi, (w_off, g_off) in enumerate(PASSES2):
                    for fo in range(8):
                        T("tensor", [wk, "F2"], [f"ps{fo}"],
                          lambda eng, fo=fo, ki=ki, n=n, w16=w16, g16=g16,
                                 w_off=w_off, g_off=g_off, pi=pi:
                              eng.matmul(PS[fo][:, 0:512],
                                         w16[0:n, w_off + 128 * fo:w_off + 128 * fo + 128],
                                         g16[0:n, g_off:g_off + 512],
                                         start=(ki == 0 and pi == 0),
                                         stop=(ki == NKI - 1 and pi == len(PASSES2) - 1)), "w216")
            for fo in range(8):
                T("vector", [f"ps{fo}", f"Y{fo}", "C"], [f"Y{fo}"],
                  lambda eng, fo=fo, s0=s0:
                      eng.scalar_tensor_tensor(ya(fo)[:, s0:s0 + 512], PS[fo][:, 0:512],
                                               CO_B2[l][:, fo:fo + 1], ya(fo)[:, s0:s0 + 512],
                                               op0=OP.add, op1=OP.add), "w2_resid")

    def emit_ffn(inv):
        emit_ln(lambda s0: XAr[:, 0:4, s0:s0 + 512], lambda s0: XAr[:, 4:8, s0:s0 + 512],
                lambda c: xa(c), [f"X{c}" for c in range(8)],
                lambda c: ya(c), [f"Y{c}" for c in range(8)],
                fa(4), "F4", fa(7), "F7", fa(5), "F5", name=f"ln0_{inv}")
        if stage == "ln0":
            return "partial"
        for c in range(8):
            dma("sdma2", xsp[c, :, :], xa(c), [f"X{c}"], [f"xsp{c}"], "xspill")
        for l in range(2):
            emit_ffn_layer(inv, l)
            if stage in (f"attn_i{inv}l{l}", f"ffn_i{inv}l{l}"):
                return "partial"
        for c in range(8):
            dma("sync", xa(c), xsp[c, :, :], [f"xsp{c}", f"X{c}"], [f"X{c}"], "xrest")
            T("vector", [f"X{c}", f"Y{c}"], [f"X{c}"],
              lambda eng, c=c: eng.scalar_tensor_tensor(xa(c), ya(c), 0.5, xa(c),
                                                        op0=OP.mult, op1=OP.add), "resid_ffn")
        return "complete"

    def emit_dense():
        emit_ln(lambda s0: XAr[:, 0:4, s0:s0 + 512], lambda s0: XAr[:, 4:8, s0:s0 + 512],
                lambda c: xa(c), [f"X{c}" for c in range(8)],
                lambda c: fa(c), [f"F{c}" for c in range(8)],
                ya(0), "Y0", ya(2), "Y2", ya(1), "Y1", name="lnd")
        for c in range(8):
            dma("sdma2", xsp[c, :, :], xa(c), [f"X{c}"], [f"xsp{c}"], "xspd")

        def proj2(wd, store, name):
            for c in range(8):
                wb = XA[:, 2048 * (c // 2) + 1024 * (c % 2):
                        2048 * (c // 2) + 1024 * (c % 2) + 1024]
                dma("sync", wb, wd[128 * c:128 * c + 128, :],
                    [f"X{c // 2}"], [f"X{c // 2}"], "ldW_" + name)
            for fo in range(8):
                for si, (s0, s1) in enumerate(SPANS):
                    bank = (fo * 4 + si) % 2
                    for c in range(8):
                        wb = XA[:, 2048 * (c // 2) + 1024 * (c % 2):
                                2048 * (c // 2) + 1024 * (c % 2) + 1024]
                        T("tensor", [f"X{c // 2}", f"F{c}"], [f"ps{bank}"],
                          lambda eng, c=c, fo=fo, s0=s0, bank=bank, wb=wb:
                              eng.matmul(PS[bank][:, 0:512], wb[:, 128 * fo:128 * fo + 128],
                                         fa(c)[:, s0:s0 + 512], start=(c == 0), stop=(c == 7)),
                          "dmm_" + name)
                    store(fo, si, bank)

        def store_v(fo, si, bank):
            s0 = 512 * si
            T("scalar", [f"ps{bank}", "X4"], ["X4"],
              lambda eng, bank=bank, s0=s0: eng.copy(xa(4)[:, s0:s0 + 512], PS[bank][:, 0:512]),
              "dv_ev")
            if si == 3:
                for tt in range(16):
                    b2_ = 4 + (tt % 2)
                    T("tensor", ["X4", "C"], [f"ps{b2_}"],
                      lambda eng, tt=tt, b2_=b2_:
                          eng.transpose(PS[b2_][:, 0:128], xa(4)[:, 128 * tt:128 * tt + 128],
                                        CO_IDENT), "dv_T")
                    T("scalar", [f"ps{b2_}"], [f"YV{tt}"],
                      lambda eng, tt=tt, fo=fo, b2_=b2_:
                          eng.copy(YA[:, 1024 * tt + 128 * fo:1024 * tt + 128 * fo + 128],
                                   PS[b2_][:, 0:128]), "dv_Te")

        def store_spill(dst, tag):
            def f(fo, si, bank):
                s0 = 512 * si
                T("scalar", [f"ps{bank}", "X4"], ["X4"],
                  lambda eng, bank=bank, s0=s0:
                      eng.copy(xa16(4)[:, s0:s0 + 512], PS[bank][:, 0:512]), "dq_evh")
                T("vector", [f"ps{bank}", "X4"], ["X4"],
                  lambda eng, bank=bank, s0=s0:
                      eng.tensor_tensor(xa16(4)[:, 2048 + s0:2048 + s0 + 512],
                                        PS[bank][:, 0:512], xa16(4)[:, s0:s0 + 512],
                                        op=OP.subtract), "dq_evl")
                if si == 3:
                    dma("sdma2", dst[fo, :, :], xa16(4)[:, 0:4096], ["X4"], [f"{tag}{fo}"],
                        "dq_sp")
            return f

        proj2(Wv_d, store_v, "v")
        proj2(Wq_d, store_spill(qsp, "qq"), "q")
        proj2(Wk_d, store_spill(ksp, "kk"), "k")
        for c in range(8):
            dma("sync", xa(c), xsp[c, :, :], [f"xsp{c}", f"X{c}"], [f"X{c}"], "xrest_d")

        for g in range(8):
            q0 = 256 * g
            qg = fa16(0)[:, 0:4096].rearrange("p (c n) -> p c n", c=8)
            src_h = bass.AP(qsp.tensor, offset=q0, ap=[[2 * S, 128], [128 * 2 * S, 8], [1, 256]])
            src_l = bass.AP(qsp.tensor, offset=S + q0,
                            ap=[[2 * S, 128], [128 * 2 * S, 8], [1, 256]])
            dma("sync", qg[:, :, 0:256], src_h,
                ["F0"] + [f"qq{fo}" for fo in range(8)], ["F0"], "ldqgh")
            dma("sync", qg[:, :, 256:512], src_l, ["F0"], ["F0"], "ldqgl")
            for kspan in range(4):
                ks0 = 512 * kspan
                base = 1 + 2 * (kspan % 2)
                ksl = FA16[:, 4096 * base:4096 * base + 8192]
                kkeys = [f"F{base}", f"F{base + 1}"]
                kg = ksl.rearrange("p (c n) -> p c n", c=8)
                srck_h = bass.AP(ksp.tensor, offset=ks0,
                                 ap=[[2 * S, 128], [128 * 2 * S, 8], [1, 512]])
                srck_l = bass.AP(ksp.tensor, offset=S + ks0,
                                 ap=[[2 * S, 128], [128 * 2 * S, 8], [1, 512]])
                dma("sync", kg[:, :, 0:512], srck_h,
                    kkeys + [f"kk{fo}" for fo in range(8)], kkeys, "ldkgh")
                dma("sync", kg[:, :, 512:1024], srck_l, kkeys, kkeys, "ldkgl")
                for qt in range(2):
                    bank = qt
                    for pi, (qh, kh) in enumerate([(0, 0), (0, 1), (1, 0)]):
                        for c in range(8):
                            T("tensor", ["F0"] + kkeys, [f"ps{bank}"],
                              lambda eng, c=c, qt=qt, bank=bank, ksl=ksl, pi=pi, qh=qh, kh=kh:
                                  eng.matmul(PS[bank][:, 0:512],
                                             fa16(0)[:, 512 * c + 256 * qh + 128 * qt:
                                                     512 * c + 256 * qh + 128 * qt + 128],
                                             ksl[:, 1024 * c + 512 * kh:1024 * c + 512 * kh + 512],
                                             start=(pi == 0 and c == 0),
                                             stop=(pi == 2 and c == 7)), "ds_mm")
                    T("scalar", [f"ps{bank}", f"F{5 + qt}"], [f"F{5 + qt}"],
                      lambda eng, qt=qt, bank=bank, ks0=ks0:
                          eng.copy(fa(5 + qt)[:, ks0:ks0 + 512], PS[bank][:, 0:512]), "ds_ev")
            for qt in range(2):
                st = fa(5 + qt)
                skey = f"F{5 + qt}"
                mx = CO[:, 1408 + qt:1409 + qt]
                sm = CO[:, 1410 + qt:1411 + qt]
                rc = CO[:, 1412 + qt:1413 + qt]
                T("vector", [skey, "BRHS"], ["BRHS"],
                  lambda eng, st=st, mx=mx:
                      eng.tensor_reduce(mx, st[:, 0:2048], axis=AX.X, op=OP.max), "dsm_max")
                T("vector", ["BRHS"], ["BRHS"],
                  lambda eng, mx=mx: eng.tensor_scalar(mx, mx, -1.0 / 32.0, None, op0=OP.mult),
                  "dsm_neg")
                T("scalar", [skey, "BRHS"], [skey, "BRHS"],
                  lambda eng, st=st, mx=mx, sm=sm:
                      eng.activation(st[:, 0:2048], st[:, 0:2048], AF.Exp, bias=mx,
                                     scale=1.0 / 32.0, accum_out=sm), "dsm_exp")
                T("vector", ["BRHS"], ["BRHS"],
                  lambda eng, sm=sm, rc=rc: eng.reciprocal(rc, sm), "dsm_rec")
                T("vector", [skey, "BRHS"], [skey],
                  lambda eng, st=st, rc=rc:
                      eng.tensor_scalar(st[:, 0:2048], st[:, 0:2048], rc, None, op0=OP.mult),
                  "dsm_nrm")
                if stage == "densedbgP" and g == 0:
                    dma("sync", y_d[256 * qt:256 * qt + 128, :], st[:, 0:1024],
                        [skey], ["dbg"], "dbgP")
                    dma("sync", y_d[256 * qt + 128:256 * qt + 256, :], st[:, 1024:2048],
                        [skey], ["dbg"], "dbgP2")
                for kc in range(16):
                    b2_ = 2 + (kc % 2)
                    pt_t = fa(0) if kc < 8 else fa(7)
                    pt_key = "F0" if kc < 8 else "F7"
                    T("tensor", [skey, "C"], [f"ps{b2_}"],
                      lambda eng, st=st, kc=kc, b2_=b2_:
                          eng.transpose(PS[b2_][:, 0:128], st[:, 128 * kc:128 * kc + 128],
                                        CO_IDENT), "dsm_T")
                    T("scalar", [f"ps{b2_}", pt_key], [pt_key],
                      lambda eng, kc=kc, qt=qt, b2_=b2_, pt_t=pt_t:
                          eng.copy(pt_t[:, 256 * (kc % 8) + 128 * qt:256 * (kc % 8) + 128 * qt + 128],
                                   PS[b2_][:, 0:128]), "dsm_Te")
            for kc in range(16):
                pt_t = fa(0) if kc < 8 else fa(7)
                pt_key = "F0" if kc < 8 else "F7"
                for dc in range(8):
                    T("tensor", [pt_key, f"YV{kc}"], [f"ps{dc}"],
                      lambda eng, kc=kc, dc=dc, pt_t=pt_t:
                          eng.matmul(PS[dc][:, 0:256],
                                     YA[:, 1024 * kc + 128 * dc:1024 * kc + 128 * dc + 128],
                                     pt_t[:, 256 * (kc % 8):256 * (kc % 8) + 256],
                                     start=(kc == 0), stop=(kc == 15)), "dav_mm")
            for dc in range(8):
                T("vector", [f"ps{dc}", f"X{dc}"], [f"X{dc}"],
                  lambda eng, dc=dc, q0=q0:
                      eng.tensor_tensor(xa(dc)[:, q0:q0 + 256], xa(dc)[:, q0:q0 + 256],
                                        PS[dc][:, 0:256], op=OP.add), "dav_resid")

    try:
        r = emit_ffn(0)
    except StopIteration:
        return bk.finalize()
    if stage in ("full", "dense", "densedbgP"):
        emit_dense()
    if stage == "full":
        emit_ffn(1)

    if stage == "densedbgP":
        return bk.finalize()
    out_is_y = (r == "partial")
    outsrc = (lambda c: ya(c)) if out_is_y else (lambda c: xa(c))
    okeys = [f"Y{c}" for c in range(8)] if out_is_y else [f"X{c}" for c in range(8)]
    for tt in range(16):
        stg = fa(4)[:, 0:1024]
        for c in range(8):
            bank = c % 2
            T("tensor", [okeys[c], "C"], [f"ps{bank}"],
              lambda eng, c=c, tt=tt, bank=bank:
                  eng.transpose(PS[bank][:, 0:128], outsrc(c)[:, 128 * tt:128 * tt + 128],
                                CO_IDENT), "oT")
            T("scalar", [f"ps{bank}", "F4"], ["F4"],
              lambda eng, c=c, bank=bank, stg=stg:
                  eng.copy(stg[:, 128 * c:128 * c + 128], PS[bank][:, 0:128]), "oTe")
        dma("sync", y_d[128 * tt:128 * tt + 128, :], stg, ["F4"], ["F4"], "sto")

    return bk.finalize()


def _in_maps(L0_attn_x, L0_attn_Wq, L0_attn_Wk, L0_attn_Wv, L0_ffn_wqkv,
             L0_ffn_q_scale, L0_ffn_k_scale, L0_ffn_wo, L0_ffn_ln_g, L0_ffn_ln_b,
             L0_ffn_w1, L0_ffn_b1, L0_ffn_w2, L0_ffn_b2):
    cvt = lambda a: np.ascontiguousarray(np.asarray(a, dtype=np.float32))
    x = cvt(L0_attn_x)
    common = {"Wq": cvt(L0_attn_Wq), "Wk": cvt(L0_attn_Wk), "Wv": cvt(L0_attn_Wv),
              "wqkv": cvt(L0_ffn_wqkv), "qs": cvt(L0_ffn_q_scale), "ks": cvt(L0_ffn_k_scale),
              "wo": cvt(L0_ffn_wo), "lng": cvt(L0_ffn_ln_g), "lnb": cvt(L0_ffn_ln_b),
              "w1": cvt(L0_ffn_w1), "b1": cvt(L0_ffn_b1), "w2": cvt(L0_ffn_w2),
              "b2": cvt(L0_ffn_b2)}
    return [dict(common, x=np.ascontiguousarray(x[b])) for b in range(NCORE)]


def kernel(**inputs):
    nc = build_program("full")
    in_maps = _in_maps(**inputs)
    res = run_bass_kernel_spmd(nc, in_maps, core_ids=list(range(NCORE)))
    out = np.stack([res.results[b]["y"] for b in range(NCORE)], axis=0)
    return out.astype(np.float32)


# version 11
# speedup vs baseline: 1.0356x; 1.0254x over previous
import numpy as np
import concourse.bass as bass
import concourse.mybir as mybir
from concourse.bass_utils import run_bass_kernel_spmd

F32 = mybir.dt.float32
F16 = mybir.dt.float16
AF = mybir.ActivationFunctionType
OP = mybir.AluOpType
AX = mybir.AxisListType

S = 2048
DM = 1024
DH = 64
INNER = 512
HID = 2730
NKI = 22
KI_LAST = 2730 - 21 * 128
NCORE = 8

PASSES1 = [(0, 0), (0, 2048), (2048, 0)]
PASSES2 = [(0, 0), (0, 512), (1024, 0)]
ENGINES = ("tensor", "vector", "scalar", "gpsimd", "sync", "sdma2", "gdma")
DMA_Q = ("sync", "sdma2", "gdma")
SPANS = [(s, s + 512) for s in range(0, S, 512)]


def make_consts():
    c = {}
    c["IDENT"] = np.eye(128, dtype=np.float32)
    prot = np.zeros((128, 128), dtype=np.float64)
    for b in (0, 64):
        for ml in range(32):
            prot[b + ml + 32, b + ml] = -1.0
        for ml in range(32, 64):
            prot[b + ml - 32, b + ml] = 1.0
    c["PROT"] = prot.astype(np.float32)
    bc0 = np.zeros((128, 128), dtype=np.float32); bc0[0, :] = 1.0
    c["BC0"] = bc0
    bc64 = np.zeros((128, 128), dtype=np.float32); bc64[64, :] = 1.0
    c["BC64"] = bc64
    bch = np.zeros((128, 128), dtype=np.float32)
    bch[0, 0:64] = 1.0; bch[1, 64:128] = 1.0
    c["BCH"] = bch
    c["SELA"] = np.ones((128, 1), dtype=np.float32)
    selh = np.zeros((128, 2), dtype=np.float32)
    selh[0:64, 0] = 1.0; selh[64:128, 1] = 1.0
    c["SELH"] = selh
    d = np.arange(64)
    i32 = d % 32
    inv_freq = 1.0 / (10000.0 ** (2.0 * i32 / 64.0))
    base = (2.0 * i32 + 0.4 * 64.0) / (1.4 * 64.0)
    t = np.arange(S, dtype=np.float64)
    freqs = t[None, :] * inv_freq[:, None]
    power = (t - S // 2) / 256.0
    scale = base[:, None] ** power[None, :]
    cosf, sinf = np.cos(freqs), np.sin(freqs)
    dup = lambda a: np.concatenate([a, a], axis=0).astype(np.float32)
    c["CQ"] = dup(cosf * scale)
    c["SQ"] = dup(sinf * scale)
    c["CK"] = dup(cosf / scale)
    c["SK"] = dup(sinf / scale)
    return c


NDMASEM = 24


class BK:

    def __init__(self, nc):
        self.nc = nc
        self.tasks = []
        self.res = {}
        self.count = {e: 0 for e in ENGINES}
        self.dma_idx = 0
        self.gdma_idx = 0
        self.duse = [0] * NDMASEM
        self.waited = {e: {} for e in ("tensor", "vector", "scalar", "gpsimd", "sync")}

    def task(self, eng, reads, writes, fn, name=""):
        stream = {"sdma2": "scalar", "gdma": "gpsimd"}.get(eng, eng)
        waits = {}
        def dep(d):
            dsem, dval = d
            waits[dsem] = max(waits.get(dsem, 0), dval)
        for r in reads:
            e = self.res.get(r)
            if e and e["w"]:
                dep(e["w"])
        for w in writes:
            e = self.res.get(w)
            if e:
                if e["w"]:
                    dep(e["w"])
                for rr in e["r"]:
                    dep(rr)
        if eng in DMA_Q:
            if eng == "gdma":
                sid = 18 + self.gdma_idx % 6
                self.gdma_idx += 1
            else:
                sid = self.dma_idx % 18
                self.dma_idx += 1
            self.duse[sid] += 1
            if self.duse[sid] > 1:
                waits[f"d{sid}"] = max(waits.get(f"d{sid}", 0), 16 * (self.duse[sid] - 1))
            me = (f"d{sid}", 16 * self.duse[sid])
        else:
            self.count[eng] += 1
            me = (eng, self.count[eng])
        if eng == "tensor":
            waits.pop("tensor", None)
        wd = self.waited[stream]
        eff = {}
        for se, val in waits.items():
            if se == stream and eng not in DMA_Q:
                pass
            if wd.get(se, -1) >= val:
                continue
            eff[se] = val
            wd[se] = val
        for r in reads:
            self.res.setdefault(r, {"w": None, "r": []})["r"].append(me)
        for w in writes:
            self.res[w] = {"w": me, "r": []}
        self.tasks.append({"eng": eng, "waits": eff, "fn": fn, "name": name,
                           "me": me})

    def finalize(self):
        nc = self.nc
        sems = {e: nc.alloc_semaphore(f"sem_{e}") for e in
                ("tensor", "vector", "scalar", "gpsimd", "sync")}
        for i in range(NDMASEM):
            sems[f"d{i}"] = nc.alloc_semaphore(f"dsem_{i}")
        with nc.Block() as block:
            def mk(ename, tasklist):
                def body(eng):
                    for t in tasklist:
                        for se, val in sorted(t["waits"].items()):
                            eng.wait_ge(sems[se], val)
                        inst = t["fn"](eng)
                        if inst is None:
                            inst = eng.nop()
                        tsem, tval = t["me"]
                        inst.then_inc(sems[tsem], 16 if t["eng"] in DMA_Q else 1)
                    if ename == "sync":
                        for e2 in ("tensor", "vector", "scalar", "gpsimd"):
                            if self.count[e2]:
                                eng.wait_ge(sems[e2], self.count[e2])
                        for i in range(NDMASEM):
                            if self.duse[i]:
                                eng.wait_ge(sems[f"d{i}"], 16 * self.duse[i])
                return body
            per = {}
            per["tensor"] = [t for t in self.tasks if t["eng"] == "tensor"]
            per["vector"] = [t for t in self.tasks if t["eng"] == "vector"]
            per["scalar"] = [t for t in self.tasks if t["eng"] in ("scalar", "sdma2")]
            per["gpsimd"] = [t for t in self.tasks if t["eng"] in ("gpsimd", "gdma")]
            per["sync"] = [t for t in self.tasks if t["eng"] == "sync"]
            block.tensor(mk("tensor", per["tensor"]))
            block.vector(mk("vector", per["vector"]))
            block.scalar(mk("scalar", per["scalar"]))
            block.gpsimd(mk("gpsimd", per["gpsimd"]))
            block.sync(mk("sync", per["sync"]))
        return nc


def build_program(stage="full"):
    nc = bass.Bass("TRN2", target_bir_lowering=False)
    C = make_consts()

    x_d = nc.dram_tensor("x", [S, DM], F32, kind="ExternalInput").ap()
    y_d = nc.dram_tensor("y", [S, DM], F32, kind="ExternalOutput").ap()
    Wq_d = nc.dram_tensor("Wq", [DM, DM], F32, kind="ExternalInput").ap()
    Wk_d = nc.dram_tensor("Wk", [DM, DM], F32, kind="ExternalInput").ap()
    Wv_d = nc.dram_tensor("Wv", [DM, DM], F32, kind="ExternalInput").ap()
    wqkv_d = nc.dram_tensor("wqkv", [2, DM, 3 * INNER], F32, kind="ExternalInput").ap()
    qs_d = nc.dram_tensor("qs", [2, DH], F32, kind="ExternalInput").ap()
    ks_d = nc.dram_tensor("ks", [2, DH], F32, kind="ExternalInput").ap()
    wo_d = nc.dram_tensor("wo", [2, INNER, DM], F32, kind="ExternalInput").ap()
    lng_d = nc.dram_tensor("lng", [2, DM], F32, kind="ExternalInput").ap()
    lnb_d = nc.dram_tensor("lnb", [2, DM], F32, kind="ExternalInput").ap()
    w1_d = nc.dram_tensor("w1", [2, DM, 2 * HID], F32, kind="ExternalInput").ap()
    b1_d = nc.dram_tensor("b1", [2, 2 * HID], F32, kind="ExternalInput").ap()
    w2_d = nc.dram_tensor("w2", [2, HID, DM], F32, kind="ExternalInput").ap()
    b2_d = nc.dram_tensor("b2", [2, DM], F32, kind="ExternalInput").ap()

    xsp = nc.dram_tensor("xsp", [8, 128, S], F32).ap()
    qsp = nc.dram_tensor("qsp", [8, 128, 2 * S], F16).ap()
    ksp = nc.dram_tensor("ksp", [8, 128, 2 * S], F16).ap()
    otsp = nc.dram_tensor("otsp", [4, 128, S], F32).ap()
    gsp = nc.dram_tensor("gsp", [NKI, 128, 2 * S], F16).ap()
    w1sp = nc.dram_tensor("w1sp", [2, NKI, 128, 4096], F16).ap()
    w2sp = nc.dram_tensor("w2sp", [2, NKI, 128, 2048], F16).ap()

    consts_d = {k: nc.inline_tensor(v, name=f"c_{k}").ap() for k, v in C.items()}

    XA = nc.alloc_sbuf_tensor("XA", [128, 8 * S], F32).ap()
    YA = nc.alloc_sbuf_tensor("YA", [128, 8 * S], F32).ap()
    FA = nc.alloc_sbuf_tensor("FA", [128, 8 * S], F32).ap()
    CO = nc.alloc_sbuf_tensor("CO", [128, 1856], F32).ap()
    GEL = nc.alloc_sbuf_tensor("GEL", [128, 1024], F32).ap()
    PS = [nc.alloc_psum_tensor(f"PS{i}", [128, 512], F32).ap() for i in range(8)]

    xa = lambda i: XA[:, 2048 * i:2048 * (i + 1)]
    ya = lambda i: YA[:, 2048 * i:2048 * (i + 1)]
    fa = lambda i: FA[:, 2048 * i:2048 * (i + 1)]
    XAr = XA.rearrange("p (c n) -> p c n", c=8)
    YAr = YA.rearrange("p (c n) -> p c n", c=8)
    FAr = FA.rearrange("p (c n) -> p c n", c=8)
    XA16 = XA.bitcast(F16)
    FA16 = FA.bitcast(F16)
    xa16 = lambda i: XA16[:, 4096 * i:4096 * (i + 1)]
    fa16 = lambda i: FA16[:, 4096 * i:4096 * (i + 1)]

    CO_IDENT = CO[:, 0:128]
    CO_PROT = CO[:, 128:256]
    CO_BC0 = CO[:, 256:384]
    CO_BC64 = CO[:, 384:512]
    CO_BCH = CO[:, 512:640]
    CO_SELA = CO[:, 640:641]
    CO_SELH = CO[:, 641:643]
    CO_EPSLN = CO[:, 643:644]
    CO_QS = [CO[:, 644 + l:645 + l] for l in range(2)]
    CO_KS = [CO[:, 646 + l:647 + l] for l in range(2)]
    CO_B2 = [CO[:, 648 + 8 * l:656 + 8 * l] for l in range(2)]
    CO_LNG = [CO[:, 664 + 8 * l:672 + 8 * l] for l in range(2)]
    CO_LNB = [CO[:, 680 + 8 * l:688 + 8 * l] for l in range(2)]
    CO_B1A = [CO[:, 696 + NKI * l:696 + NKI * (l + 1)] for l in range(2)]
    CO_B1G = [CO[:, 740 + NKI * l:740 + NKI * (l + 1)] for l in range(2)]
    CO_BRHS = CO[:, 832:1856]

    bk = BK(nc)
    T = bk.task

    def dma(qeng, dst, src, reads, writes, name="", slow=False):
        if slow:
            T(qeng, reads, writes,
              lambda eng, d=dst, s=src: eng.dma_start(out=d, in_=s,
                                                      allow_slow_non_contiguous=True), name)
        else:
            T(qeng, reads, writes, lambda eng, d=dst, s=src: eng.dma_start(out=d, in_=s), name)

    for key, dst in [("IDENT", CO_IDENT), ("PROT", CO_PROT), ("BC0", CO_BC0),
                     ("BC64", CO_BC64), ("BCH", CO_BCH), ("SELA", CO_SELA),
                     ("SELH", CO_SELH)]:
        dma("sync", dst, consts_d[key][:, :], ["C"], ["C"], f"ld_{key}")
    T("vector", ["C"], ["C"], lambda eng: eng.memset(CO_EPSLN, 1e-5), "eps")
    T("vector", ["BRHS"], ["BRHS"], lambda eng: eng.memset(CO_BRHS, 0.0), "brhs0")
    for l in range(2):
        for half in (0, 1):
            dma("sync", CO_QS[l][64 * half:64 * half + 64, :],
                qs_d[l:l + 1, :].transpose([1, 0]), ["C"], ["C"], "qs", slow=True)
            dma("sync", CO_KS[l][64 * half:64 * half + 64, :],
                ks_d[l:l + 1, :].transpose([1, 0]), ["C"], ["C"], "ks", slow=True)
        dma("sync", CO_B2[l], b2_d[l:l + 1, :].rearrange("a (c p) -> (a p) c", p=128),
            ["C"], ["C"], "b2", slow=True)
        dma("sync", CO_LNG[l], lng_d[l:l + 1, :].rearrange("a (c p) -> (a p) c", p=128),
            ["C"], ["C"], "lng", slow=True)
        dma("sync", CO_LNB[l], lnb_d[l:l + 1, :].rearrange("a (c p) -> (a p) c", p=128),
            ["C"], ["C"], "lnb", slow=True)
        for ki in range(NKI):
            n = 128 if ki < NKI - 1 else KI_LAST
            dma("sync", CO_B1A[l][0:n, ki:ki + 1],
                b1_d[l:l + 1, 128 * ki:128 * ki + n].transpose([1, 0]), ["C"], ["C"], "b1a", slow=True)
            dma("sync", CO_B1G[l][0:n, ki:ki + 1],
                b1_d[l:l + 1, HID + 128 * ki:HID + 128 * ki + n].transpose([1, 0]),
                ["C"], ["C"], "b1g", slow=True)

    for tt in range(16):
        stg = fa(tt % 2)[:, 0:1024]
        skey = f"F{tt % 2}"
        dma("sync", stg, x_d[128 * tt:128 * tt + 128, :], [skey], [skey], "ldx")
        for c in range(8):
            ps, pk = PS[c % 2], f"ps{c % 2}"
            T("tensor", [skey, "C"], [pk],
              lambda eng, ps=ps, stg=stg, c=c:
                  eng.transpose(ps[:, 0:128], stg[:, 128 * c:128 * c + 128], CO_IDENT), "xT")
            T("scalar", [pk], [f"X{c}"],
              lambda eng, ps=ps, c=c, tt=tt:
                  eng.copy(xa(c)[:, 128 * tt:128 * tt + 128], ps[:, 0:128]), "xTe")

    def emit_ln(srcA4, srcB4, src, srckeys, out, outkeys, sqA, sqA_key, sqB, sqB_key,
                st_scr, st_key, gamma=None, beta=None, name="ln", split_out=None):
        for si, (s0, s1) in enumerate(SPANS):
            b_sx, k_sx = PS[(si % 2) * 2], f"ps{(si % 2) * 2}"
            b_sq, k_sq = PS[(si % 2) * 2 + 1], f"ps{(si % 2) * 2 + 1}"
            b_r, k_r = PS[4 + (si % 2) * 2], f"ps{4 + (si % 2) * 2}"
            b_m, k_m = PS[5 + (si % 2) * 2], f"ps{5 + (si % 2) * 2}"
            for c in range(8):
                T("tensor", [srckeys[c], "C"], [k_sx],
                  lambda eng, c=c, s0=s0, s1=s1, b=b_sx:
                      eng.matmul(b[0:1, 0:512], CO_SELA, src(c)[:, s0:s1],
                                 start=(c == 0), stop=(c == 7)), name + "_sx")
            T("scalar", srckeys[0:4] + [sqA_key], [sqA_key],
              lambda eng, s0=s0:
                  eng.activation(sqA.rearrange("p (c n) -> p c n", c=4),
                                 srcA4(s0), AF.Square), name + "_sqA")
            T("vector", srckeys[4:8] + [sqB_key], [sqB_key],
              lambda eng, s0=s0:
                  eng.tensor_tensor(sqB.rearrange("p (c n) -> p c n", c=4),
                                    srcB4(s0), srcB4(s0), op=OP.mult), name + "_sqB")
            for c in range(8):
                sq = (sqA if c < 4 else sqB)[:, 512 * (c % 4):512 * (c % 4) + 512]
                T("tensor", [(sqA_key if c < 4 else sqB_key), "C"], [k_sq],
                  lambda eng, c=c, b=b_sq, sq=sq:
                      eng.matmul(b[0:1, 0:512], CO_SELA, sq, start=(c == 0), stop=(c == 7)),
                  name + "_sq")
            st2 = st_scr
            T("scalar", [k_sx], [st_key],
              lambda eng, b=b_sx: eng.copy(st2[0:1, 0:512], b[0:1, 0:512]), name + "_ev1")
            T("scalar", [k_sq, st_key], [st_key],
              lambda eng, b=b_sq: eng.copy(st2[0:1, 512:1024], b[0:1, 0:512]), name + "_ev2")
            T("vector", [st_key], [st_key],
              lambda eng: eng.tensor_scalar(st2[0:1, 0:1024], st2[0:1, 0:1024],
                                            1.0 / DM, None, op0=OP.mult), name + "_sc")
            T("vector", [st_key], [st_key],
              lambda eng: eng.tensor_tensor(st2[0:1, 1024:1536], st2[0:1, 0:512],
                                            st2[0:1, 0:512], op=OP.mult), name + "_m2")
            T("vector", [st_key], [st_key],
              lambda eng: eng.tensor_tensor(st2[0:1, 1024:1536], st2[0:1, 512:1024],
                                            st2[0:1, 1024:1536], op=OP.subtract), name + "_var")
            T("scalar", [st_key, "C"], [st_key],
              lambda eng: eng.activation(st2[0:1, 1536:2048], st2[0:1, 1024:1536],
                                         AF.Ln, bias=CO_EPSLN[0:1, :]), name + "_ln")
            T("scalar", [st_key], ["BRHS"],
              lambda eng: eng.activation(CO_BRHS[0:1, 0:512], st2[0:1, 1536:2048],
                                         AF.Exp, scale=-0.5), name + "_exp")
            T("vector", [st_key, "BRHS"], ["BRHS"],
              lambda eng: eng.tensor_tensor(CO_BRHS[0:1, 512:1024], st2[0:1, 0:512],
                                            CO_BRHS[0:1, 0:512], op=OP.mult), name + "_mr")
            T("tensor", ["BRHS", "C"], [k_r],
              lambda eng, b=b_r: eng.matmul(b[:, 0:512], CO_BC0, CO_BRHS[:, 0:512],
                                            start=True, stop=True), name + "_bcr")
            T("tensor", ["BRHS", "C"], [k_m],
              lambda eng, b=b_m: eng.matmul(b[:, 0:512], CO_BC0, CO_BRHS[:, 512:1024],
                                            start=True, stop=True), name + "_bcm")
            for c in range(8):
                okc = outkeys[c] if isinstance(outkeys[c], list) else [outkeys[c]]
                if split_out is None:
                    dst = lambda c=c, s0=s0, s1=s1: out(c)[:, s0:s1]
                    T("vector", [srckeys[c], k_r], okc,
                      lambda eng, c=c, s0=s0, s1=s1, b=b_r, dst=dst:
                          eng.tensor_tensor(dst(), src(c)[:, s0:s1], b[:, 0:512],
                                            op=OP.mult), name + "_ap1")
                    T("vector", okc + [k_m], okc,
                      lambda eng, b=b_m, dst=dst:
                          eng.tensor_tensor(dst(), dst(), b[:, 0:512],
                                            op=OP.subtract), name + "_ap2")
                else:
                    hi_fn, lo_fn = split_out
                    tmp = GEL[:, 512 * (c % 2):512 * (c % 2) + 512]
                    tk = f"GE{c % 2}"
                    T("vector", [srckeys[c], k_r, tk], [tk],
                      lambda eng, c=c, s0=s0, s1=s1, b=b_r, tmp=tmp:
                          eng.tensor_tensor(tmp, src(c)[:, s0:s1], b[:, 0:512],
                                            op=OP.mult), name + "_ap1")
                    T("vector", [tk, k_m], [tk],
                      lambda eng, b=b_m, tmp=tmp:
                          eng.tensor_tensor(tmp, tmp, b[:, 0:512],
                                            op=OP.subtract), name + "_ap2")
                    T("vector", [tk, "C"], [tk],
                      lambda eng, c=c, tmp=tmp:
                          eng.tensor_scalar(tmp, tmp, gamma[:, c:c + 1], beta[:, c:c + 1],
                                            op0=OP.mult, op1=OP.add), name + "_gb")
                    T("scalar", [tk] + okc, okc,
                      lambda eng, c=c, s0=s0, tmp=tmp, hi_fn=hi_fn:
                          eng.copy(hi_fn(c)[:, s0:s0 + 512], tmp), name + "_hi")
                    T("vector", [tk] + okc, okc,
                      lambda eng, c=c, s0=s0, tmp=tmp, hi_fn=hi_fn, lo_fn=lo_fn:
                          eng.tensor_tensor(lo_fn(c)[:, s0:s0 + 512], tmp,
                                            hi_fn(c)[:, s0:s0 + 512], op=OP.subtract),
                      name + "_lo")

    def ldwqkv(l, p):
        wsl = []
        for c in range(8):
            if c < 5:
                wsl.append((xa(3)[:, 384 * c:384 * (c + 1)], "X3"))
            else:
                wsl.append((xa(4)[:, 384 * (c - 5):384 * (c - 4)], "X4"))
        for c in range(8):
            src = bass.AP(wqkv_d.tensor,
                          offset=l * DM * 3 * INNER + (128 * c) * (3 * INNER) + 128 * p,
                          ap=[[3 * INNER, 128], [INNER, 3], [1, 128]])
            dma("sync", wsl[c][0].rearrange("p (b n) -> p b n", b=3), src,
                [wsl[c][1]], [wsl[c][1]], "ldwqkv")
        return wsl

    def emit_pair(inv, l, p, wsl):
        lt = f"i{inv}l{l}p{p}"
        PQ, PK, PV = xa(0), xa(1), xa(2)
        kPQ, kPK, kPV = "X0", "X1", "X2"
        def qkv_rhs(c):
            return ya(c) if c < 4 else fa(c - 4)
        qkv_rkey = [f"Y{c}" if c < 4 else f"F{c - 4}" for c in range(8)]
        for si, (s0, s1) in enumerate(SPANS):
            for c in range(8):
                w, wkey = wsl[c]
                for t_i, bank in enumerate([0, 1, 2]):
                    T("tensor", [wkey, qkv_rkey[c]], [f"ps{bank}"],
                      lambda eng, c=c, t_i=t_i, bank=bank, s0=s0, s1=s1, w=w:
                          eng.matmul(PS[bank][:, 0:512], w[:, 128 * t_i:128 * t_i + 128],
                                     qkv_rhs(c)[:, s0:s1], start=(c == 0), stop=(c == 7)),
                      "qkv_mm")
            for bank, (dst, key) in enumerate([(PQ, kPQ), (PK, kPK), (PV, kPV)]):
                wk = [key] if bank == 2 else [key, f"{key}s{si}"]
                T("scalar", [f"ps{bank}"], wk,
                  lambda eng, bank=bank, dst=dst, s0=s0:
                      eng.copy(dst[:, s0:s0 + 512], PS[bank][:, 0:512]), "qkv_ev")
        dma("sync", fa(7), consts_d["SK"][:, :], ["F7"], ["F7"], "ldSK")
        dma("sync", xa(3), consts_d["CQ"][:, :], ["X3"], ["X3"], "ldCQ")
        dma("sync", xa(4), consts_d["SQ"][:, :], ["X4"], ["X4"], "ldSQ")
        chains = []
        for TP, tb, ck, sk_, scl in [(PQ, kPQ, "X3", "X4", CO_QS[l]),
                                     (PK, kPK, "X3", "F7", CO_KS[l])]:
            for si, (s0, s1) in enumerate(SPANS):
                chains.append((TP, tb, ck, sk_, scl, si, s0))

        def chain_head(idx):
            TP, tb, ck, sk_, scl, si, s0 = chains[idx]
            par = idx % 2
            bss, kss = (PS[3], "ps3") if par == 0 else (PS[0], "ps0")
            off = 512 * par
            sq, sqk = (fa(4)[:, 0:512], "F4") if par == 0 else (GEL[:, 0:512], "GE0")
            skey = f"{tb}s{si}"
            T("scalar", [skey, sqk], [sqk],
              lambda eng, TP=TP, s0=s0, sq=sq:
                  eng.activation(sq, TP[:, s0:s0 + 512], AF.Square), "l2_sq")
            T("tensor", [sqk, "C"], [kss],
              lambda eng, sq=sq, bss=bss:
                  eng.matmul(bss[0:2, 0:512], CO_SELH, sq, start=True, stop=True), "l2_mm")
            T("vector", [kss], [kss],
              lambda eng, bss=bss: eng.tensor_scalar(bss[0:2, 0:512], bss[0:2, 0:512],
                                                     1e-18, None, op0=OP.max), "l2_clip")
            T("scalar", [kss], [kss],
              lambda eng, bss=bss: eng.activation(bss[0:2, 0:512], bss[0:2, 0:512],
                                                  AF.Ln), "l2_ln")
            T("scalar", [kss, "BRHS"], ["BRHS"],
              lambda eng, bss=bss, off=off:
                  eng.activation(CO_BRHS[0:2, off:off + 512], bss[0:2, 0:512],
                                 AF.Exp, scale=-0.5), "l2_exp")

        def chain_tail(idx):
            TP, tb, ck, sk_, scl, si, s0 = chains[idx]
            par = idx % 2
            brq, krq = (PS[4], "ps4") if par == 0 else (PS[1], "ps1")
            brot, krot = (PS[5], "ps5") if par == 0 else (PS[2], "ps2")
            off = 512 * par
            t1, t1k = (fa(4)[:, 512:1024], "F4") if par == 0 else (GEL[:, 512:1024], "GE1")
            t2, t2k = (fa(5)[:, 0:512], "F5") if par == 0 else (fa(5)[:, 512:1024], "F5")
            skey = f"{tb}s{si}"
            cconst = xa(3) if ck == "X3" else None
            sconst = xa(4) if sk_ == "X4" else fa(7)
            T("tensor", ["BRHS", "C"], [krq],
              lambda eng, brq=brq, off=off:
                  eng.matmul(brq[:, 0:512], CO_BCH, CO_BRHS[:, off:off + 512],
                             start=True, stop=True), "l2_bc")
            T("vector", [skey, krq, "C"], [skey, tb],
              lambda eng, TP=TP, s0=s0, scl=scl, brq=brq:
                  eng.scalar_tensor_tensor(TP[:, s0:s0 + 512], TP[:, s0:s0 + 512],
                                           scl, brq[:, 0:512], op0=OP.mult, op1=OP.mult),
              "l2_ap")
            T("tensor", [skey, "C"], [krot],
              lambda eng, TP=TP, s0=s0, brot=brot:
                  eng.matmul(brot[:, 0:512], CO_PROT, TP[:, s0:s0 + 512],
                             start=True, stop=True), "rot_mm")
            T("vector", [skey, ck, t1k], [t1k],
              lambda eng, TP=TP, s0=s0, t1=t1:
                  eng.tensor_tensor(t1, TP[:, s0:s0 + 512],
                                    xa(3)[:, s0:s0 + 512], op=OP.mult), "rot_t1")
            T("vector", [krot, sk_, t2k], [t2k],
              lambda eng, s0=s0, t2=t2, brot=brot, sconst=sconst:
                  eng.tensor_tensor(t2, brot[:, 0:512],
                                    sconst[:, s0:s0 + 512], op=OP.mult), "rot_t2")
            T("vector", [t1k, t2k], [skey, tb],
              lambda eng, TP=TP, s0=s0, t1=t1, t2=t2:
                  eng.tensor_tensor(TP[:, s0:s0 + 512], t1, t2, op=OP.add), "rot_add")

        chain_head(0)
        for idx in range(8):
            if idx + 1 < 8:
                chain_head(idx + 1)
            chain_tail(idx)
            if idx == 3:
                dma("sync", xa(3), consts_d["CK"][:, :], ["X3"], ["X3"], "ldCK")
        nwsl = ldwqkv(l, p + 1) if p + 1 < 4 else None
        VT0 = 5 * 2048
        T("vector", ["X5"], ["X5"],
          lambda eng: eng.memset(XA[:, VT0 + 64:VT0 + 16 * 132:132], 1.0), "vt1")
        T("vector", ["X5"], ["X5"],
          lambda eng: eng.memset(XA[:, VT0 + 65:VT0 + 16 * 132:132], 1.0), "vt2")
        T("vector", ["X5", "X6"], ["X5", "X6"],
          lambda eng: eng.memset(XA[:, VT0 + 130:VT0 + 16 * 132:132], 1.0), "vt3")
        T("vector", ["X5", "X6"], ["X5", "X6"],
          lambda eng: eng.memset(XA[:, VT0 + 131:VT0 + 16 * 132:132], 1.0), "vt4")
        for tt in range(16):
            bank = 6 + (tt % 2)
            T("tensor", [kPV, "C"], [f"ps{bank}"],
              lambda eng, tt=tt, bank=bank:
                  eng.transpose(PS[bank][:, 0:128], PV[:, 128 * tt:128 * tt + 128], CO_IDENT),
              "vt_T")
            T("scalar", [f"ps{bank}"], ["X5", "X6"],
              lambda eng, tt=tt, bank=bank:
                  eng.copy(XA[:, VT0 + 132 * tt:VT0 + 132 * tt + 64], PS[bank][:, 0:64]),
              "vt_evA")
            T("scalar", [f"ps{bank}"], ["X5", "X6"],
              lambda eng, tt=tt, bank=bank:
                  eng.copy(XA[:, VT0 + 132 * tt + 66:VT0 + 132 * tt + 130],
                           PS[bank][:, 64:128]), "vt_evB")
        for w in range(4):
            emit_window(l, p, w, PQ, PK, kPQ, kPK, VT0)
        dma("sdma2", otsp[p, 0:64, :], xa(7)[0:64, :], ["X7"], [f"ot{p}"], "ot_spA")
        return nwsl

    def emit_window(l, p, w, PQ, PK, kPQ, kPK, VT0):
        s0 = 512 * w
        OTA = xa(7)
        first_av = [True, True]

        def geom(m):
            ca = 4 * (w - 1) + m if m < 4 else 4 * w + (m - 4)
            q_lo = 0 if m < 4 else 128 * (m - 4)
            q_hi = 128 * (m + 1) if m < 4 else 512
            return ca, q_lo, q_hi - q_lo

        def emit_st(m, i):
            ca, q_lo, n = geom(m)
            for h, (bank, base) in enumerate([((i % 2) * 2, 0), ((i % 2) * 2 + 1, 64)]):
                T("tensor", [f"{kPQ}s{w}", f"{kPK}s{ca // 4}"], [f"ps{bank}"],
                  lambda eng, bank=bank, base=base, ca=ca, q_lo=q_lo, n=n:
                      eng.matmul(PS[bank][:, q_lo:q_lo + n],
                                 PK[base:base + 64, 128 * ca:128 * ca + 128],
                                 PQ[base:base + 64, s0 + q_lo:s0 + q_lo + n],
                                 start=True, stop=True), "st_mm")

        def emit_expmask_av(m, i):
            ca, q_lo, n = geom(m)
            for h, bank in enumerate([(i % 2) * 2, (i % 2) * 2 + 1]):
                qk = f"F6q{2 * (i % 2) + h}"
                off = 1024 * (i % 2) + 512 * h
                T("scalar", [f"ps{bank}", qk], [qk],
                  lambda eng, bank=bank, off=off, q_lo=q_lo, n=n:
                      eng.activation(fa(6)[:, off + q_lo:off + q_lo + n],
                                     PS[bank][:, q_lo:q_lo + n], AF.Exp, scale=8.0), "st_exp")
                if m < 4:
                    T("gpsimd", [qk], [qk],
                      lambda eng, off=off, m=m, n=n:
                          eng.affine_select(fa(6)[:, off:off + n], fa(6)[:, off:off + n],
                                            pattern=[[-1, n]], compare_op=OP.is_ge,
                                            fill=0.0, base=128 * m, channel_multiplier=1),
                      "mask_lb")
                else:
                    T("gpsimd", [qk], [qk],
                      lambda eng, off=off, q_lo=q_lo, n=n:
                          eng.affine_select(fa(6)[:, off + q_lo:off + q_lo + n],
                                            fa(6)[:, off + q_lo:off + q_lo + n],
                                            pattern=[[1, n]], compare_op=OP.is_ge,
                                            fill=0.0, base=0, channel_multiplier=-1),
                      "mask_iw")
            for h, bank in enumerate([4, 5]):
                qk = f"F6q{2 * (i % 2) + h}"
                off = 1024 * (i % 2) + 512 * h
                vt = XA[:, VT0 + 132 * ca + 66 * h:VT0 + 132 * ca + 66 * h + 66]
                st = first_av[h]
                T("tensor", [qk, "X5", "X6"], [f"ps{bank}"],
                  lambda eng, bank=bank, vt=vt, off=off, q_lo=q_lo, n=n, st=st:
                      eng.matmul(PS[bank][0:66, q_lo:q_lo + n], vt,
                                 fa(6)[:, off + q_lo:off + q_lo + n],
                                 start=st, stop=False, skip_group_check=True), "av_mm")
                first_av[h] = False

        ms = [m for m in [4, 0, 1, 2, 3, 5, 6, 7] if not (w == 0 and m < 4)]
        emit_st(ms[0], 0)
        for i, m in enumerate(ms):
            if i + 1 < len(ms):
                emit_st(ms[i + 1], i + 1)
            emit_expmask_av(m, i)
        RSS = CO[:, 832:1344]
        for h, bank in enumerate([4, 5]):
            T("scalar", [f"ps{bank}", "BRHS"], ["BRHS"],
              lambda eng, bank=bank: eng.copy(RSS[64:65, 0:512], PS[bank][64:65, 0:512]),
              "av_sums")
            T("tensor", ["BRHS", "C"], [f"ps{6 + h}"],
              lambda eng, h=h: eng.matmul(PS[6 + h][0:64, 0:512], CO_BC64[:, 0:64],
                                          RSS[:, 0:512], start=True, stop=True), "av_bc")
            T("scalar", [f"ps{6 + h}"], [f"ps{6 + h}"],
              lambda eng, h=h: eng.activation(PS[6 + h][0:64, 0:512], PS[6 + h][0:64, 0:512],
                                              AF.Ln), "av_lnr")
            T("scalar", [f"ps{6 + h}", "F4"], ["F4"],
              lambda eng, h=h: eng.activation(fa(4)[0:64, 512 * h:512 * h + 512],
                                              PS[6 + h][0:64, 0:512], AF.Exp, scale=-1.0),
              "av_rec")
        T("vector", ["ps4", "F4", "X7"], ["X7"],
          lambda eng: eng.tensor_tensor(OTA[0:64, s0:s0 + 512], PS[4][0:64, 0:512],
                                        fa(4)[0:64, 0:512], op=OP.mult), "av_evA")
        T("vector", ["ps5", "F4", "F5"], ["F5"],
          lambda eng: eng.tensor_tensor(fa(5)[0:64, 0:512], PS[5][0:64, 0:512],
                                        fa(4)[0:64, 512:1024], op=OP.mult), "av_evB")
        dma("sdma2", otsp[p, 64:128, s0:s0 + 512], fa(5)[0:64, 0:512], ["F5"], [f"ot{p}"],
            "ot_spB")

    def emit_ffn_layer(inv, l):
        for j in range(4):
            c = 4 + j
            T("vector", [f"F{j}"], [f"F{j}"],
              lambda eng, j=j: eng.memset(fa(j)[:, 0:1], 0.0), "shz")
            T("vector", [f"Y{c}", f"F{j}"], [f"F{j}"],
              lambda eng, j=j, c=c: eng.tensor_copy(fa(j)[:, 1:2048], ya(c)[:, 0:2047]), "shc")
        wsl = ldwqkv(l, 0)
        for p in range(4):
            wsl = emit_pair(inv, l, p, wsl)
        for j in range(4):
            dst = fa(j // 2)[:, 1024 * (j % 2):1024 * (j % 2) + 1024]
            dma("sync", dst, wo_d[l, 128 * j:128 * j + 128, :],
                [f"F{j // 2}"], [f"F{j // 2}"], "ldwo")
        for si, (s0, s1) in enumerate(SPANS):
            dma("sync", fa(6)[:, 0:512], otsp[0, :, s0:s1], ["F6q0", "ot0"], ["F6q0"], "ldot")
            for j in range(4):
                ots = fa(6)[:, 512 * j:512 * j + 512]
                if j + 1 < 4:
                    dma("sync", fa(6)[:, 512 * (j + 1):512 * (j + 1) + 512],
                        otsp[j + 1, :, s0:s1], [f"F6q{j + 1}", f"ot{j + 1}"],
                        [f"F6q{j + 1}"], "ldot")
                for fo in range(8):
                    wob = fa(j // 2)[:, 1024 * (j % 2) + 128 * fo:1024 * (j % 2) + 128 * fo + 128]
                    T("tensor", [f"F6q{j}", f"F{j // 2}"], [f"ps{fo}"],
                      lambda eng, fo=fo, j=j, wob=wob, ots=ots:
                          eng.matmul(PS[fo][:, 0:512], wob, ots, start=(j == 0), stop=(j == 3)),
                      "wo_mm")
            for fo in range(8):
                T("vector", [f"ps{fo}", f"Y{fo}"], [f"Y{fo}"],
                  lambda eng, fo=fo, s0=s0:
                      eng.tensor_tensor(ya(fo)[:, s0:s0 + 512], ya(fo)[:, s0:s0 + 512],
                                        PS[fo][:, 0:512], op=OP.add), "wo_resid")
        if stage == f"attn_i{inv}l{l}":
            return False
        emit_mlp(inv, l)
        return True

    def emit_mlp(inv, l):
        for j in range(4):
            c = 4 + j
            T("vector", [f"F{j}"], [f"F{j}"],
              lambda eng, j=j: eng.memset(fa(j)[:, 0:1], 0.0), "shz2")
            T("vector", [f"Y{c}", f"F{j}"], [f"F{j}"],
              lambda eng, j=j, c=c: eng.tensor_copy(fa(j)[:, 1:2048], ya(c)[:, 0:2047]), "shc2")
        hsrc = lambda c: (ya(c) if c < 4 else fa(c - 4))
        hkeys = [f"Y{c}" if c < 4 else f"F{c - 4}" for c in range(8)]
        h2h = lambda c: xa16(c)[:, 0:2048]
        h2l = lambda c: xa16(c)[:, 2048:4096]
        emit_ln(lambda s0: YAr[:, 0:4, s0:s0 + 512], lambda s0: FAr[:, 0:4, s0:s0 + 512],
                hsrc, hkeys, None,
                [[f"X{c}"] + [f"X{c}s{si}" for si in range(4)] if c < 2 else f"X{c}"
                 for c in range(8)],
                fa(4), "F4", fa(7), "F7", fa(5), "F5",
                gamma=CO_LNG[l], beta=CO_LNB[l], name=f"lnm{inv}{l}",
                split_out=(h2h, h2l))
        if stage == "mlpdbgH" and inv == 0 and l == 0:
            for c in range(8):
                dma("sync", y_d[128 * c:128 * c + 128, :], xa(c)[:, 0:1024],
                    [f"X{c}"], ["dbg"], "dbgH")
                dma("sync", y_d[1024 + 128 * c:1024 + 128 * c + 128, :], xa(c)[:, 1024:2048],
                    [f"X{c}"], ["dbg"], "dbgH2")
            raise StopIteration
        def ldw1(ki):
            n = 128 if ki < NKI - 1 else KI_LAST
            if inv == 0:
                stg = fa(6 + ki % 2)
                skeys = (["F6q0", "F6q1", "F6q2", "F6q3"] if ki % 2 == 0 else ["F7"])
                src = bass.AP(w1_d.tensor,
                              offset=l * DM * 2 * HID + 128 * ki,
                              ap=[[2 * HID, 128], [HID, 2], [2 * HID * 128, 8], [1, n]])
                dma("sync", stg.rearrange("p (b c n) -> p b c n", b=2, c=8)[:, :, :, 0:n],
                    src, skeys, skeys, "ldw1")
            else:
                w16 = fa16(ki % 2)
                wkey = f"F{ki % 2}"
                dma("sync", w16[:, :], w1sp[l, ki, :, :], [wkey, f"w1s{l}_{ki}"],
                    [wkey], "ldw1s")

        def split_w1(ki):
            if inv != 0:
                return
            stg = fa(6 + ki % 2)
            skeys = (["F6q0", "F6q1", "F6q2", "F6q3"] if ki % 2 == 0 else ["F7"])
            skey = skeys[0]
            w16 = fa16(ki % 2)
            wkey = f"F{ki % 2}"
            T("scalar", skeys + [wkey], [wkey],
              lambda eng, stg=stg, w16=w16: eng.copy(w16[:, 0:2048], stg[:, 0:2048]), "w1hi")
            T("vector", skeys + [wkey], [wkey],
              lambda eng, stg=stg, w16=w16:
                  eng.tensor_tensor(w16[:, 2048:4096], stg[:, 0:2048], w16[:, 0:2048],
                                    op=OP.subtract), "w1lo")
            dma("sdma2", w1sp[l, ki, :, :], w16[:, :], [wkey], [f"w1s{l}_{ki}"], "w1spl")

        ldw1(0)
        split_w1(0)
        if NKI > 1:
            ldw1(1)
        for ki in range(NKI):
            n = 128 if ki < NKI - 1 else KI_LAST
            w16 = fa16(ki % 2)
            wkey = f"F{ki % 2}"
            for si, (s0, s1) in enumerate(SPANS):
                if si == 1:
                    if ki + 1 < NKI:
                        split_w1(ki + 1)
                    if inv == 0 and ki + 2 < NKI:
                        ldw1(ki + 2)
                bA, kA = PS[(si % 2) * 2], f"ps{(si % 2) * 2}"
                bG, kG = PS[(si % 2) * 2 + 1], f"ps{(si % 2) * 2 + 1}"
                for half, (bank, kb) in [(0, (bA, kA)), (1, (bG, kG))]:
                    for pi, (w_off, x_off) in enumerate(PASSES1):
                        for c in range(8):
                            T("tensor", [wkey, f"X{c}"], [kb],
                              lambda eng, c=c, s0=s0, n=n, bank=bank, w16=w16,
                                     half=half, w_off=w_off, x_off=x_off, pi=pi:
                                  eng.matmul(bank[0:n, 0:512],
                                             w16[:, w_off + 1024 * half + 128 * c:
                                                 w_off + 1024 * half + 128 * c + n],
                                             xa16(c)[:, x_off + 512 * (s0 // 512):
                                                     x_off + 512 * (s0 // 512) + 512],
                                             start=(pi == 0 and c == 0),
                                             stop=(pi == len(PASSES1) - 1 and c == 7)), "u16")
                gsc, gk = GEL[:, 512 * (si % 2):512 * (si % 2) + 512], f"GE{si % 2}"
                T("scalar", [kG, "C", gk], [gk],
                  lambda eng, ki=ki, n=n, bG=bG, gsc=gsc:
                      eng.activation(gsc[0:n, 0:512], bG[0:n, 0:512], AF.Gelu,
                                     bias=CO_B1G[l][0:n, ki:ki + 1]), "gelu")
                T("vector", [kA, gk, "C", "F5"], ["F5"],
                  lambda eng, ki=ki, si=si, n=n, bA=bA, gsc=gsc:
                      eng.scalar_tensor_tensor(fa(5)[0:n, 512 * si:512 * si + 512],
                                               bA[0:n, 0:512], CO_B1A[l][0:n, ki:ki + 1],
                                               gsc[0:n, 0:512], op0=OP.add, op1=OP.mult),
                  "glu")
            if inv != 0 and ki + 2 < NKI:
                ldw1(ki + 2)
            g16 = fa16(2 + ki % 2)
            gkey = f"F{2 + ki % 2}"
            T("scalar", ["F5", gkey], [gkey],
              lambda eng, g16=g16, n=n: eng.copy(g16[0:n, 0:2048], fa(5)[0:n, :]), "ghi")
            T("vector", ["F5", gkey], [gkey],
              lambda eng, g16=g16, n=n:
                  eng.tensor_tensor(g16[0:n, 2048:4096], fa(5)[0:n, :], g16[0:n, 0:2048],
                                    op=OP.subtract), "glo")
            dma("sdma2", gsp[ki, 0:n, :], g16[0:n, :], [gkey], [f"g{ki}"], "g_sp")
        W2R = [(fa16(0)[:, 0:2048], "F0"), (fa16(1)[:, 0:2048], "F1"),
               (fa16(3)[:, 0:2048], "F3")]
        RSTG = [(fa(4)[:, 0:1024], "F4"), (fa(5)[:, 0:1024], "F5")]

        def p2_loads(ki, si, s0):
            n = 128 if ki < NKI - 1 else KI_LAST
            w16, wk = W2R[ki % 3]
            g16 = fa16(2)[:, 1024 * (ki % 3):1024 * (ki % 3) + 1024]
            if inv == 0 and si == 0:
                wstg, sk = RSTG[ki % 2]
                dma("sync", wstg[0:n, :], w2_d[l, 128 * ki:128 * ki + n, :],
                    [sk], [sk], "ldw2")
                T("scalar", [sk, wk], [wk],
                  lambda eng, wstg=wstg, w16=w16: eng.copy(w16[:, 0:1024], wstg[:, 0:1024]),
                  "w2hi")
                T("vector", [sk, wk], [wk],
                  lambda eng, wstg=wstg, w16=w16:
                      eng.tensor_tensor(w16[:, 1024:2048], wstg[:, 0:1024], w16[:, 0:1024],
                                        op=OP.subtract), "w2lo")
                dma("sdma2", w2sp[l, ki, :, :], w16[:, :], [wk], [f"w2s{l}_{ki}"], "w2spl")
            else:
                dma("sync", w16[:, :], w2sp[l, ki, :, :], [wk, f"w2s{l}_{ki}"], [wk],
                    "ldw2s")
            src_g = bass.AP(gsp.tensor,
                            offset=ki * 128 * 4096 + s0,
                            ap=[[4096, n], [2048, 2], [1, 512]])
            dma("gdma", g16.rearrange("p (h x) -> p h x", h=2)[0:n, :, :], src_g,
                [f"g{ki}", "F2"], ["F2"], "ldg")

        for si, (s0, s1) in enumerate(SPANS):
            p2_loads(0, si, s0)
            if NKI > 1:
                p2_loads(1, si, s0)
            for ki in range(NKI):
                n = 128 if ki < NKI - 1 else KI_LAST
                w16, wk = W2R[ki % 3]
                g16 = fa16(2)[:, 1024 * (ki % 3):1024 * (ki % 3) + 1024]
                if ki + 2 < NKI:
                    p2_loads(ki + 2, si, s0)
                for pi, (w_off, g_off) in enumerate(PASSES2):
                    for fo in range(8):
                        T("tensor", [wk, "F2"], [f"ps{fo}"],
                          lambda eng, fo=fo, ki=ki, n=n, w16=w16, g16=g16,
                                 w_off=w_off, g_off=g_off, pi=pi:
                              eng.matmul(PS[fo][:, 0:512],
                                         w16[0:n, w_off + 128 * fo:w_off + 128 * fo + 128],
                                         g16[0:n, g_off:g_off + 512],
                                         start=(ki == 0 and pi == 0),
                                         stop=(ki == NKI - 1 and pi == len(PASSES2) - 1)), "w216")
            for fo in range(8):
                T("vector", [f"ps{fo}", f"Y{fo}", "C"], [f"Y{fo}"],
                  lambda eng, fo=fo, s0=s0:
                      eng.scalar_tensor_tensor(ya(fo)[:, s0:s0 + 512], PS[fo][:, 0:512],
                                               CO_B2[l][:, fo:fo + 1], ya(fo)[:, s0:s0 + 512],
                                               op0=OP.add, op1=OP.add), "w2_resid")

    def emit_ffn(inv):
        emit_ln(lambda s0: XAr[:, 0:4, s0:s0 + 512], lambda s0: XAr[:, 4:8, s0:s0 + 512],
                lambda c: xa(c), [f"X{c}" for c in range(8)],
                lambda c: ya(c), [f"Y{c}" for c in range(8)],
                fa(4), "F4", fa(7), "F7", fa(5), "F5", name=f"ln0_{inv}")
        if stage == "ln0":
            return "partial"
        for c in range(8):
            dma("sdma2", xsp[c, :, :], xa(c), [f"X{c}"], [f"xsp{c}"], "xspill")
        for l in range(2):
            emit_ffn_layer(inv, l)
            if stage in (f"attn_i{inv}l{l}", f"ffn_i{inv}l{l}"):
                return "partial"
        for c in range(8):
            dma("sync", xa(c), xsp[c, :, :], [f"xsp{c}", f"X{c}"], [f"X{c}"], "xrest")
            T("vector", [f"X{c}", f"Y{c}"], [f"X{c}"],
              lambda eng, c=c: eng.scalar_tensor_tensor(xa(c), ya(c), 0.5, xa(c),
                                                        op0=OP.mult, op1=OP.add), "resid_ffn")
        return "complete"

    def emit_dense():
        emit_ln(lambda s0: XAr[:, 0:4, s0:s0 + 512], lambda s0: XAr[:, 4:8, s0:s0 + 512],
                lambda c: xa(c), [f"X{c}" for c in range(8)],
                lambda c: fa(c), [f"F{c}" for c in range(8)],
                ya(0), "Y0", ya(2), "Y2", ya(1), "Y1", name="lnd")
        for c in range(8):
            dma("sdma2", xsp[c, :, :], xa(c), [f"X{c}"], [f"xsp{c}"], "xspd")

        def proj2(wd, store, name):
            for c in range(8):
                wb = XA[:, 2048 * (c // 2) + 1024 * (c % 2):
                        2048 * (c // 2) + 1024 * (c % 2) + 1024]
                dma("sync", wb, wd[128 * c:128 * c + 128, :],
                    [f"X{c // 2}"], [f"X{c // 2}"], "ldW_" + name)
            for fo in range(8):
                for si, (s0, s1) in enumerate(SPANS):
                    bank = (fo * 4 + si) % 2
                    for c in range(8):
                        wb = XA[:, 2048 * (c // 2) + 1024 * (c % 2):
                                2048 * (c // 2) + 1024 * (c % 2) + 1024]
                        T("tensor", [f"X{c // 2}", f"F{c}"], [f"ps{bank}"],
                          lambda eng, c=c, fo=fo, s0=s0, bank=bank, wb=wb:
                              eng.matmul(PS[bank][:, 0:512], wb[:, 128 * fo:128 * fo + 128],
                                         fa(c)[:, s0:s0 + 512], start=(c == 0), stop=(c == 7)),
                          "dmm_" + name)
                    store(fo, si, bank)

        def store_v(fo, si, bank):
            s0 = 512 * si
            T("scalar", [f"ps{bank}", "X4"], ["X4"],
              lambda eng, bank=bank, s0=s0: eng.copy(xa(4)[:, s0:s0 + 512], PS[bank][:, 0:512]),
              "dv_ev")
            if si == 3:
                for tt in range(16):
                    b2_ = 4 + (tt % 2)
                    T("tensor", ["X4", "C"], [f"ps{b2_}"],
                      lambda eng, tt=tt, b2_=b2_:
                          eng.transpose(PS[b2_][:, 0:128], xa(4)[:, 128 * tt:128 * tt + 128],
                                        CO_IDENT), "dv_T")
                    T("scalar", [f"ps{b2_}"], [f"YV{tt}"],
                      lambda eng, tt=tt, fo=fo, b2_=b2_:
                          eng.copy(YA[:, 1024 * tt + 128 * fo:1024 * tt + 128 * fo + 128],
                                   PS[b2_][:, 0:128]), "dv_Te")

        def store_spill(dst, tag):
            def f(fo, si, bank):
                s0 = 512 * si
                T("scalar", [f"ps{bank}", "X4"], ["X4"],
                  lambda eng, bank=bank, s0=s0:
                      eng.copy(xa16(4)[:, s0:s0 + 512], PS[bank][:, 0:512]), "dq_evh")
                T("vector", [f"ps{bank}", "X4"], ["X4"],
                  lambda eng, bank=bank, s0=s0:
                      eng.tensor_tensor(xa16(4)[:, 2048 + s0:2048 + s0 + 512],
                                        PS[bank][:, 0:512], xa16(4)[:, s0:s0 + 512],
                                        op=OP.subtract), "dq_evl")
                if si == 3:
                    dma("sdma2", dst[fo, :, :], xa16(4)[:, 0:4096], ["X4"], [f"{tag}{fo}"],
                        "dq_sp")
            return f

        proj2(Wv_d, store_v, "v")
        proj2(Wq_d, store_spill(qsp, "qq"), "q")
        proj2(Wk_d, store_spill(ksp, "kk"), "k")
        for c in range(8):
            dma("sync", xa(c), xsp[c, :, :], [f"xsp{c}", f"X{c}"], [f"X{c}"], "xrest_d")

        for g in range(8):
            q0 = 256 * g
            qg = fa16(0)[:, 0:4096].rearrange("p (c n) -> p c n", c=8)
            src_h = bass.AP(qsp.tensor, offset=q0, ap=[[2 * S, 128], [128 * 2 * S, 8], [1, 256]])
            src_l = bass.AP(qsp.tensor, offset=S + q0,
                            ap=[[2 * S, 128], [128 * 2 * S, 8], [1, 256]])
            dma("sync", qg[:, :, 0:256], src_h,
                ["F0"] + [f"qq{fo}" for fo in range(8)], ["F0"], "ldqgh")
            dma("sync", qg[:, :, 256:512], src_l, ["F0"], ["F0"], "ldqgl")
            for kspan in range(4):
                ks0 = 512 * kspan
                base = 1 + 2 * (kspan % 2)
                ksl = FA16[:, 4096 * base:4096 * base + 8192]
                kkeys = [f"F{base}", f"F{base + 1}"]
                kg = ksl.rearrange("p (c n) -> p c n", c=8)
                srck_h = bass.AP(ksp.tensor, offset=ks0,
                                 ap=[[2 * S, 128], [128 * 2 * S, 8], [1, 512]])
                srck_l = bass.AP(ksp.tensor, offset=S + ks0,
                                 ap=[[2 * S, 128], [128 * 2 * S, 8], [1, 512]])
                dma("sync", kg[:, :, 0:512], srck_h,
                    kkeys + [f"kk{fo}" for fo in range(8)], kkeys, "ldkgh")
                dma("sync", kg[:, :, 512:1024], srck_l, kkeys, kkeys, "ldkgl")
                for qt in range(2):
                    bank = qt
                    for pi, (qh, kh) in enumerate([(0, 0), (0, 1), (1, 0)]):
                        for c in range(8):
                            T("tensor", ["F0"] + kkeys, [f"ps{bank}"],
                              lambda eng, c=c, qt=qt, bank=bank, ksl=ksl, pi=pi, qh=qh, kh=kh:
                                  eng.matmul(PS[bank][:, 0:512],
                                             fa16(0)[:, 512 * c + 256 * qh + 128 * qt:
                                                     512 * c + 256 * qh + 128 * qt + 128],
                                             ksl[:, 1024 * c + 512 * kh:1024 * c + 512 * kh + 512],
                                             start=(pi == 0 and c == 0),
                                             stop=(pi == 2 and c == 7)), "ds_mm")
                    T("scalar", [f"ps{bank}", f"F{5 + qt}"], [f"F{5 + qt}"],
                      lambda eng, qt=qt, bank=bank, ks0=ks0:
                          eng.copy(fa(5 + qt)[:, ks0:ks0 + 512], PS[bank][:, 0:512]), "ds_ev")
            for qt in range(2):
                st = fa(5 + qt)
                skey = f"F{5 + qt}"
                mx = CO[:, 1408 + qt:1409 + qt]
                sm = CO[:, 1410 + qt:1411 + qt]
                rc = CO[:, 1412 + qt:1413 + qt]
                T("vector", [skey, "BRHS"], ["BRHS"],
                  lambda eng, st=st, mx=mx:
                      eng.tensor_reduce(mx, st[:, 0:2048], axis=AX.X, op=OP.max), "dsm_max")
                T("vector", ["BRHS"], ["BRHS"],
                  lambda eng, mx=mx: eng.tensor_scalar(mx, mx, -1.0 / 32.0, None, op0=OP.mult),
                  "dsm_neg")
                T("scalar", [skey, "BRHS"], [skey, "BRHS"],
                  lambda eng, st=st, mx=mx, sm=sm:
                      eng.activation(st[:, 0:2048], st[:, 0:2048], AF.Exp, bias=mx,
                                     scale=1.0 / 32.0, accum_out=sm), "dsm_exp")
                T("vector", ["BRHS"], ["BRHS"],
                  lambda eng, sm=sm, rc=rc: eng.reciprocal(rc, sm), "dsm_rec")
                T("vector", [skey, "BRHS"], [skey],
                  lambda eng, st=st, rc=rc:
                      eng.tensor_scalar(st[:, 0:2048], st[:, 0:2048], rc, None, op0=OP.mult),
                  "dsm_nrm")
                if stage == "densedbgP" and g == 0:
                    dma("sync", y_d[256 * qt:256 * qt + 128, :], st[:, 0:1024],
                        [skey], ["dbg"], "dbgP")
                    dma("sync", y_d[256 * qt + 128:256 * qt + 256, :], st[:, 1024:2048],
                        [skey], ["dbg"], "dbgP2")
                for kc in range(16):
                    b2_ = 2 + (kc % 2)
                    pt_t = fa(0) if kc < 8 else fa(7)
                    pt_key = "F0" if kc < 8 else "F7"
                    T("tensor", [skey, "C"], [f"ps{b2_}"],
                      lambda eng, st=st, kc=kc, b2_=b2_:
                          eng.transpose(PS[b2_][:, 0:128], st[:, 128 * kc:128 * kc + 128],
                                        CO_IDENT), "dsm_T")
                    T("scalar", [f"ps{b2_}", pt_key], [pt_key],
                      lambda eng, kc=kc, qt=qt, b2_=b2_, pt_t=pt_t:
                          eng.copy(pt_t[:, 256 * (kc % 8) + 128 * qt:256 * (kc % 8) + 128 * qt + 128],
                                   PS[b2_][:, 0:128]), "dsm_Te")
            for kc in range(16):
                pt_t = fa(0) if kc < 8 else fa(7)
                pt_key = "F0" if kc < 8 else "F7"
                for dc in range(8):
                    T("tensor", [pt_key, f"YV{kc}"], [f"ps{dc}"],
                      lambda eng, kc=kc, dc=dc, pt_t=pt_t:
                          eng.matmul(PS[dc][:, 0:256],
                                     YA[:, 1024 * kc + 128 * dc:1024 * kc + 128 * dc + 128],
                                     pt_t[:, 256 * (kc % 8):256 * (kc % 8) + 256],
                                     start=(kc == 0), stop=(kc == 15)), "dav_mm")
            for dc in range(8):
                T("vector", [f"ps{dc}", f"X{dc}"], [f"X{dc}"],
                  lambda eng, dc=dc, q0=q0:
                      eng.tensor_tensor(xa(dc)[:, q0:q0 + 256], xa(dc)[:, q0:q0 + 256],
                                        PS[dc][:, 0:256], op=OP.add), "dav_resid")

    try:
        r = emit_ffn(0)
    except StopIteration:
        return bk.finalize()
    if stage in ("full", "dense", "densedbgP"):
        emit_dense()
    if stage == "full":
        emit_ffn(1)

    if stage == "densedbgP":
        return bk.finalize()
    out_is_y = (r == "partial")
    outsrc = (lambda c: ya(c)) if out_is_y else (lambda c: xa(c))
    okeys = [f"Y{c}" for c in range(8)] if out_is_y else [f"X{c}" for c in range(8)]
    for tt in range(16):
        stg = fa(4)[:, 0:1024]
        for c in range(8):
            bank = c % 2
            T("tensor", [okeys[c], "C"], [f"ps{bank}"],
              lambda eng, c=c, tt=tt, bank=bank:
                  eng.transpose(PS[bank][:, 0:128], outsrc(c)[:, 128 * tt:128 * tt + 128],
                                CO_IDENT), "oT")
            T("scalar", [f"ps{bank}", "F4"], ["F4"],
              lambda eng, c=c, bank=bank, stg=stg:
                  eng.copy(stg[:, 128 * c:128 * c + 128], PS[bank][:, 0:128]), "oTe")
        dma("sync", y_d[128 * tt:128 * tt + 128, :], stg, ["F4"], ["F4"], "sto")

    return bk.finalize()


def _in_maps(L0_attn_x, L0_attn_Wq, L0_attn_Wk, L0_attn_Wv, L0_ffn_wqkv,
             L0_ffn_q_scale, L0_ffn_k_scale, L0_ffn_wo, L0_ffn_ln_g, L0_ffn_ln_b,
             L0_ffn_w1, L0_ffn_b1, L0_ffn_w2, L0_ffn_b2):
    cvt = lambda a: np.ascontiguousarray(np.asarray(a, dtype=np.float32))
    x = cvt(L0_attn_x)
    common = {"Wq": cvt(L0_attn_Wq), "Wk": cvt(L0_attn_Wk), "Wv": cvt(L0_attn_Wv),
              "wqkv": cvt(L0_ffn_wqkv), "qs": cvt(L0_ffn_q_scale), "ks": cvt(L0_ffn_k_scale),
              "wo": cvt(L0_ffn_wo), "lng": cvt(L0_ffn_ln_g), "lnb": cvt(L0_ffn_ln_b),
              "w1": cvt(L0_ffn_w1), "b1": cvt(L0_ffn_b1), "w2": cvt(L0_ffn_w2),
              "b2": cvt(L0_ffn_b2)}
    return [dict(common, x=np.ascontiguousarray(x[b])) for b in range(NCORE)]


def kernel(**inputs):
    nc = build_program("full")
    in_maps = _in_maps(**inputs)
    res = run_bass_kernel_spmd(nc, in_maps, core_ids=list(range(NCORE)))
    out = np.stack([res.results[b]["y"] for b in range(NCORE)], axis=0)
    return out.astype(np.float32)


# version 12
# speedup vs baseline: 1.0895x; 1.0520x over previous
import numpy as np
import concourse.bass as bass
import concourse.mybir as mybir
from concourse.bass_utils import run_bass_kernel_spmd

F32 = mybir.dt.float32
F16 = mybir.dt.float16
AF = mybir.ActivationFunctionType
OP = mybir.AluOpType
AX = mybir.AxisListType

S = 2048
DM = 1024
DH = 64
INNER = 512
HID = 2730
NKI = 22
KI_LAST = 2730 - 21 * 128
NCORE = 8

PASSES1 = [(0, 0), (0, 2048), (2048, 0)]
PASSES2 = [(0, 0), (0, 512), (1024, 0)]
ENGINES = ("tensor", "vector", "scalar", "gpsimd", "sync", "sdma2", "gdma")
DMA_Q = ("sync", "sdma2", "gdma")
SPANS = [(s, s + 512) for s in range(0, S, 512)]


def make_consts():
    c = {}
    c["IDENT"] = np.eye(128, dtype=np.float32)
    prot = np.zeros((128, 128), dtype=np.float64)
    for b in (0, 64):
        for ml in range(32):
            prot[b + ml + 32, b + ml] = -1.0
        for ml in range(32, 64):
            prot[b + ml - 32, b + ml] = 1.0
    c["PROT"] = prot.astype(np.float32)
    bc0 = np.zeros((128, 128), dtype=np.float32); bc0[0, :] = 1.0
    c["BC0"] = bc0
    bc64 = np.zeros((128, 128), dtype=np.float32); bc64[64, :] = 1.0
    c["BC64"] = bc64
    bch = np.zeros((128, 128), dtype=np.float32)
    bch[0, 0:64] = 1.0; bch[1, 64:128] = 1.0
    c["BCH"] = bch
    c["SELA"] = np.ones((128, 1), dtype=np.float32)
    selh = np.zeros((128, 2), dtype=np.float32)
    selh[0:64, 0] = 1.0; selh[64:128, 1] = 1.0
    c["SELH"] = selh
    d = np.arange(64)
    i32 = d % 32
    inv_freq = 1.0 / (10000.0 ** (2.0 * i32 / 64.0))
    base = (2.0 * i32 + 0.4 * 64.0) / (1.4 * 64.0)
    t = np.arange(S, dtype=np.float64)
    freqs = t[None, :] * inv_freq[:, None]
    power = (t - S // 2) / 256.0
    scale = base[:, None] ** power[None, :]
    cosf, sinf = np.cos(freqs), np.sin(freqs)
    dup = lambda a: np.concatenate([a, a], axis=0).astype(np.float32)
    c["CQ"] = dup(cosf * scale)
    c["SQ"] = dup(sinf * scale)
    c["CK"] = dup(cosf / scale)
    c["SK"] = dup(sinf / scale)
    return c


NDMASEM = 24


class BK:

    def __init__(self, nc):
        self.nc = nc
        self.tasks = []
        self.res = {}
        self.count = {e: 0 for e in ENGINES}
        self.dma_idx = 0
        self.gdma_idx = 0
        self.duse = [0] * NDMASEM
        self.waited = {e: {} for e in ("tensor", "vector", "scalar", "gpsimd", "sync")}

    def task(self, eng, reads, writes, fn, name=""):
        stream = {"sdma2": "scalar", "gdma": "gpsimd"}.get(eng, eng)
        waits = {}
        def dep(d):
            dsem, dval = d
            waits[dsem] = max(waits.get(dsem, 0), dval)
        for r in reads:
            e = self.res.get(r)
            if e and e["w"]:
                dep(e["w"])
        for w in writes:
            e = self.res.get(w)
            if e:
                if e["w"]:
                    dep(e["w"])
                for rr in e["r"]:
                    dep(rr)
        if eng in DMA_Q:
            if eng == "gdma":
                sid = 18 + self.gdma_idx % 6
                self.gdma_idx += 1
            else:
                sid = self.dma_idx % 18
                self.dma_idx += 1
            self.duse[sid] += 1
            if self.duse[sid] > 1:
                waits[f"d{sid}"] = max(waits.get(f"d{sid}", 0), 16 * (self.duse[sid] - 1))
            me = (f"d{sid}", 16 * self.duse[sid])
        else:
            self.count[eng] += 1
            me = (eng, self.count[eng])
        if eng == "tensor":
            waits.pop("tensor", None)
        wd = self.waited[stream]
        eff = {}
        for se, val in waits.items():
            if se == stream and eng not in DMA_Q:
                pass
            if wd.get(se, -1) >= val:
                continue
            eff[se] = val
            wd[se] = val
        for r in reads:
            self.res.setdefault(r, {"w": None, "r": []})["r"].append(me)
        for w in writes:
            self.res[w] = {"w": me, "r": []}
        self.tasks.append({"eng": eng, "waits": eff, "fn": fn, "name": name,
                           "me": me})

    def finalize(self):
        nc = self.nc
        sems = {e: nc.alloc_semaphore(f"sem_{e}") for e in
                ("tensor", "vector", "scalar", "gpsimd", "sync")}
        for i in range(NDMASEM):
            sems[f"d{i}"] = nc.alloc_semaphore(f"dsem_{i}")
        with nc.Block() as block:
            def mk(ename, tasklist):
                def body(eng):
                    for t in tasklist:
                        for se, val in sorted(t["waits"].items()):
                            eng.wait_ge(sems[se], val)
                        inst = t["fn"](eng)
                        if inst is None:
                            inst = eng.nop()
                        tsem, tval = t["me"]
                        inst.then_inc(sems[tsem], 16 if t["eng"] in DMA_Q else 1)
                    if ename == "sync":
                        for e2 in ("tensor", "vector", "scalar", "gpsimd"):
                            if self.count[e2]:
                                eng.wait_ge(sems[e2], self.count[e2])
                        for i in range(NDMASEM):
                            if self.duse[i]:
                                eng.wait_ge(sems[f"d{i}"], 16 * self.duse[i])
                return body
            per = {}
            per["tensor"] = [t for t in self.tasks if t["eng"] == "tensor"]
            per["vector"] = [t for t in self.tasks if t["eng"] == "vector"]
            per["scalar"] = [t for t in self.tasks if t["eng"] in ("scalar", "sdma2")]
            per["gpsimd"] = [t for t in self.tasks if t["eng"] in ("gpsimd", "gdma")]
            per["sync"] = [t for t in self.tasks if t["eng"] == "sync"]
            block.tensor(mk("tensor", per["tensor"]))
            block.vector(mk("vector", per["vector"]))
            block.scalar(mk("scalar", per["scalar"]))
            block.gpsimd(mk("gpsimd", per["gpsimd"]))
            block.sync(mk("sync", per["sync"]))
        return nc


def build_program(stage="full"):
    nc = bass.Bass("TRN2", target_bir_lowering=False)
    C = make_consts()

    x_d = nc.dram_tensor("x", [S, DM], F32, kind="ExternalInput").ap()
    y_d = nc.dram_tensor("y", [S, DM], F32, kind="ExternalOutput").ap()
    Wq_d = nc.dram_tensor("Wq", [DM, DM], F32, kind="ExternalInput").ap()
    Wk_d = nc.dram_tensor("Wk", [DM, DM], F32, kind="ExternalInput").ap()
    Wv_d = nc.dram_tensor("Wv", [DM, DM], F32, kind="ExternalInput").ap()
    wqkv_d = nc.dram_tensor("wqkv", [2, DM, 3 * INNER], F32, kind="ExternalInput").ap()
    qs_d = nc.dram_tensor("qs", [2, DH], F32, kind="ExternalInput").ap()
    ks_d = nc.dram_tensor("ks", [2, DH], F32, kind="ExternalInput").ap()
    wo_d = nc.dram_tensor("wo", [2, INNER, DM], F32, kind="ExternalInput").ap()
    lng_d = nc.dram_tensor("lng", [2, DM], F32, kind="ExternalInput").ap()
    lnb_d = nc.dram_tensor("lnb", [2, DM], F32, kind="ExternalInput").ap()
    w1_d = nc.dram_tensor("w1", [2, DM, 2 * HID], F32, kind="ExternalInput").ap()
    b1_d = nc.dram_tensor("b1", [2, 2 * HID], F32, kind="ExternalInput").ap()
    w2_d = nc.dram_tensor("w2", [2, HID, DM], F32, kind="ExternalInput").ap()
    b2_d = nc.dram_tensor("b2", [2, DM], F32, kind="ExternalInput").ap()

    xsp = nc.dram_tensor("xsp", [8, 128, S], F32).ap()
    qsp = nc.dram_tensor("qsp", [8, 128, 2 * S], F16).ap()
    ksp = nc.dram_tensor("ksp", [8, 128, 2 * S], F16).ap()
    otsp = nc.dram_tensor("otsp", [4, 128, S], F32).ap()
    gsp = nc.dram_tensor("gsp", [NKI, 128, 2 * S], F16).ap()
    w1sp = nc.dram_tensor("w1sp", [2, NKI, 128, 4096], F16).ap()
    w2sp = nc.dram_tensor("w2sp", [2, NKI, 128, 2048], F16).ap()

    consts_d = {k: nc.inline_tensor(v, name=f"c_{k}").ap() for k, v in C.items()}

    XA = nc.alloc_sbuf_tensor("XA", [128, 8 * S], F32).ap()
    YA = nc.alloc_sbuf_tensor("YA", [128, 8 * S], F32).ap()
    FA = nc.alloc_sbuf_tensor("FA", [128, 8 * S], F32).ap()
    CO = nc.alloc_sbuf_tensor("CO", [128, 1856], F32).ap()
    GEL = nc.alloc_sbuf_tensor("GEL", [128, 1024], F32).ap()
    PS = [nc.alloc_psum_tensor(f"PS{i}", [128, 512], F32).ap() for i in range(8)]

    xa = lambda i: XA[:, 2048 * i:2048 * (i + 1)]
    ya = lambda i: YA[:, 2048 * i:2048 * (i + 1)]
    fa = lambda i: FA[:, 2048 * i:2048 * (i + 1)]
    XAr = XA.rearrange("p (c n) -> p c n", c=8)
    YAr = YA.rearrange("p (c n) -> p c n", c=8)
    FAr = FA.rearrange("p (c n) -> p c n", c=8)
    XA16 = XA.bitcast(F16)
    FA16 = FA.bitcast(F16)
    xa16 = lambda i: XA16[:, 4096 * i:4096 * (i + 1)]
    fa16 = lambda i: FA16[:, 4096 * i:4096 * (i + 1)]

    CO_IDENT = CO[:, 0:128]
    CO_PROT = CO[:, 128:256]
    CO_BC0 = CO[:, 256:384]
    CO_BC64 = CO[:, 384:512]
    CO_BCH = CO[:, 512:640]
    CO_SELA = CO[:, 640:641]
    CO_SELH = CO[:, 641:643]
    CO_EPSLN = CO[:, 643:644]
    CO_QS = [CO[:, 644 + l:645 + l] for l in range(2)]
    CO_KS = [CO[:, 646 + l:647 + l] for l in range(2)]
    CO_B2 = [CO[:, 648 + 8 * l:656 + 8 * l] for l in range(2)]
    CO_LNG = [CO[:, 664 + 8 * l:672 + 8 * l] for l in range(2)]
    CO_LNB = [CO[:, 680 + 8 * l:688 + 8 * l] for l in range(2)]
    CO_B1A = [CO[:, 696 + NKI * l:696 + NKI * (l + 1)] for l in range(2)]
    CO_B1G = [CO[:, 740 + NKI * l:740 + NKI * (l + 1)] for l in range(2)]
    CO_BRHS = CO[:, 832:1856]

    bk = BK(nc)
    T = bk.task

    def dma(qeng, dst, src, reads, writes, name="", slow=False):
        if slow:
            T(qeng, reads, writes,
              lambda eng, d=dst, s=src: eng.dma_start(out=d, in_=s,
                                                      allow_slow_non_contiguous=True), name)
        else:
            T(qeng, reads, writes, lambda eng, d=dst, s=src: eng.dma_start(out=d, in_=s), name)

    for key, dst in [("IDENT", CO_IDENT), ("PROT", CO_PROT), ("BC0", CO_BC0),
                     ("BC64", CO_BC64), ("BCH", CO_BCH), ("SELA", CO_SELA),
                     ("SELH", CO_SELH)]:
        dma("sync", dst, consts_d[key][:, :], ["C"], ["C"], f"ld_{key}")
    T("vector", ["C"], ["C"], lambda eng: eng.memset(CO_EPSLN, 1e-5), "eps")
    T("vector", ["BRHS", "BRHSb"], ["BRHS", "BRHSb"],
      lambda eng: eng.memset(CO_BRHS, 0.0), "brhs0")
    for l in range(2):
        for half in (0, 1):
            dma("sync", CO_QS[l][64 * half:64 * half + 64, :],
                qs_d[l:l + 1, :].transpose([1, 0]), ["C"], ["C"], "qs", slow=True)
            dma("sync", CO_KS[l][64 * half:64 * half + 64, :],
                ks_d[l:l + 1, :].transpose([1, 0]), ["C"], ["C"], "ks", slow=True)
        dma("sync", CO_B2[l], b2_d[l:l + 1, :].rearrange("a (c p) -> (a p) c", p=128),
            ["C"], ["C"], "b2", slow=True)
        dma("sync", CO_LNG[l], lng_d[l:l + 1, :].rearrange("a (c p) -> (a p) c", p=128),
            ["C"], ["C"], "lng", slow=True)
        dma("sync", CO_LNB[l], lnb_d[l:l + 1, :].rearrange("a (c p) -> (a p) c", p=128),
            ["C"], ["C"], "lnb", slow=True)
        for ki in range(NKI):
            n = 128 if ki < NKI - 1 else KI_LAST
            dma("sync", CO_B1A[l][0:n, ki:ki + 1],
                b1_d[l:l + 1, 128 * ki:128 * ki + n].transpose([1, 0]), ["C"], ["C"], "b1a", slow=True)
            dma("sync", CO_B1G[l][0:n, ki:ki + 1],
                b1_d[l:l + 1, HID + 128 * ki:HID + 128 * ki + n].transpose([1, 0]),
                ["C"], ["C"], "b1g", slow=True)

    for tt in range(16):
        stg = fa(tt % 2)[:, 0:1024]
        skey = f"F{tt % 2}"
        dma("sync", stg, x_d[128 * tt:128 * tt + 128, :], [skey], [skey], "ldx")
        for c in range(8):
            ps, pk = PS[c % 2], f"ps{c % 2}"
            T("tensor", [skey, "C"], [pk],
              lambda eng, ps=ps, stg=stg, c=c:
                  eng.transpose(ps[:, 0:128], stg[:, 128 * c:128 * c + 128], CO_IDENT), "xT")
            T("scalar", [pk], [f"X{c}"],
              lambda eng, ps=ps, c=c, tt=tt:
                  eng.copy(xa(c)[:, 128 * tt:128 * tt + 128], ps[:, 0:128]), "xTe")

    def emit_ln(srcA4, srcB4, src, srckeys, out, outkeys, sqA, sqA_key, sqB, sqB_key,
                st_scr, st_key, gamma=None, beta=None, name="ln", split_out=None):
        for si, (s0, s1) in enumerate(SPANS):
            b_sx, k_sx = PS[(si % 2) * 2], f"ps{(si % 2) * 2}"
            b_sq, k_sq = PS[(si % 2) * 2 + 1], f"ps{(si % 2) * 2 + 1}"
            b_r, k_r = PS[4 + (si % 2) * 2], f"ps{4 + (si % 2) * 2}"
            b_m, k_m = PS[5 + (si % 2) * 2], f"ps{5 + (si % 2) * 2}"
            for c in range(8):
                T("tensor", [srckeys[c], "C"], [k_sx],
                  lambda eng, c=c, s0=s0, s1=s1, b=b_sx:
                      eng.matmul(b[0:1, 0:512], CO_SELA, src(c)[:, s0:s1],
                                 start=(c == 0), stop=(c == 7)), name + "_sx")
            T("scalar", srckeys[0:4] + [sqA_key], [sqA_key],
              lambda eng, s0=s0:
                  eng.activation(sqA.rearrange("p (c n) -> p c n", c=4),
                                 srcA4(s0), AF.Square), name + "_sqA")
            T("vector", srckeys[4:8] + [sqB_key], [sqB_key],
              lambda eng, s0=s0:
                  eng.tensor_tensor(sqB.rearrange("p (c n) -> p c n", c=4),
                                    srcB4(s0), srcB4(s0), op=OP.mult), name + "_sqB")
            for c in range(8):
                sq = (sqA if c < 4 else sqB)[:, 512 * (c % 4):512 * (c % 4) + 512]
                T("tensor", [(sqA_key if c < 4 else sqB_key), "C"], [k_sq],
                  lambda eng, c=c, b=b_sq, sq=sq:
                      eng.matmul(b[0:1, 0:512], CO_SELA, sq, start=(c == 0), stop=(c == 7)),
                  name + "_sq")
            st2 = st_scr
            T("scalar", [k_sx], [st_key],
              lambda eng, b=b_sx: eng.copy(st2[0:1, 0:512], b[0:1, 0:512]), name + "_ev1")
            T("scalar", [k_sq, st_key], [st_key],
              lambda eng, b=b_sq: eng.copy(st2[0:1, 512:1024], b[0:1, 0:512]), name + "_ev2")
            T("vector", [st_key], [st_key],
              lambda eng: eng.tensor_scalar(st2[0:1, 0:1024], st2[0:1, 0:1024],
                                            1.0 / DM, None, op0=OP.mult), name + "_sc")
            T("vector", [st_key], [st_key],
              lambda eng: eng.tensor_tensor(st2[0:1, 1024:1536], st2[0:1, 0:512],
                                            st2[0:1, 0:512], op=OP.mult), name + "_m2")
            T("vector", [st_key], [st_key],
              lambda eng: eng.tensor_tensor(st2[0:1, 1024:1536], st2[0:1, 512:1024],
                                            st2[0:1, 1024:1536], op=OP.subtract), name + "_var")
            T("scalar", [st_key, "C"], [st_key],
              lambda eng: eng.activation(st2[0:1, 1536:2048], st2[0:1, 1024:1536],
                                         AF.Ln, bias=CO_EPSLN[0:1, :]), name + "_ln")
            T("scalar", [st_key], ["BRHS"],
              lambda eng: eng.activation(CO_BRHS[0:1, 0:512], st2[0:1, 1536:2048],
                                         AF.Exp, scale=-0.5), name + "_exp")
            T("vector", [st_key, "BRHS"], ["BRHS"],
              lambda eng: eng.tensor_tensor(CO_BRHS[0:1, 512:1024], st2[0:1, 0:512],
                                            CO_BRHS[0:1, 0:512], op=OP.mult), name + "_mr")
            T("tensor", ["BRHS", "C"], [k_r],
              lambda eng, b=b_r: eng.matmul(b[:, 0:512], CO_BC0, CO_BRHS[:, 0:512],
                                            start=True, stop=True), name + "_bcr")
            T("tensor", ["BRHS", "C"], [k_m],
              lambda eng, b=b_m: eng.matmul(b[:, 0:512], CO_BC0, CO_BRHS[:, 512:1024],
                                            start=True, stop=True), name + "_bcm")
            for c in range(8):
                okc = outkeys[c] if isinstance(outkeys[c], list) else [outkeys[c]]
                if split_out is None:
                    dst = lambda c=c, s0=s0, s1=s1: out(c)[:, s0:s1]
                    T("vector", [srckeys[c], k_r], okc,
                      lambda eng, c=c, s0=s0, s1=s1, b=b_r, dst=dst:
                          eng.tensor_tensor(dst(), src(c)[:, s0:s1], b[:, 0:512],
                                            op=OP.mult), name + "_ap1")
                    T("vector", okc + [k_m], okc,
                      lambda eng, b=b_m, dst=dst:
                          eng.tensor_tensor(dst(), dst(), b[:, 0:512],
                                            op=OP.subtract), name + "_ap2")
                else:
                    hi_fn, lo_fn = split_out
                    tmp = GEL[:, 512 * (c % 2):512 * (c % 2) + 512]
                    tk = f"GE{c % 2}"
                    T("vector", [srckeys[c], k_r, tk], [tk],
                      lambda eng, c=c, s0=s0, s1=s1, b=b_r, tmp=tmp:
                          eng.tensor_tensor(tmp, src(c)[:, s0:s1], b[:, 0:512],
                                            op=OP.mult), name + "_ap1")
                    T("vector", [tk, k_m], [tk],
                      lambda eng, b=b_m, tmp=tmp:
                          eng.tensor_tensor(tmp, tmp, b[:, 0:512],
                                            op=OP.subtract), name + "_ap2")
                    T("vector", [tk, "C"], [tk],
                      lambda eng, c=c, tmp=tmp:
                          eng.tensor_scalar(tmp, tmp, gamma[:, c:c + 1], beta[:, c:c + 1],
                                            op0=OP.mult, op1=OP.add), name + "_gb")
                    T("scalar", [tk] + okc, okc,
                      lambda eng, c=c, s0=s0, tmp=tmp, hi_fn=hi_fn:
                          eng.copy(hi_fn(c)[:, s0:s0 + 512], tmp), name + "_hi")
                    T("vector", [tk] + okc, okc,
                      lambda eng, c=c, s0=s0, tmp=tmp, hi_fn=hi_fn, lo_fn=lo_fn:
                          eng.tensor_tensor(lo_fn(c)[:, s0:s0 + 512], tmp,
                                            hi_fn(c)[:, s0:s0 + 512], op=OP.subtract),
                      name + "_lo")

    def ldwqkv(l, p):
        wsl = []
        for c in range(8):
            if c < 5:
                wsl.append((xa(3)[:, 384 * c:384 * (c + 1)], "X3"))
            else:
                wsl.append((xa(4)[:, 384 * (c - 5):384 * (c - 4)], "X4"))
        for c in range(8):
            src = bass.AP(wqkv_d.tensor,
                          offset=l * DM * 3 * INNER + (128 * c) * (3 * INNER) + 128 * p,
                          ap=[[3 * INNER, 128], [INNER, 3], [1, 128]])
            dma("sync", wsl[c][0].rearrange("p (b n) -> p b n", b=3), src,
                [wsl[c][1]], [wsl[c][1]], "ldwqkv")
        return wsl

    def emit_pair(inv, l, p, wsl):
        lt = f"i{inv}l{l}p{p}"
        PQ, PK, PV = xa(0), xa(1), xa(2)
        kPQ, kPK, kPV = "X0", "X1", "X2"
        def qkv_rhs(c):
            return ya(c) if c < 4 else fa(c - 4)
        qkv_rkey = [f"Y{c}" if c < 4 else f"F{c - 4}" for c in range(8)]
        for si, (s0, s1) in enumerate(SPANS):
            for c in range(8):
                w, wkey = wsl[c]
                for t_i, bank in enumerate([0, 1, 2]):
                    T("tensor", [wkey, qkv_rkey[c]], [f"ps{bank}"],
                      lambda eng, c=c, t_i=t_i, bank=bank, s0=s0, s1=s1, w=w:
                          eng.matmul(PS[bank][:, 0:512], w[:, 128 * t_i:128 * t_i + 128],
                                     qkv_rhs(c)[:, s0:s1], start=(c == 0), stop=(c == 7)),
                      "qkv_mm")
            for bank, (dst, key) in enumerate([(PQ, kPQ), (PK, kPK), (PV, kPV)]):
                wk = [key] if bank == 2 else [key, f"{key}s{si}"]
                T("scalar", [f"ps{bank}"], wk,
                  lambda eng, bank=bank, dst=dst, s0=s0:
                      eng.copy(dst[:, s0:s0 + 512], PS[bank][:, 0:512]), "qkv_ev")
        dma("sync", fa(7), consts_d["SK"][:, :], ["F7"], ["F7"], "ldSK")
        dma("sync", xa(3), consts_d["CQ"][:, :], ["X3"], ["X3"], "ldCQ")
        dma("sync", xa(4), consts_d["SQ"][:, :], ["X4"], ["X4"], "ldSQ")
        chains = []
        for TP, tb, ck, sk_, scl in [(PQ, kPQ, "X3", "X4", CO_QS[l]),
                                     (PK, kPK, "X3", "F7", CO_KS[l])]:
            for si, (s0, s1) in enumerate(SPANS):
                chains.append((TP, tb, ck, sk_, scl, si, s0))

        def chain_head(idx):
            TP, tb, ck, sk_, scl, si, s0 = chains[idx]
            par = idx % 2
            bss, kss = (PS[3], "ps3") if par == 0 else (PS[0], "ps0")
            off = 512 * par
            sq, sqk = (fa(4)[:, 0:512], "F4") if par == 0 else (GEL[:, 0:512], "GE0")
            skey = f"{tb}s{si}"
            T("scalar", [skey, sqk], [sqk],
              lambda eng, TP=TP, s0=s0, sq=sq:
                  eng.activation(sq, TP[:, s0:s0 + 512], AF.Square), "l2_sq")
            T("tensor", [sqk, "C"], [kss],
              lambda eng, sq=sq, bss=bss:
                  eng.matmul(bss[0:2, 0:512], CO_SELH, sq, start=True, stop=True), "l2_mm")
            T("vector", [kss], [kss],
              lambda eng, bss=bss: eng.tensor_scalar(bss[0:2, 0:512], bss[0:2, 0:512],
                                                     1e-18, None, op0=OP.max), "l2_clip")
            T("scalar", [kss], [kss],
              lambda eng, bss=bss: eng.activation(bss[0:2, 0:512], bss[0:2, 0:512],
                                                  AF.Ln), "l2_ln")
            bhk = "BRHS" if par == 0 else "BRHSb"
            T("scalar", [kss, bhk], [bhk],
              lambda eng, bss=bss, off=off:
                  eng.activation(CO_BRHS[0:2, off:off + 512], bss[0:2, 0:512],
                                 AF.Exp, scale=-0.5), "l2_exp")

        def chain_tail(idx):
            TP, tb, ck, sk_, scl, si, s0 = chains[idx]
            par = idx % 2
            brq, krq = (PS[4], "ps4") if par == 0 else (PS[1], "ps1")
            brot, krot = (PS[5], "ps5") if par == 0 else (PS[2], "ps2")
            off = 512 * par
            t1, t1k = (fa(4)[:, 512:1024], "F4") if par == 0 else (GEL[:, 512:1024], "GE1")
            t2, t2k = (fa(5)[:, 0:512], "F5") if par == 0 else (fa(5)[:, 512:1024], "F5")
            skey = f"{tb}s{si}"
            cconst = xa(3) if ck == "X3" else None
            sconst = xa(4) if sk_ == "X4" else fa(7)
            bhk = "BRHS" if par == 0 else "BRHSb"
            T("tensor", [bhk, "C"], [krq],
              lambda eng, brq=brq, off=off:
                  eng.matmul(brq[:, 0:512], CO_BCH, CO_BRHS[:, off:off + 512],
                             start=True, stop=True), "l2_bc")
            T("vector", [skey, krq, "C"], [skey, tb],
              lambda eng, TP=TP, s0=s0, scl=scl, brq=brq:
                  eng.scalar_tensor_tensor(TP[:, s0:s0 + 512], TP[:, s0:s0 + 512],
                                           scl, brq[:, 0:512], op0=OP.mult, op1=OP.mult),
              "l2_ap")
            T("tensor", [skey, "C"], [krot],
              lambda eng, TP=TP, s0=s0, brot=brot:
                  eng.matmul(brot[:, 0:512], CO_PROT, TP[:, s0:s0 + 512],
                             start=True, stop=True), "rot_mm")
            T("vector", [skey, ck, t1k], [t1k],
              lambda eng, TP=TP, s0=s0, t1=t1:
                  eng.tensor_tensor(t1, TP[:, s0:s0 + 512],
                                    xa(3)[:, s0:s0 + 512], op=OP.mult), "rot_t1")
            T("vector", [krot, sk_, t2k], [t2k],
              lambda eng, s0=s0, t2=t2, brot=brot, sconst=sconst:
                  eng.tensor_tensor(t2, brot[:, 0:512],
                                    sconst[:, s0:s0 + 512], op=OP.mult), "rot_t2")
            T("vector", [t1k, t2k], [skey, tb],
              lambda eng, TP=TP, s0=s0, t1=t1, t2=t2:
                  eng.tensor_tensor(TP[:, s0:s0 + 512], t1, t2, op=OP.add), "rot_add")

        chain_head(0)
        for idx in range(8):
            if idx + 1 < 8:
                chain_head(idx + 1)
            chain_tail(idx)
            if idx == 3:
                dma("sync", xa(3), consts_d["CK"][:, :], ["X3"], ["X3"], "ldCK")
        nwsl = ldwqkv(l, p + 1) if p + 1 < 4 else None
        VT0 = 5 * 2048
        T("vector", ["X5"], ["X5"],
          lambda eng: eng.memset(XA[:, VT0 + 64:VT0 + 16 * 132:132], 1.0), "vt1")
        T("vector", ["X5"], ["X5"],
          lambda eng: eng.memset(XA[:, VT0 + 65:VT0 + 16 * 132:132], 1.0), "vt2")
        T("vector", ["X5", "X6"], ["X5", "X6"],
          lambda eng: eng.memset(XA[:, VT0 + 130:VT0 + 16 * 132:132], 1.0), "vt3")
        T("vector", ["X5", "X6"], ["X5", "X6"],
          lambda eng: eng.memset(XA[:, VT0 + 131:VT0 + 16 * 132:132], 1.0), "vt4")
        for tt in range(16):
            bank = 6 + (tt % 2)
            T("tensor", [kPV, "C"], [f"ps{bank}"],
              lambda eng, tt=tt, bank=bank:
                  eng.transpose(PS[bank][:, 0:128], PV[:, 128 * tt:128 * tt + 128], CO_IDENT),
              "vt_T")
            T("scalar", [f"ps{bank}"], ["X5", "X6"],
              lambda eng, tt=tt, bank=bank:
                  eng.copy(XA[:, VT0 + 132 * tt:VT0 + 132 * tt + 64], PS[bank][:, 0:64]),
              "vt_evA")
            T("scalar", [f"ps{bank}"], ["X5", "X6"],
              lambda eng, tt=tt, bank=bank:
                  eng.copy(XA[:, VT0 + 132 * tt + 66:VT0 + 132 * tt + 130],
                           PS[bank][:, 64:128]), "vt_evB")
        for w in range(4):
            emit_window(l, p, w, PQ, PK, kPQ, kPK, VT0)
        dma("sdma2", otsp[p, 0:64, :], xa(7)[0:64, :], ["X7"], [f"ot{p}"], "ot_spA")
        return nwsl

    def emit_window(l, p, w, PQ, PK, kPQ, kPK, VT0):
        s0 = 512 * w
        OTA = xa(7)
        first_av = [True, True]

        def geom(m):
            ca = 4 * (w - 1) + m if m < 4 else 4 * w + (m - 4)
            q_lo = 0 if m < 4 else 128 * (m - 4)
            q_hi = 128 * (m + 1) if m < 4 else 512
            return ca, q_lo, q_hi - q_lo

        def emit_st(m, i):
            ca, q_lo, n = geom(m)
            for h, (bank, base) in enumerate([((i % 2) * 2, 0), ((i % 2) * 2 + 1, 64)]):
                T("tensor", [f"{kPQ}s{w}", f"{kPK}s{ca // 4}"], [f"ps{bank}"],
                  lambda eng, bank=bank, base=base, ca=ca, q_lo=q_lo, n=n:
                      eng.matmul(PS[bank][:, q_lo:q_lo + n],
                                 PK[base:base + 64, 128 * ca:128 * ca + 128],
                                 PQ[base:base + 64, s0 + q_lo:s0 + q_lo + n],
                                 start=True, stop=True), "st_mm")

        def emit_expmask_av(m, i):
            ca, q_lo, n = geom(m)
            for h, bank in enumerate([(i % 2) * 2, (i % 2) * 2 + 1]):
                qk = f"F6q{2 * (i % 2) + h}"
                off = 1024 * (i % 2) + 512 * h
                T("scalar", [f"ps{bank}", qk], [qk],
                  lambda eng, bank=bank, off=off, q_lo=q_lo, n=n:
                      eng.activation(fa(6)[:, off + q_lo:off + q_lo + n],
                                     PS[bank][:, q_lo:q_lo + n], AF.Exp, scale=8.0), "st_exp")
                if m < 4:
                    T("gpsimd", [qk], [qk],
                      lambda eng, off=off, m=m, n=n:
                          eng.affine_select(fa(6)[:, off:off + n], fa(6)[:, off:off + n],
                                            pattern=[[-1, n]], compare_op=OP.is_ge,
                                            fill=0.0, base=128 * m, channel_multiplier=1),
                      "mask_lb")
                else:
                    T("gpsimd", [qk], [qk],
                      lambda eng, off=off, q_lo=q_lo, n=n:
                          eng.affine_select(fa(6)[:, off + q_lo:off + q_lo + n],
                                            fa(6)[:, off + q_lo:off + q_lo + n],
                                            pattern=[[1, n]], compare_op=OP.is_ge,
                                            fill=0.0, base=0, channel_multiplier=-1),
                      "mask_iw")
            for h, bank in enumerate([4, 5]):
                qk = f"F6q{2 * (i % 2) + h}"
                off = 1024 * (i % 2) + 512 * h
                vt = XA[:, VT0 + 132 * ca + 66 * h:VT0 + 132 * ca + 66 * h + 66]
                st = first_av[h]
                T("tensor", [qk, "X5", "X6"], [f"ps{bank}"],
                  lambda eng, bank=bank, vt=vt, off=off, q_lo=q_lo, n=n, st=st:
                      eng.matmul(PS[bank][0:66, q_lo:q_lo + n], vt,
                                 fa(6)[:, off + q_lo:off + q_lo + n],
                                 start=st, stop=False, skip_group_check=True), "av_mm")
                first_av[h] = False

        ms = [m for m in [4, 0, 1, 2, 3, 5, 6, 7] if not (w == 0 and m < 4)]
        emit_st(ms[0], 0)
        for i, m in enumerate(ms):
            if i + 1 < len(ms):
                emit_st(ms[i + 1], i + 1)
            emit_expmask_av(m, i)
        RSS = CO[:, 832:1344]
        for h, bank in enumerate([4, 5]):
            T("scalar", [f"ps{bank}", "BRHS"], ["BRHS"],
              lambda eng, bank=bank: eng.copy(RSS[64:65, 0:512], PS[bank][64:65, 0:512]),
              "av_sums")
            T("tensor", ["BRHS", "C"], [f"ps{6 + h}"],
              lambda eng, h=h: eng.matmul(PS[6 + h][0:64, 0:512], CO_BC64[:, 0:64],
                                          RSS[:, 0:512], start=True, stop=True), "av_bc")
            T("scalar", [f"ps{6 + h}"], [f"ps{6 + h}"],
              lambda eng, h=h: eng.activation(PS[6 + h][0:64, 0:512], PS[6 + h][0:64, 0:512],
                                              AF.Ln), "av_lnr")
            T("scalar", [f"ps{6 + h}", "F4"], ["F4"],
              lambda eng, h=h: eng.activation(fa(4)[0:64, 512 * h:512 * h + 512],
                                              PS[6 + h][0:64, 0:512], AF.Exp, scale=-1.0),
              "av_rec")
        T("vector", ["ps4", "F4", "X7"], ["X7"],
          lambda eng: eng.tensor_tensor(OTA[0:64, s0:s0 + 512], PS[4][0:64, 0:512],
                                        fa(4)[0:64, 0:512], op=OP.mult), "av_evA")
        T("vector", ["ps5", "F4", "F5"], ["F5"],
          lambda eng: eng.tensor_tensor(fa(5)[0:64, 0:512], PS[5][0:64, 0:512],
                                        fa(4)[0:64, 512:1024], op=OP.mult), "av_evB")
        dma("sdma2", otsp[p, 64:128, s0:s0 + 512], fa(5)[0:64, 0:512], ["F5"], [f"ot{p}"],
            "ot_spB")

    def emit_ffn_layer(inv, l):
        for j in range(4):
            c = 4 + j
            T("vector", [f"F{j}"], [f"F{j}"],
              lambda eng, j=j: eng.memset(fa(j)[:, 0:1], 0.0), "shz")
            T("vector", [f"Y{c}", f"F{j}"], [f"F{j}"],
              lambda eng, j=j, c=c: eng.tensor_copy(fa(j)[:, 1:2048], ya(c)[:, 0:2047]), "shc")
        wsl = ldwqkv(l, 0)
        for p in range(4):
            wsl = emit_pair(inv, l, p, wsl)
        for j in range(4):
            dst = fa(j // 2)[:, 1024 * (j % 2):1024 * (j % 2) + 1024]
            dma("sync", dst, wo_d[l, 128 * j:128 * j + 128, :],
                [f"F{j // 2}"], [f"F{j // 2}"], "ldwo")
        for si, (s0, s1) in enumerate(SPANS):
            dma("sync", fa(6)[:, 0:512], otsp[0, :, s0:s1], ["F6q0", "ot0"], ["F6q0"], "ldot")
            for j in range(4):
                ots = fa(6)[:, 512 * j:512 * j + 512]
                if j + 1 < 4:
                    dma("sync", fa(6)[:, 512 * (j + 1):512 * (j + 1) + 512],
                        otsp[j + 1, :, s0:s1], [f"F6q{j + 1}", f"ot{j + 1}"],
                        [f"F6q{j + 1}"], "ldot")
                for fo in range(8):
                    wob = fa(j // 2)[:, 1024 * (j % 2) + 128 * fo:1024 * (j % 2) + 128 * fo + 128]
                    T("tensor", [f"F6q{j}", f"F{j // 2}"], [f"ps{fo}"],
                      lambda eng, fo=fo, j=j, wob=wob, ots=ots:
                          eng.matmul(PS[fo][:, 0:512], wob, ots, start=(j == 0), stop=(j == 3)),
                      "wo_mm")
            for fo in range(8):
                T("vector", [f"ps{fo}", f"Y{fo}"], [f"Y{fo}"],
                  lambda eng, fo=fo, s0=s0:
                      eng.tensor_tensor(ya(fo)[:, s0:s0 + 512], ya(fo)[:, s0:s0 + 512],
                                        PS[fo][:, 0:512], op=OP.add), "wo_resid")
        if stage == f"attn_i{inv}l{l}":
            return False
        emit_mlp(inv, l)
        return True

    def emit_mlp(inv, l):
        for j in range(4):
            c = 4 + j
            T("vector", [f"F{j}"], [f"F{j}"],
              lambda eng, j=j: eng.memset(fa(j)[:, 0:1], 0.0), "shz2")
            T("vector", [f"Y{c}", f"F{j}"], [f"F{j}"],
              lambda eng, j=j, c=c: eng.tensor_copy(fa(j)[:, 1:2048], ya(c)[:, 0:2047]), "shc2")
        hsrc = lambda c: (ya(c) if c < 4 else fa(c - 4))
        hkeys = [f"Y{c}" if c < 4 else f"F{c - 4}" for c in range(8)]
        h2h = lambda c: xa16(c)[:, 0:2048]
        h2l = lambda c: xa16(c)[:, 2048:4096]
        emit_ln(lambda s0: YAr[:, 0:4, s0:s0 + 512], lambda s0: FAr[:, 0:4, s0:s0 + 512],
                hsrc, hkeys, None,
                [[f"X{c}"] + [f"X{c}s{si}" for si in range(4)] if c < 2 else f"X{c}"
                 for c in range(8)],
                fa(4), "F4", fa(7), "F7", fa(5), "F5",
                gamma=CO_LNG[l], beta=CO_LNB[l], name=f"lnm{inv}{l}",
                split_out=(h2h, h2l))
        if stage == "mlpdbgH" and inv == 0 and l == 0:
            for c in range(8):
                dma("sync", y_d[128 * c:128 * c + 128, :], xa(c)[:, 0:1024],
                    [f"X{c}"], ["dbg"], "dbgH")
                dma("sync", y_d[1024 + 128 * c:1024 + 128 * c + 128, :], xa(c)[:, 1024:2048],
                    [f"X{c}"], ["dbg"], "dbgH2")
            raise StopIteration
        def ldw1(ki):
            n = 128 if ki < NKI - 1 else KI_LAST
            if inv == 0:
                stg = fa(6 + ki % 2)
                skeys = (["F6q0", "F6q1", "F6q2", "F6q3"] if ki % 2 == 0 else ["F7"])
                src = bass.AP(w1_d.tensor,
                              offset=l * DM * 2 * HID + 128 * ki,
                              ap=[[2 * HID, 128], [HID, 2], [2 * HID * 128, 8], [1, n]])
                dma("sync", stg.rearrange("p (b c n) -> p b c n", b=2, c=8)[:, :, :, 0:n],
                    src, skeys, skeys, "ldw1")
            else:
                w16 = fa16(ki % 2)
                wkey = f"F{ki % 2}"
                dma("sync", w16[:, :], w1sp[l, ki, :, :], [wkey, f"w1s{l}_{ki}"],
                    [wkey], "ldw1s")

        def split_w1(ki):
            if inv != 0:
                return
            stg = fa(6 + ki % 2)
            skeys = (["F6q0", "F6q1", "F6q2", "F6q3"] if ki % 2 == 0 else ["F7"])
            skey = skeys[0]
            w16 = fa16(ki % 2)
            wkey = f"F{ki % 2}"
            T("scalar", skeys + [wkey], [wkey],
              lambda eng, stg=stg, w16=w16: eng.copy(w16[:, 0:2048], stg[:, 0:2048]), "w1hi")
            T("vector", skeys + [wkey], [wkey],
              lambda eng, stg=stg, w16=w16:
                  eng.tensor_tensor(w16[:, 2048:4096], stg[:, 0:2048], w16[:, 0:2048],
                                    op=OP.subtract), "w1lo")
            dma("sdma2", w1sp[l, ki, :, :], w16[:, :], [wkey], [f"w1s{l}_{ki}"], "w1spl")

        ldw1(0)
        split_w1(0)
        if NKI > 1:
            ldw1(1)
        for ki in range(NKI):
            n = 128 if ki < NKI - 1 else KI_LAST
            w16 = fa16(ki % 2)
            wkey = f"F{ki % 2}"
            for si, (s0, s1) in enumerate(SPANS):
                if si == 1:
                    if ki + 1 < NKI:
                        split_w1(ki + 1)
                    if inv == 0 and ki + 2 < NKI:
                        ldw1(ki + 2)
                bA, kA = PS[(si % 2) * 2], f"ps{(si % 2) * 2}"
                bG, kG = PS[(si % 2) * 2 + 1], f"ps{(si % 2) * 2 + 1}"
                for half, (bank, kb) in [(0, (bA, kA)), (1, (bG, kG))]:
                    for pi, (w_off, x_off) in enumerate(PASSES1):
                        for c in range(8):
                            T("tensor", [wkey, f"X{c}"], [kb],
                              lambda eng, c=c, s0=s0, n=n, bank=bank, w16=w16,
                                     half=half, w_off=w_off, x_off=x_off, pi=pi:
                                  eng.matmul(bank[0:n, 0:512],
                                             w16[:, w_off + 1024 * half + 128 * c:
                                                 w_off + 1024 * half + 128 * c + n],
                                             xa16(c)[:, x_off + 512 * (s0 // 512):
                                                     x_off + 512 * (s0 // 512) + 512],
                                             start=(pi == 0 and c == 0),
                                             stop=(pi == len(PASSES1) - 1 and c == 7)), "u16")
                gsc, gk = GEL[:, 512 * (si % 2):512 * (si % 2) + 512], f"GE{si % 2}"
                T("scalar", [kG, "C", gk], [gk],
                  lambda eng, ki=ki, n=n, bG=bG, gsc=gsc:
                      eng.activation(gsc[0:n, 0:512], bG[0:n, 0:512], AF.Gelu,
                                     bias=CO_B1G[l][0:n, ki:ki + 1]), "gelu")
                T("vector", [kA, gk, "C", "F5"], ["F5"],
                  lambda eng, ki=ki, si=si, n=n, bA=bA, gsc=gsc:
                      eng.scalar_tensor_tensor(fa(5)[0:n, 512 * si:512 * si + 512],
                                               bA[0:n, 0:512], CO_B1A[l][0:n, ki:ki + 1],
                                               gsc[0:n, 0:512], op0=OP.add, op1=OP.mult),
                  "glu")
            if inv != 0 and ki + 2 < NKI:
                ldw1(ki + 2)
            g16 = fa16(2 + ki % 2)
            gkey = f"F{2 + ki % 2}"
            T("scalar", ["F5", gkey], [gkey],
              lambda eng, g16=g16, n=n: eng.copy(g16[0:n, 0:2048], fa(5)[0:n, :]), "ghi")
            T("vector", ["F5", gkey], [gkey],
              lambda eng, g16=g16, n=n:
                  eng.tensor_tensor(g16[0:n, 2048:4096], fa(5)[0:n, :], g16[0:n, 0:2048],
                                    op=OP.subtract), "glo")
            dma("sdma2", gsp[ki, 0:n, :], g16[0:n, :], [gkey], [f"g{ki}"], "g_sp")
        W2R = [(fa16(0)[:, 0:2048], "F0"), (fa16(1)[:, 0:2048], "F1"),
               (fa16(3)[:, 0:2048], "F3")]
        RSTG = [(fa(4)[:, 0:1024], "F4"), (fa(5)[:, 0:1024], "F5")]

        def p2_loads(ki, si, s0):
            n = 128 if ki < NKI - 1 else KI_LAST
            w16, wk = W2R[ki % 3]
            g16 = fa16(2)[:, 1024 * (ki % 3):1024 * (ki % 3) + 1024]
            if inv == 0 and si == 0:
                wstg, sk = RSTG[ki % 2]
                dma("sync", wstg[0:n, :], w2_d[l, 128 * ki:128 * ki + n, :],
                    [sk], [sk], "ldw2")
                T("scalar", [sk, wk], [wk],
                  lambda eng, wstg=wstg, w16=w16: eng.copy(w16[:, 0:1024], wstg[:, 0:1024]),
                  "w2hi")
                T("vector", [sk, wk], [wk],
                  lambda eng, wstg=wstg, w16=w16:
                      eng.tensor_tensor(w16[:, 1024:2048], wstg[:, 0:1024], w16[:, 0:1024],
                                        op=OP.subtract), "w2lo")
                dma("sdma2", w2sp[l, ki, :, :], w16[:, :], [wk], [f"w2s{l}_{ki}"], "w2spl")
            else:
                dma("sync", w16[:, :], w2sp[l, ki, :, :], [wk, f"w2s{l}_{ki}"], [wk],
                    "ldw2s")
            src_g = bass.AP(gsp.tensor,
                            offset=ki * 128 * 4096 + s0,
                            ap=[[4096, n], [2048, 2], [1, 512]])
            dma("gdma", g16.rearrange("p (h x) -> p h x", h=2)[0:n, :, :], src_g,
                [f"g{ki}", "F2"], ["F2"], "ldg")

        for si, (s0, s1) in enumerate(SPANS):
            p2_loads(0, si, s0)
            if NKI > 1:
                p2_loads(1, si, s0)
            for ki in range(NKI):
                n = 128 if ki < NKI - 1 else KI_LAST
                w16, wk = W2R[ki % 3]
                g16 = fa16(2)[:, 1024 * (ki % 3):1024 * (ki % 3) + 1024]
                if ki + 2 < NKI:
                    p2_loads(ki + 2, si, s0)
                for pi, (w_off, g_off) in enumerate(PASSES2):
                    for fo in range(8):
                        T("tensor", [wk, "F2"], [f"ps{fo}"],
                          lambda eng, fo=fo, ki=ki, n=n, w16=w16, g16=g16,
                                 w_off=w_off, g_off=g_off, pi=pi:
                              eng.matmul(PS[fo][:, 0:512],
                                         w16[0:n, w_off + 128 * fo:w_off + 128 * fo + 128],
                                         g16[0:n, g_off:g_off + 512],
                                         start=(ki == 0 and pi == 0),
                                         stop=(ki == NKI - 1 and pi == len(PASSES2) - 1)), "w216")
            for fo in range(8):
                T("vector", [f"ps{fo}", f"Y{fo}", "C"], [f"Y{fo}"],
                  lambda eng, fo=fo, s0=s0:
                      eng.scalar_tensor_tensor(ya(fo)[:, s0:s0 + 512], PS[fo][:, 0:512],
                                               CO_B2[l][:, fo:fo + 1], ya(fo)[:, s0:s0 + 512],
                                               op0=OP.add, op1=OP.add), "w2_resid")

    def emit_ffn(inv):
        emit_ln(lambda s0: XAr[:, 0:4, s0:s0 + 512], lambda s0: XAr[:, 4:8, s0:s0 + 512],
                lambda c: xa(c), [f"X{c}" for c in range(8)],
                lambda c: ya(c), [f"Y{c}" for c in range(8)],
                fa(4), "F4", fa(7), "F7", fa(5), "F5", name=f"ln0_{inv}")
        if stage == "ln0":
            return "partial"
        for c in range(8):
            dma("sdma2", xsp[c, :, :], xa(c), [f"X{c}"], [f"xsp{c}"], "xspill")
        for l in range(2):
            emit_ffn_layer(inv, l)
            if stage in (f"attn_i{inv}l{l}", f"ffn_i{inv}l{l}"):
                return "partial"
        for c in range(8):
            dma("sync", xa(c), xsp[c, :, :], [f"xsp{c}", f"X{c}"], [f"X{c}"], "xrest")
            T("vector", [f"X{c}", f"Y{c}"], [f"X{c}"],
              lambda eng, c=c: eng.scalar_tensor_tensor(xa(c), ya(c), 0.5, xa(c),
                                                        op0=OP.mult, op1=OP.add), "resid_ffn")
        return "complete"

    def emit_dense():
        emit_ln(lambda s0: XAr[:, 0:4, s0:s0 + 512], lambda s0: XAr[:, 4:8, s0:s0 + 512],
                lambda c: xa(c), [f"X{c}" for c in range(8)],
                lambda c: fa(c), [f"F{c}" for c in range(8)],
                ya(0), "Y0", ya(2), "Y2", ya(1), "Y1", name="lnd")
        for c in range(8):
            dma("sdma2", xsp[c, :, :], xa(c), [f"X{c}"], [f"xsp{c}"], "xspd")

        def proj2(wd, store, name):
            for c in range(8):
                wb = XA[:, 2048 * (c // 2) + 1024 * (c % 2):
                        2048 * (c // 2) + 1024 * (c % 2) + 1024]
                dma("sync", wb, wd[128 * c:128 * c + 128, :],
                    [f"X{c // 2}"], [f"X{c // 2}"], "ldW_" + name)
            for fo in range(8):
                for si, (s0, s1) in enumerate(SPANS):
                    bank = (fo * 4 + si) % 2
                    for c in range(8):
                        wb = XA[:, 2048 * (c // 2) + 1024 * (c % 2):
                                2048 * (c // 2) + 1024 * (c % 2) + 1024]
                        T("tensor", [f"X{c // 2}", f"F{c}"], [f"ps{bank}"],
                          lambda eng, c=c, fo=fo, s0=s0, bank=bank, wb=wb:
                              eng.matmul(PS[bank][:, 0:512], wb[:, 128 * fo:128 * fo + 128],
                                         fa(c)[:, s0:s0 + 512], start=(c == 0), stop=(c == 7)),
                          "dmm_" + name)
                    store(fo, si, bank)

        def store_v(fo, si, bank):
            s0 = 512 * si
            T("scalar", [f"ps{bank}", "X4"], ["X4"],
              lambda eng, bank=bank, s0=s0: eng.copy(xa(4)[:, s0:s0 + 512], PS[bank][:, 0:512]),
              "dv_ev")
            if si == 3:
                for tt in range(16):
                    b2_ = 4 + (tt % 2)
                    T("tensor", ["X4", "C"], [f"ps{b2_}"],
                      lambda eng, tt=tt, b2_=b2_:
                          eng.transpose(PS[b2_][:, 0:128], xa(4)[:, 128 * tt:128 * tt + 128],
                                        CO_IDENT), "dv_T")
                    T("scalar", [f"ps{b2_}"], [f"YV{tt}"],
                      lambda eng, tt=tt, fo=fo, b2_=b2_:
                          eng.copy(YA[:, 1024 * tt + 128 * fo:1024 * tt + 128 * fo + 128],
                                   PS[b2_][:, 0:128]), "dv_Te")

        def store_spill(dst, tag):
            def f(fo, si, bank):
                s0 = 512 * si
                T("scalar", [f"ps{bank}", "X4"], ["X4"],
                  lambda eng, bank=bank, s0=s0:
                      eng.copy(xa16(4)[:, s0:s0 + 512], PS[bank][:, 0:512]), "dq_evh")
                T("vector", [f"ps{bank}", "X4"], ["X4"],
                  lambda eng, bank=bank, s0=s0:
                      eng.tensor_tensor(xa16(4)[:, 2048 + s0:2048 + s0 + 512],
                                        PS[bank][:, 0:512], xa16(4)[:, s0:s0 + 512],
                                        op=OP.subtract), "dq_evl")
                if si == 3:
                    dma("sdma2", dst[fo, :, :], xa16(4)[:, 0:4096], ["X4"], [f"{tag}{fo}"],
                        "dq_sp")
            return f

        proj2(Wv_d, store_v, "v")
        proj2(Wq_d, store_spill(qsp, "qq"), "q")
        proj2(Wk_d, store_spill(ksp, "kk"), "k")
        for c in range(8):
            dma("sync", xa(c), xsp[c, :, :], [f"xsp{c}", f"X{c}"], [f"X{c}"], "xrest_d")

        for g in range(8):
            q0 = 256 * g
            qg = fa16(0)[:, 0:4096].rearrange("p (c n) -> p c n", c=8)
            src_h = bass.AP(qsp.tensor, offset=q0, ap=[[2 * S, 128], [128 * 2 * S, 8], [1, 256]])
            src_l = bass.AP(qsp.tensor, offset=S + q0,
                            ap=[[2 * S, 128], [128 * 2 * S, 8], [1, 256]])
            dma("sync", qg[:, :, 0:256], src_h,
                ["F0"] + [f"qq{fo}" for fo in range(8)], ["F0"], "ldqgh")
            dma("sync", qg[:, :, 256:512], src_l, ["F0"], ["F0"], "ldqgl")
            for kspan in range(4):
                ks0 = 512 * kspan
                base = 1 + 2 * (kspan % 2)
                ksl = FA16[:, 4096 * base:4096 * base + 8192]
                kkeys = [f"F{base}", f"F{base + 1}"]
                kg = ksl.rearrange("p (c n) -> p c n", c=8)
                srck_h = bass.AP(ksp.tensor, offset=ks0,
                                 ap=[[2 * S, 128], [128 * 2 * S, 8], [1, 512]])
                srck_l = bass.AP(ksp.tensor, offset=S + ks0,
                                 ap=[[2 * S, 128], [128 * 2 * S, 8], [1, 512]])
                dma("sync", kg[:, :, 0:512], srck_h,
                    kkeys + [f"kk{fo}" for fo in range(8)], kkeys, "ldkgh")
                dma("sync", kg[:, :, 512:1024], srck_l, kkeys, kkeys, "ldkgl")
                for qt in range(2):
                    bank = qt
                    for pi, (qh, kh) in enumerate([(0, 0), (0, 1), (1, 0)]):
                        for c in range(8):
                            T("tensor", ["F0"] + kkeys, [f"ps{bank}"],
                              lambda eng, c=c, qt=qt, bank=bank, ksl=ksl, pi=pi, qh=qh, kh=kh:
                                  eng.matmul(PS[bank][:, 0:512],
                                             fa16(0)[:, 512 * c + 256 * qh + 128 * qt:
                                                     512 * c + 256 * qh + 128 * qt + 128],
                                             ksl[:, 1024 * c + 512 * kh:1024 * c + 512 * kh + 512],
                                             start=(pi == 0 and c == 0),
                                             stop=(pi == 2 and c == 7)), "ds_mm")
                    T("scalar", [f"ps{bank}", f"F{5 + qt}"], [f"F{5 + qt}"],
                      lambda eng, qt=qt, bank=bank, ks0=ks0:
                          eng.copy(fa(5 + qt)[:, ks0:ks0 + 512], PS[bank][:, 0:512]), "ds_ev")
            for qt in range(2):
                st = fa(5 + qt)
                skey = f"F{5 + qt}"
                mx = CO[:, 1408 + qt:1409 + qt]
                sm = CO[:, 1410 + qt:1411 + qt]
                rc = CO[:, 1412 + qt:1413 + qt]
                T("vector", [skey, "BRHS"], ["BRHS"],
                  lambda eng, st=st, mx=mx:
                      eng.tensor_reduce(mx, st[:, 0:2048], axis=AX.X, op=OP.max), "dsm_max")
                T("vector", ["BRHS"], ["BRHS"],
                  lambda eng, mx=mx: eng.tensor_scalar(mx, mx, -1.0 / 32.0, None, op0=OP.mult),
                  "dsm_neg")
                T("scalar", [skey, "BRHS"], [skey, "BRHS"],
                  lambda eng, st=st, mx=mx, sm=sm:
                      eng.activation(st[:, 0:2048], st[:, 0:2048], AF.Exp, bias=mx,
                                     scale=1.0 / 32.0, accum_out=sm), "dsm_exp")
                T("vector", ["BRHS"], ["BRHS"],
                  lambda eng, sm=sm, rc=rc: eng.reciprocal(rc, sm), "dsm_rec")
                T("vector", [skey, "BRHS"], [skey],
                  lambda eng, st=st, rc=rc:
                      eng.tensor_scalar(st[:, 0:2048], st[:, 0:2048], rc, None, op0=OP.mult),
                  "dsm_nrm")
                if stage == "densedbgP" and g == 0:
                    dma("sync", y_d[256 * qt:256 * qt + 128, :], st[:, 0:1024],
                        [skey], ["dbg"], "dbgP")
                    dma("sync", y_d[256 * qt + 128:256 * qt + 256, :], st[:, 1024:2048],
                        [skey], ["dbg"], "dbgP2")
                for kc in range(16):
                    b2_ = 2 + (kc % 2)
                    pt_t = fa(0) if kc < 8 else fa(7)
                    pt_key = "F0" if kc < 8 else "F7"
                    T("tensor", [skey, "C"], [f"ps{b2_}"],
                      lambda eng, st=st, kc=kc, b2_=b2_:
                          eng.transpose(PS[b2_][:, 0:128], st[:, 128 * kc:128 * kc + 128],
                                        CO_IDENT), "dsm_T")
                    T("scalar", [f"ps{b2_}", pt_key], [pt_key],
                      lambda eng, kc=kc, qt=qt, b2_=b2_, pt_t=pt_t:
                          eng.copy(pt_t[:, 256 * (kc % 8) + 128 * qt:256 * (kc % 8) + 128 * qt + 128],
                                   PS[b2_][:, 0:128]), "dsm_Te")
            for kc in range(16):
                pt_t = fa(0) if kc < 8 else fa(7)
                pt_key = "F0" if kc < 8 else "F7"
                for dc in range(8):
                    T("tensor", [pt_key, f"YV{kc}"], [f"ps{dc}"],
                      lambda eng, kc=kc, dc=dc, pt_t=pt_t:
                          eng.matmul(PS[dc][:, 0:256],
                                     YA[:, 1024 * kc + 128 * dc:1024 * kc + 128 * dc + 128],
                                     pt_t[:, 256 * (kc % 8):256 * (kc % 8) + 256],
                                     start=(kc == 0), stop=(kc == 15)), "dav_mm")
            for dc in range(8):
                T("vector", [f"ps{dc}", f"X{dc}"], [f"X{dc}"],
                  lambda eng, dc=dc, q0=q0:
                      eng.tensor_tensor(xa(dc)[:, q0:q0 + 256], xa(dc)[:, q0:q0 + 256],
                                        PS[dc][:, 0:256], op=OP.add), "dav_resid")

    try:
        r = emit_ffn(0)
    except StopIteration:
        return bk.finalize()
    if stage in ("full", "dense", "densedbgP"):
        emit_dense()
    if stage == "full":
        emit_ffn(1)

    if stage == "densedbgP":
        return bk.finalize()
    out_is_y = (r == "partial")
    outsrc = (lambda c: ya(c)) if out_is_y else (lambda c: xa(c))
    okeys = [f"Y{c}" for c in range(8)] if out_is_y else [f"X{c}" for c in range(8)]
    for tt in range(16):
        stg = fa(4)[:, 0:1024]
        for c in range(8):
            bank = c % 2
            T("tensor", [okeys[c], "C"], [f"ps{bank}"],
              lambda eng, c=c, tt=tt, bank=bank:
                  eng.transpose(PS[bank][:, 0:128], outsrc(c)[:, 128 * tt:128 * tt + 128],
                                CO_IDENT), "oT")
            T("scalar", [f"ps{bank}", "F4"], ["F4"],
              lambda eng, c=c, bank=bank, stg=stg:
                  eng.copy(stg[:, 128 * c:128 * c + 128], PS[bank][:, 0:128]), "oTe")
        dma("sync", y_d[128 * tt:128 * tt + 128, :], stg, ["F4"], ["F4"], "sto")

    return bk.finalize()


def _in_maps(L0_attn_x, L0_attn_Wq, L0_attn_Wk, L0_attn_Wv, L0_ffn_wqkv,
             L0_ffn_q_scale, L0_ffn_k_scale, L0_ffn_wo, L0_ffn_ln_g, L0_ffn_ln_b,
             L0_ffn_w1, L0_ffn_b1, L0_ffn_w2, L0_ffn_b2):
    cvt = lambda a: np.ascontiguousarray(np.asarray(a, dtype=np.float32))
    x = cvt(L0_attn_x)
    common = {"Wq": cvt(L0_attn_Wq), "Wk": cvt(L0_attn_Wk), "Wv": cvt(L0_attn_Wv),
              "wqkv": cvt(L0_ffn_wqkv), "qs": cvt(L0_ffn_q_scale), "ks": cvt(L0_ffn_k_scale),
              "wo": cvt(L0_ffn_wo), "lng": cvt(L0_ffn_ln_g), "lnb": cvt(L0_ffn_ln_b),
              "w1": cvt(L0_ffn_w1), "b1": cvt(L0_ffn_b1), "w2": cvt(L0_ffn_w2),
              "b2": cvt(L0_ffn_b2)}
    return [dict(common, x=np.ascontiguousarray(x[b])) for b in range(NCORE)]


def kernel(**inputs):
    nc = build_program("full")
    in_maps = _in_maps(**inputs)
    res = run_bass_kernel_spmd(nc, in_maps, core_ids=list(range(NCORE)))
    out = np.stack([res.results[b]["y"] for b in range(NCORE)], axis=0)
    return out.astype(np.float32)
